# version 61
# baseline (speedup 1.0000x reference)
import contextlib
import numpy as np
import ml_dtypes

import concourse.bass as bass
import concourse.mybir as mybir
from concourse.bass_utils import run_bass_kernel_spmd

F32 = mybir.dt.float32
BF16 = mybir.dt.bfloat16
I32 = mybir.dt.int32
AF = mybir.ActivationFunctionType
ALU = mybir.AluOpType
AX = mybir.AxisListType
BFnp = ml_dtypes.bfloat16

NT = 16
NHC = 24
NF = 32
SCALE_QK = float(64.0 ** -0.25)
INV127 = float(1.0 / 127.0)
HALF127 = float(0.5 / 127.0)
INV_SQM = float(1.0 / np.sqrt(np.float64(128.0)))
EPS_F = 1e-6
ENGINES = ("sync", "gpsimd", "tensor", "vector", "scalar")

A_Q = 0
A_K = 8192
A_V = 16384
A_Y = 0
A_RELU = 0
A_LNTM = 16384
ACTB_W = 24704

C_ATT = 0
C_QKVW = 16384
C_WO = 16384
BUFC_W = 24576


class Sem:
    def __init__(self, h):
        self.h = h
        self.n = 0


def build(stage=3, attn_lvl=6):
    nc = bass.Bass("TRN2", target_bir_lowering=False)

    xin = nc.dram_tensor("xin", [2048, 1024], F32, kind="ExternalInput").ap()
    wgT_i = nc.dram_tensor("wgT_i", [1024, 3072], I32, kind="ExternalInput").ap()
    wuT_i = nc.dram_tensor("wuT_i", [1024, 3072], I32, kind="ExternalInput").ap()
    wdT_i = nc.dram_tensor("wdT_i", [3072, 1024], I32, kind="ExternalInput").ap()
    Wq_t = nc.dram_tensor("Wq_t", [2, 128, 8, 512], F32, kind="ExternalInput").ap()
    Wk_t = nc.dram_tensor("Wk_t", [2, 128, 8, 512], F32, kind="ExternalInput").ap()
    Wv_t = nc.dram_tensor("Wv_t", [2, 128, 8, 512], F32, kind="ExternalInput").ap()
    Wo_sw = nc.dram_tensor("Wo_sw", [128, 8, 1024], F32, kind="ExternalInput").ap()
    W1_sw = nc.dram_tensor("W1_sw", [128, 32, 8, 128], F32, kind="ExternalInput").ap()
    W2_sw = nc.dram_tensor("W2_sw", [128, 8, 32, 128], F32, kind="ExternalInput").ap()
    omTd_in = nc.dram_tensor("omD", [128, 256], F32, kind="ExternalInput").ap()
    b1t_in = nc.dram_tensor("b1t", [128, 32], F32, kind="ExternalInput").ap()
    b2t_in = nc.dram_tensor("b2t", [128, 8], F32, kind="ExternalInput").ap()
    gam_in = nc.dram_tensor("gam_bc", [128, 1024], F32, kind="ExternalInput").ap()
    gamF_in = nc.dram_tensor("gamF", [128, 8], F32, kind="ExternalInput").ap()
    betF_in = nc.dram_tensor("betF", [128, 8], F32, kind="ExternalInput").ap()
    bet_in = nc.dram_tensor("bet_bc", [128, 1024], F32, kind="ExternalInput").ap()
    idb_in = nc.dram_tensor("ident_bf", [128, 128], BF16, kind="ExternalInput").ap()
    idf_in = nc.dram_tensor("ident_f32", [128, 128], F32, kind="ExternalInput").ap()
    msk_in = nc.dram_tensor("maskji", [128, 128], BF16, kind="ExternalInput").ap()
    out = nc.dram_tensor("out", [2048, 1024], F32, kind="ExternalOutput").ap()

    wgT_bf = nc.dram_tensor("wgT_bf", [1024, 3072], BF16, kind="Internal").ap()
    wuT_bf = nc.dram_tensor("wuT_bf", [1024, 3072], BF16, kind="Internal").ap()
    wdT_bf = nc.dram_tensor("wdT_bf", [3072, 1024], BF16, kind="Internal").ap()
    W1_bf = nc.dram_tensor("W1_bf", [128, 32, 8, 128], BF16, kind="Internal").ap()
    W2_bf = nc.dram_tensor("W2_bf", [128, 8, 32, 128], BF16, kind="Internal").ap()
    Wq_bf = nc.dram_tensor("Wq_bf", [2, 128, 8, 512], BF16, kind="Internal").ap()
    Wk_bf = nc.dram_tensor("Wk_bf", [2, 128, 8, 512], BF16, kind="Internal").ap()
    Wv_bf = nc.dram_tensor("Wv_bf", [2, 128, 8, 512], BF16, kind="Internal").ap()

    es = contextlib.ExitStack()
    with es:
        x_tm = es.enter_context(nc.sbuf_tensor("x_tm", [128, NT * 1024], F32))
        fmA = es.enter_context(nc.sbuf_tensor("fmA", [128, 8 * 2048], BF16))
        actB = es.enter_context(nc.sbuf_tensor("actB", [128, ACTB_W], BF16))
        BufC = es.enter_context(nc.sbuf_tensor("BufC", [128, BUFC_W], BF16))
        tmpF = es.enter_context(nc.sbuf_tensor("tmpF", [128, 1024], F32))
        gam_bf = es.enter_context(nc.sbuf_tensor("gam_bf", [128, 1024], BF16))
        gamF_s = es.enter_context(nc.sbuf_tensor("gamF_s", [128, 8], F32))
        betF_s = es.enter_context(nc.sbuf_tensor("betF_s", [128, 8], F32))
        bet_bf = es.enter_context(nc.sbuf_tensor("bet_bf", [128, 1024], BF16))
        Sf_t = es.enter_context(nc.sbuf_tensor("Sf_t", [128, 528], F32))
        amT = es.enter_context(nc.sbuf_tensor("amT", [128, 512], BF16))
        sbT = es.enter_context(nc.sbuf_tensor("sbT", [128, 1056], BF16))
        den_t = es.enter_context(nc.sbuf_tensor("den_t", [128, 8], F32))
        rec_t = es.enter_context(nc.sbuf_tensor("rec_t", [128, 8], F32))
        musum = es.enter_context(nc.sbuf_tensor("musum", [128, 16], F32))
        lneps = es.enter_context(nc.sbuf_tensor("lneps", [128, 1], F32))
        mu_c = es.enter_context(nc.sbuf_tensor("mu_c", [128, 16], F32))
        varsum = es.enter_context(nc.sbuf_tensor("varsum", [128, 16], F32))
        rstd_c = es.enter_context(nc.sbuf_tensor("rstd_c", [128, 16], F32))
        mu2 = es.enter_context(nc.sbuf_tensor("mu2", [128, 16], F32))
        rstd2 = es.enter_context(nc.sbuf_tensor("rstd2", [128, 16], F32))
        m2r2 = es.enter_context(nc.sbuf_tensor("m2r2", [128, 16], F32))
        st_nm = es.enter_context(nc.sbuf_tensor("st_nm", [128, 16], F32))
        omT_bf = es.enter_context(nc.sbuf_tensor("omT_bf", [128, 256], BF16))
        ident_b = es.enter_context(nc.sbuf_tensor("ident_b", [128, 128], BF16))
        ident_f = es.enter_context(nc.sbuf_tensor("ident_f", [128, 128], F32))
        maskji = es.enter_context(nc.sbuf_tensor("maskji_s", [128, 128], BF16))
        b1t = es.enter_context(nc.sbuf_tensor("b1t_s", [128, 32], F32))
        b2t = es.enter_context(nc.sbuf_tensor("b2t_s", [128, 8], F32))

        p0 = es.enter_context(nc.psum_tensor("p0", [128, 512], F32))
        p1 = es.enter_context(nc.psum_tensor("p1", [128, 512], F32))
        p2 = es.enter_context(nc.psum_tensor("p2", [128, 512], F32))
        p3 = es.enter_context(nc.psum_tensor("p3", [128, 512], F32))
        p4 = es.enter_context(nc.psum_tensor("p4", [128, 512], F32))
        p5 = es.enter_context(nc.psum_tensor("p5", [128, 512], F32))
        p6a = es.enter_context(nc.psum_tensor("p6a", [128, 1024], BF16))
        p6b = es.enter_context(nc.psum_tensor("p6b", [128, 1024], BF16))
        pP = (p0, p1)
        pQ = (p2, p3)
        p6 = (p6a, p6b)
        pOb = p6b.bitcast(F32)

        def new_sem(name):
            return Sem(es.enter_context(nc.semaphore(name)))
        s_x = new_sem("s_x")
        s_xc = [s_x] + [new_sem(f"s_x{k}") for k in (1, 2, 3)]
        s_c = new_sem("s_c")
        cwg = [new_sem(f"cwg{q}") for q in range(4)]
        cwu = [new_sem(f"cwu{q}") for q in range(4)]
        cwd = new_sem("cwd")
        cw1 = new_sem("cw1")
        cw2 = new_sem("cw2")
        s_gam = new_sem("s_gam")
        swA = new_sem("swA")
        swB = new_sem("swB")
        s_wd = new_sem("s_wd")
        sW2a = new_sem("sW2a")
        sW2b = new_sem("sW2b")
        qkvA = new_sem("qkvA")
        qkvB = new_sem("qkvB")
        s_wo = new_sem("s_wo")
        sync_dsems = [s_x, s_c, swA, swB, s_wd, sW2a, sW2b, qkvB] + s_xc[1:]
        gp_dsems = cwg + cwu + [cwd, cw1, cw2, s_gam, qkvA, s_wo]
        eT = Sem(es.enter_context(nc.semaphore("eT")))
        eV = Sem(es.enter_context(nc.semaphore("eV")))
        eS = Sem(es.enter_context(nc.semaphore("eS")))
        mSY = Sem(es.enter_context(nc.semaphore("mSY")))
        mGP = Sem(es.enter_context(nc.semaphore("mGP")))
        eG = Sem(es.enter_context(nc.semaphore("eG")))

        ops = {e: [] for e in ENGINES}

        def op(eng, fn, waits=(), inc=None, inc_by=1):
            tag = None
            if inc is not None:
                inc.n += inc_by
                tag = inc.n
            wl = tuple((s, v) for (s, v) in waits if v and v > 0)

            def run(e, fn=fn, wl=wl, inc=inc, inc_by=inc_by):
                for (s, v) in wl:
                    e.wait_ge(s.h, v)
                r = fn(e)
                if inc is not None:
                    r.then_inc(inc.h, inc_by)
            ops[eng].append(run)
            return tag

        def dma(q, sem, out_ap, in_ap, waits=(), **kw):
            return op(q, lambda e, o=out_ap, i=in_ap, kw=dict(kw): e.dma_start(o, i, **kw),
                      waits, inc=sem, inc_by=16)

        def barrier():
            t1 = op("sync", lambda e: e.nop(),
                    waits=[(sm, sm.n) for sm in sync_dsems], inc=mSY)
            t2 = op("gpsimd", lambda e: e.nop(),
                    waits=[(sm, sm.n) for sm in gp_dsems] + [(eG, eG.n)], inc=mGP)
            t3 = op("tensor", lambda e: e.nop(), waits=[(eT, eT.n)], inc=eT)
            t4 = op("vector", lambda e: e.nop(), waits=[(eV, eV.n)], inc=eV)
            t5 = op("scalar", lambda e: e.nop(), waits=[(eS, eS.n)], inc=eS)
            wl = [(mSY, t1), (mGP, t2), (eT, t3), (eV, t4), (eS, t5)]
            for eng in ENGINES:
                op(eng, lambda e: None, waits=wl)

        t_xchunk = {}
        for c in range(NT):
            t = dma("sync" if c % 2 == 0 else "scalar", s_xc[c // 4],
                    x_tm[:, c * 1024:(c + 1) * 1024],
                    xin[c * 128:(c + 1) * 128, :])
            t_xchunk[c // 4] = t
        t_xall = s_x.n
        dma("sync", s_c, ident_b[:, :], idb_in)
        dma("sync", s_c, ident_f[:, :], idf_in)
        dma("sync", s_c, maskji[:, :], msk_in)
        dma("sync", s_c, b1t[:, :], b1t_in)
        dma("sync", s_c, b2t[:, :], b2t_in)
        t_call = s_c.n

        cast_wd_box = [None]

        lneps_tag = [0]

        def emit_outer_ln(eps, mu_store=None, rstd_store=None, src_wait=None,
                          src_tags=None, xl_wait=None, pipe=False,
                          pipe_chunks=True, pipe_stagger=False):
            mu_d = mu_store if mu_store is not None else mu_c
            rs_d = rstd_store if rstd_store is not None else rstd_c
            tRSd = {}
            if pipe:
                t_le = op("vector", lambda e, eps=float(eps):
                          e.memset(lneps[:, :], eps), inc=eV)
                lneps_tag[0] = t_le
                dump2 = actB[:, A_LNTM + 2048:A_LNTM + 3072]

                def emit_pstats(c):
                    w = []
                    if pipe_chunks and c % 4 == 0:
                        w = [(s_xc[c // 4], t_xchunk[c // 4])]
                    tR = op("vector", lambda e, c=c: e.reduce_sum(
                        musum[:, c:c + 1], x_tm[:, c * 1024:(c + 1) * 1024],
                        axis=AX.X), waits=w, inc=eV)
                    tMU = op("vector", lambda e, c=c, mu=mu_d: e.tensor_scalar(
                        mu[:, c:c + 1], musum[:, c:c + 1], 1.0 / 1024.0, None,
                        ALU.mult, ALU.bypass), waits=[(eV, tR)], inc=eV)
                    tSQ = op("scalar", lambda e, c=c, mu=mu_d, dump2=dump2:
                             e.activation(
                        dump2, x_tm[:, c * 1024:(c + 1) * 1024], AF.Square,
                        bias=mu[:, c:c + 1], scale=-1.0,
                        accum_out=varsum[:, c:c + 1]),
                        waits=[(eV, tMU)], inc=eS)
                    wl = [(eS, tSQ)] + ([(eV, t_le)] if c == 0 else [])
                    tLN = op("scalar", lambda e, c=c: e.activation(
                        varsum[:, c:c + 1], varsum[:, c:c + 1], AF.Ln,
                        bias=lneps[:, 0:1], scale=1.0 / 1024.0),
                        waits=wl, inc=eS)
                    tRSd[c] = op("scalar", lambda e, c=c, rs=rs_d: e.activation(
                        rs[:, c:c + 1], varsum[:, c:c + 1], AF.Exp, scale=-0.5),
                        waits=[(eS, tLN)], inc=eS)
            else:
                tR = None
                for c in range(NT):
                    if src_tags is not None:
                        w = [src_tags[c]]
                    else:
                        w = [src_wait] if (src_wait and c == 0) else []
                    tR = op("vector", lambda e, c=c: e.reduce_sum(
                        musum[:, c:c + 1], x_tm[:, c * 1024:(c + 1) * 1024], axis=AX.X),
                        waits=w, inc=eV)
                tMU = op("vector", lambda e, mu=mu_d: e.tensor_scalar(
                    mu[:, :], musum[:, :], 1.0 / 1024.0, None, ALU.mult, ALU.bypass),
                    waits=[(eV, tR)], inc=eV)
                tVS = None
                for c in range(NT):
                    w = [(eV, tMU)] if c == 0 else [(eS, tVS)]
                    tVS = op("scalar", lambda e, c=c, mu=mu_d: e.activation(
                        actB[:, A_LNTM:A_LNTM + 1024], x_tm[:, c * 1024:(c + 1) * 1024], AF.Square,
                        bias=mu[:, c:c + 1], scale=-1.0, accum_out=varsum[:, c:c + 1]),
                        waits=w, inc=eS)
                tSV = op("vector", lambda e, eps=float(eps): e.tensor_scalar(
                    varsum[:, :], varsum[:, :], 1.0 / 1024.0, eps, ALU.mult, ALU.add),
                    waits=[(eS, tVS)], inc=eV)
                tLN = op("scalar", lambda e: e.activation(varsum[:, :], varsum[:, :], AF.Ln),
                         waits=[(eV, tSV)], inc=eS)
                tRS = op("scalar", lambda e, rs=rs_d: e.activation(
                    rs[:, :], varsum[:, :], AF.Exp, scale=-0.5),
                    waits=[(eS, tLN)], inc=eS)
                for c in range(NT):
                    tRSd[c] = tRS
                if mu_store is not None:
                    op("vector", lambda e, mu=mu_d, rs=rs_d: e.scalar_tensor_tensor(
                        m2r2[:, :], mu[:, :], -1.0, rs[:, :], ALU.mult, ALU.mult),
                        waits=[(eS, tRS)], inc=eV)
            ev_tag, xl_tag, tr_tag = {}, {}, {}

            def emit_pxform(c):
                pi = c % 2
                lntm = actB[:, A_LNTM + pi * 1024: A_LNTM + (pi + 1) * 1024]
                w = [(eS, tRSd[c])]
                if xl_wait is not None and c == 0:
                    w.append(xl_wait)
                if c >= 2:
                    w.append((eT, tr_tag[c - 2]))
                xl_tag[c] = op("vector", lambda e, c=c, lntm=lntm, mu=mu_d, rs=rs_d:
                               e.tensor_scalar(lntm, x_tm[:, c * 1024:(c + 1) * 1024],
                                               mu[:, c:c + 1], rs[:, c:c + 1],
                                               ALU.subtract, ALU.mult),
                               waits=w, inc=eV)
                w2 = [(eV, xl_tag[c]), (s_c, t_call)]
                if c >= 2:
                    w2.append(ev_tag[c - 2])
                tt = None
                for j in range(8):
                    tt = op("tensor", lambda e, j=j, pi=pi, lntm=lntm: e.transpose(
                        p6[pi][:, j * 128:(j + 1) * 128], lntm[:, j * 128:(j + 1) * 128],
                        ident_b[:, :]),
                        waits=(w2 if j == 0 else ()), inc=eT)
                tr_tag[c] = tt

            def emit_pev(c):
                pi = c % 2
                dst = fmA[:, :].rearrange("p (k t) -> p k t", k=8)[:, :, c * 128:(c + 1) * 128]
                src = p6[pi][:, :].rearrange("p (k t) -> p k t", k=8)
                if pipe:
                    ev_tag[c] = (eS, op("scalar", lambda e, dst=dst, src=src:
                                        e.copy(dst, src),
                                        waits=[(eT, tr_tag[c])], inc=eS))
                else:
                    ev_tag[c] = (eV, op("vector", lambda e, dst=dst, src=src:
                                        e.tensor_copy(dst, src),
                                        waits=[(eT, tr_tag[c])], inc=eV))

            if pipe and pipe_stagger:
                for c in range(NT + 2):
                    if c < NT:
                        emit_pstats(c)
                    if c >= 2:
                        emit_pxform(c - 2)
                    if c >= 3:
                        emit_pev(c - 3)
                emit_pev(NT - 1)
            elif pipe:
                for c in range(NT):
                    emit_pstats(c)
                for c in range(NT):
                    emit_pxform(c)
                    emit_pev(c)
            else:
                for c in range(NT):
                    emit_pxform(c)
                    emit_pev(c)
            return ev_tag

        ffn_last = {"mm": 0, "evac": 0}

        def emit_ffn(ln_ev, xtags=None, stats_cb=None):
            pg = (p0, p1)
            pu = (p2, p3)
            n = [0]
            tagSilu, tagAct, tagG, tagU = {}, {}, {}, {}
            last_down_mm = [0]
            last_down_evac = [0]
            q_last_mm = {}
            for th in range(2):
                tt0 = th * 1024
                for q in range(4):
                    pq = q % 2
                    wg_off = pq * 12288
                    wu_off = pq * 12288 + 6144
                    swq = (cwg[0], cwg[1])[q % 2]
                    w = []
                    gq = th * 4 + q
                    if gq >= 2:
                        w.append((eT, q_last_mm[gq - 2]))
                    if th == 1 and q < 2:
                        w.append((eT, last_down_mm[0]))
                    src = wgT_i.rearrange("(k p) h -> p k h", p=128)[:, :, q * 768:(q + 1) * 768]
                    semh1 = (cwg[0], cwg[1])[q % 2]
                    semh2 = (cwg[2], cwg[3])[q % 2]
                    srcu = wuT_i.rearrange("(k p) h -> p k h", p=128)[:, :, q * 768:(q + 1) * 768]
                    dma("gpsimd", semh1, BufC[:, wg_off:wg_off + 3072]
                        .rearrange("p (k h) -> p k h", k=4), src[:, 0:4, :], waits=w)
                    tH1 = dma("gpsimd", semh1, BufC[:, wu_off:wu_off + 3072]
                              .rearrange("p (k h) -> p k h", k=4), srcu[:, 0:4, :])
                    dma("gpsimd", semh2, BufC[:, wg_off + 3072:wg_off + 6144]
                        .rearrange("p (k h) -> p k h", k=4), src[:, 4:8, :])
                    tH2 = dma("gpsimd", semh2, BufC[:, wu_off + 3072:wu_off + 6144]
                              .rearrange("p (k h) -> p k h", k=4), srcu[:, 4:8, :])
                    if gq == 1 and cast_wd_box[0] is None:
                        cast_wd_box[0] = dma("gpsimd", cwd, wdT_bf[:, :],
                                             wdT_i[:, :], max_dma_last_dim=8192)
                    for hcl in range(6):
                        hc = q * 6 + hcl
                        for tsl in range(2):
                            i = n[0]
                            pi = i % 2
                            tg = None
                            for k in range(8):
                                wts = []
                                if k == 0:
                                    if hcl == 0:
                                        wts = [(semh1, tH1),
                                               ln_ev[th * 8 + tsl * 4 + 3]]
                                    if i >= 2:
                                        wts.append((eS, tagSilu[i - 2]))
                                    if th == 1 and i == 48:
                                        wts.append((eV, last_down_evac[0]))
                                elif k == 4 and hcl == 0 and tsl == 0:
                                    wts = [(semh2, tH2)]
                                tg = op("tensor", lambda e, k=k, pi=pi, wg_off=wg_off,
                                        hcl=hcl, tt0=tt0, tsl=tsl:
                                        e.matmul(pg[pi][:, 0:512],
                                                 BufC[:, wg_off + k * 768 + hcl * 128:
                                                      wg_off + k * 768 + (hcl + 1) * 128],
                                                 fmA[:, k * 2048 + tt0 + tsl * 512:
                                                     k * 2048 + tt0 + (tsl + 1) * 512],
                                                 start=(k == 0), stop=(k == 7)),
                                        waits=wts, inc=(eT if k == 7 else None))
                            tagG[i] = tg
                            tu = None
                            for k in range(8):
                                wts = []
                                if k == 0:
                                    if hcl == 0:
                                        wts = [(semh1, tH1)]
                                    if i >= 2:
                                        wts.append((eV, tagAct[i - 2]))
                                elif k == 4 and hcl == 0 and tsl == 0:
                                    wts = [(semh2, tH2)]
                                tu = op("tensor", lambda e, k=k, pi=pi, wu_off=wu_off,
                                        hcl=hcl, tt0=tt0, tsl=tsl:
                                        e.matmul(pu[pi][:, 0:512],
                                                 BufC[:, wu_off + k * 768 + hcl * 128:
                                                      wu_off + k * 768 + (hcl + 1) * 128],
                                                 fmA[:, k * 2048 + tt0 + tsl * 512:
                                                     k * 2048 + tt0 + (tsl + 1) * 512],
                                                 start=(k == 0), stop=(k == 7)),
                                        waits=wts, inc=(eT if k == 7 else None))
                            tagU[i] = tu
                            wts = [(eT, tagG[i])]
                            if i >= 2:
                                wts.append((eV, tagAct[i - 2]))
                            tagSilu[i] = op("scalar", lambda e, pi=pi: e.activation(
                                tmpF[:, pi * 512:(pi + 1) * 512], pg[pi][:, 0:512],
                                AF.Silu, scale=INV127),
                                waits=wts, inc=eS)
                            tagAct[i] = op("vector", lambda e, pi=pi, hc=hc, tsl=tsl:
                                           e.scalar_tensor_tensor(
                                actB[:, hc * 1024 + tsl * 512: hc * 1024 + (tsl + 1) * 512],
                                pu[pi][:, 0:512], INV127,
                                tmpF[:, pi * 512:(pi + 1) * 512], ALU.mult, ALU.mult),
                                waits=[(eT, tagU[i]), (eS, tagSilu[i])], inc=eV)
                            n[0] += 1
                    q_last_mm[gq] = tagU[n[0] - 1]
                dma("sync", s_wd, BufC[:, 0:12288].rearrange("p (hc d) -> p hc d", hc=12),
                    wdT_bf.rearrange("(hc p) d -> p hc d", p=128)[:, 0:12, :],
                    waits=[(cwd, cast_wd_box[0]), (eT, q_last_mm[th * 4 + 2])])
                tWdh = s_wd.n
                tWd1 = dma("sync", cwu[0], BufC[:, 12288:16384]
                           .rearrange("p (hc d) -> p hc d", hc=4),
                           wdT_bf.rearrange("(hc p) d -> p hc d", p=128)[:, 12:16, :],
                           waits=[(eT, q_last_mm[th * 4 + 3])])
                tWd2 = dma("sync", cwu[1], BufC[:, 16384:20480]
                           .rearrange("p (hc d) -> p hc d", hc=4),
                           wdT_bf.rearrange("(hc p) d -> p hc d", p=128)[:, 16:20, :])
                tWd3 = dma("sync", cwu[2], BufC[:, 20480:24576]
                           .rearrange("p (hc d) -> p hc d", hc=4),
                           wdT_bf.rearrange("(hc p) d -> p hc d", p=128)[:, 20:24, :])
                tXU = {}
                for tl in range(8):
                    c = th * 8 + tl
                    da, db = ((p4, p5), (p0, p1))[tl % 2]
                    td = None
                    for hc in range(NHC):
                        wts = []
                        if hc == 0:
                            if tl == 0:
                                wts = [(s_wd, tWdh), (eV, tagAct[n[0] - 1])]
                            if tl >= 2:
                                wts.append((eV, tXU[tl - 2]))
                        elif tl == 0 and hc == 12:
                            wts = [(cwu[0], tWd1)]
                        elif tl == 0 and hc == 16:
                            wts = [(cwu[1], tWd2)]
                        elif tl == 0 and hc == 20:
                            wts = [(cwu[2], tWd3)]
                        op("tensor", lambda e, hc=hc, tl=tl, da=da:
                           e.matmul(da[:, 0:512],
                                    actB[:, hc * 1024 + tl * 128: hc * 1024 + (tl + 1) * 128],
                                    BufC[:, hc * 1024: hc * 1024 + 512],
                                    start=(hc == 0), stop=(hc == NHC - 1)),
                           waits=wts, inc=None)
                        td = op("tensor", lambda e, hc=hc, tl=tl, db=db:
                                e.matmul(db[:, 0:512],
                                         actB[:, hc * 1024 + tl * 128: hc * 1024 + (tl + 1) * 128],
                                         BufC[:, hc * 1024 + 512: hc * 1024 + 1024],
                                         start=(hc == 0), stop=(hc == NHC - 1)),
                                waits=(), inc=(eT if hc == NHC - 1 else None))
                    op("vector", lambda e, c=c, da=da: e.scalar_tensor_tensor(
                        x_tm[:, c * 1024: c * 1024 + 512], da[:, 0:512], HALF127,
                        x_tm[:, c * 1024: c * 1024 + 512], ALU.mult, ALU.add),
                        waits=[(eT, td)], inc=eV)
                    tXU[tl] = op("vector", lambda e, c=c, db=db: e.scalar_tensor_tensor(
                        x_tm[:, c * 1024 + 512: (c + 1) * 1024], db[:, 0:512], HALF127,
                        x_tm[:, c * 1024 + 512: (c + 1) * 1024], ALU.mult, ALU.add),
                        waits=(), inc=eV)
                    last_down_mm[0] = td
                    last_down_evac[0] = tXU[tl]
                    ffn_last["mm"] = td
                    ffn_last["evac"] = tXU[tl]
                    if xtags is not None:
                        xtags[c] = (eV, tXU[tl])
                    if stats_cb is not None:
                        stats_cb(c)

        ln1_ev = emit_outer_ln(1e-5, pipe=True)
        ffn1_x = {}
        ln2_sqt = {}

        def ln2_stats_cb(c):
            tRc = op("vector", lambda e, c=c: e.reduce_sum(
                musum[:, c:c + 1], x_tm[:, c * 1024:(c + 1) * 1024], axis=AX.X),
                waits=[ffn1_x[c]], inc=eV)
            tMUc = op("vector", lambda e, c=c: e.tensor_scalar(
                mu2[:, c:c + 1], musum[:, c:c + 1], 1.0 / 1024.0, None,
                ALU.mult, ALU.bypass), waits=[(eV, tRc)], inc=eV)
            tSQc = op("scalar", lambda e, c=c: e.activation(
                tmpF[:, 0:1024], x_tm[:, c * 1024:(c + 1) * 1024], AF.Square,
                bias=mu2[:, c:c + 1], scale=-1.0, accum_out=varsum[:, c:c + 1]),
                waits=[(eV, tMUc)], inc=eS)
            wl = [(eS, tSQc)] + ([(eV, lneps_tag[0])] if c == 0 else [])
            tLNc = op("scalar", lambda e, c=c: e.activation(
                varsum[:, c:c + 1], varsum[:, c:c + 1], AF.Ln,
                bias=lneps[:, 0:1], scale=1.0 / 1024.0), waits=wl, inc=eS)
            ln2_sqt[c] = op("scalar", lambda e, c=c: e.activation(
                rstd2[:, c:c + 1], varsum[:, c:c + 1], AF.Exp, scale=-0.5),
                waits=[(eS, tLNc)], inc=eS)
            op("gpsimd", lambda e, c=c: e.tensor_scalar(
                m2r2[:, c:c + 1], mu2[:, c:c + 1], -1.0, rstd2[:, c:c + 1],
                ALU.mult, ALU.mult), waits=[(eS, ln2_sqt[c])], inc=eG)

        emit_ffn(ln1_ev, xtags=ffn1_x,
                 stats_cb=(ln2_stats_cb if stage >= 2 else None))
        cast_wq = dma("gpsimd", qkvA, Wq_bf, Wq_t, max_dma_last_dim=8192)
        cast_wk = dma("gpsimd", qkvA, Wk_bf, Wk_t, max_dma_last_dim=8192)
        cast_wv = dma("gpsimd", qkvA, Wv_bf, Wv_t, max_dma_last_dim=8192)
        t_qkvcast = qkvA.n
        dma("gpsimd", s_gam, gamF_s[:, :], gamF_in)
        dma("gpsimd", s_gam, betF_s[:, :], betF_in)
        dma("gpsimd", s_gam, gam_bf[:, :], gam_in, max_dma_last_dim=8192)
        dma("gpsimd", s_gam, bet_bf[:, :], bet_in, max_dma_last_dim=8192)
        dma("gpsimd", s_gam, omT_bf[:, :], omTd_in, max_dma_last_dim=8192)
        t_gamall = s_gam.n
        if stage < 2:
            barrier()

        if stage >= 2:
            ffn1_mm_tag = ffn_last["mm"]
            ln2_ev, xl_tag2, tr_tag2 = {}, {}, {}
            for c in range(NT):
                pi = c % 2
                lntm = actB[:, A_LNTM + pi * 1024: A_LNTM + (pi + 1) * 1024]
                w = [(eS, ln2_sqt[c])]
                if c <= 1:
                    w.append((eT, ffn1_mm_tag))
                if c >= 2:
                    w.append((eT, tr_tag2[c - 2]))
                xl_tag2[c] = op("vector", lambda e, c=c, lntm=lntm:
                                e.tensor_scalar(lntm, x_tm[:, c * 1024:(c + 1) * 1024],
                                                mu2[:, c:c + 1], rstd2[:, c:c + 1],
                                                ALU.subtract, ALU.mult),
                                waits=w, inc=eV)
                w2 = [(eV, xl_tag2[c])]
                if c >= 2:
                    w2.append(ln2_ev[c - 2])
                tt = None
                for j in range(8):
                    tt = op("tensor", lambda e, j=j, pi=pi, lntm=lntm: e.transpose(
                        p6[pi][:, j * 128:(j + 1) * 128], lntm[:, j * 128:(j + 1) * 128],
                        ident_b[:, :]), waits=(w2 if j == 0 else ()), inc=eT)
                tr_tag2[c] = tt
                dst = fmA[:, :].rearrange("p (k t) -> p k t", k=8)[:, :, c * 128:(c + 1) * 128]
                src = p6[pi][:, :].rearrange("p (k t) -> p k t", k=8)
                ln2_ev[c] = (eS, op("scalar", lambda e, dst=dst, src=src:
                                    e.copy(dst, src), waits=[(eT, tt)], inc=eS))
            t_ln2 = ln2_ev[NT - 1][1]

            def ln2_ev_w(c):
                return ln2_ev[c]
            cast_w1 = dma("gpsimd", cw1, W1_bf[:, :, :, :], W1_sw,
                          waits=[(eS, t_ln2)], max_dma_last_dim=8192)
            cast_w2 = dma("gpsimd", cw2, W2_bf[:, :, :, :], W2_sw,
                          waits=[(eS, t_ln2)], max_dma_last_dim=8192)

            QGW = 12352

            def a_q(pg):
                return pg * QGW

            def a_k(pg):
                return pg * QGW + 4096

            def a_v(pg):
                return pg * QGW + 8192

            C_SLAB = 16384
            T_TM = 20480
            T_FM = 22528

            t_ones = None
            for pg in range(2):
                ap1 = bass.AP(actB, a_v(pg) + 64, [[ACTB_W, 128], [65, 64]])
                t_ones = op("vector", lambda e, ap1=ap1: e.memset(ap1, 1.0), inc=eV)

            slab_lastmm = {}
            slab_cnt = [0]
            qkv_rot_hist = []
            qkv_rot_cnt = [0]
            qk_done = {}
            v_done = {}

            def qkv_gen(qg):
                g, qq, pg = qg // 2, qg % 2, qg % 2
                for wi, (Wsrc, isV) in enumerate(((Wq_bf, 0), (Wk_bf, 0), (Wv_bf, 1))):
                    pw = slab_cnt[0] % 2
                    slab_cnt[0] += 1
                    qsem = qkvB
                    wts = [(qsem, qsem.n), (qkvA, t_qkvcast)]
                    if pw in slab_lastmm:
                        wts.append((eT, slab_lastmm[pw]))
                    else:
                        wts.append((eT, ffn1_mm_tag))
                    tW = dma("sync", qsem, BufC[:, C_SLAB + pw * 2048:
                                                C_SLAB + (pw + 1) * 2048]
                             .rearrange("p (k j) -> p k j", k=8),
                             Wsrc[g, :, :, qq * 256:(qq + 1) * 256],
                             waits=wts)
                    lm = None
                    if not isV:
                        dst0 = a_q(pg) if wi == 0 else a_k(pg)
                        for mloc in range(2):
                            for ts in range(4):
                                pr = p4
                                qkv_rot_cnt[0] += 1
                                for k in range(8):
                                    wts = []
                                    if k == 0:
                                        wts = [ln2_ev_w(ts * 4 + 3)]
                                        if mloc == 0 and ts == 0:
                                            wts.append((qsem, tW))
                                        if len(qkv_rot_hist) >= 1:
                                            wts.append(qkv_rot_hist[-1])
                                    lm = op("tensor", lambda e, k=k, mloc=mloc, ts=ts,
                                            pr=pr, pw=pw:
                                            e.matmul(pr[:, 0:512],
                                                     BufC[:, C_SLAB + pw * 2048 + k * 256
                                                          + mloc * 128:
                                                          C_SLAB + pw * 2048 + k * 256
                                                          + (mloc + 1) * 128],
                                                     fmA[:, k * 2048 + ts * 512:
                                                         k * 2048 + (ts + 1) * 512],
                                                     start=(k == 0), stop=(k == 7)),
                                            waits=wts, inc=(eT if k == 7 else None))
                                yield
                                tE = op("scalar", lambda e, pr=pr, mloc=mloc, ts=ts,
                                        dst0=dst0: e.activation(
                                    actB[:, dst0 + mloc * 2048 + ts * 512:
                                         dst0 + mloc * 2048 + (ts + 1) * 512],
                                    pr[:, 0:512], AF.Identity, scale=SCALE_QK),
                                    waits=[(eT, lm)], inc=eS)
                                qkv_rot_hist.append((eS, tE))
                                qk_done[qg] = tE
                                yield
                        slab_lastmm[pw] = lm
                    else:
                        for c16 in range(16):
                            pr = p4
                            po = (c16 % 2) * 256
                            qkv_rot_cnt[0] += 1
                            for k in range(8):
                                wts = []
                                if k == 0:
                                    wts = [ln2_ev_w(c16)]
                                    if c16 == 0:
                                        wts.append((qsem, tW))
                                    if c16 % 2 == 0 and len(qkv_rot_hist) >= 1:
                                        wts.append(qkv_rot_hist[-1])
                                lm = op("tensor", lambda e, k=k, c16=c16, pr=pr,
                                        pw=pw, po=po:
                                        e.matmul(pr[:, po:po + 256],
                                                 fmA[:, k * 2048 + c16 * 128:
                                                     k * 2048 + (c16 + 1) * 128],
                                                 BufC[:, C_SLAB + pw * 2048 + k * 256:
                                                      C_SLAB + pw * 2048 + (k + 1) * 256],
                                                 start=(k == 0), stop=(k == 7)),
                                        waits=wts, inc=(eT if k == 7 else None))
                            yield
                            if c16 % 2 == 1:
                                dst = bass.AP(actB, a_v(pg) + (c16 - 1) * 260,
                                              [[ACTB_W, 128], [65, 8], [1, 64]])
                                srcv = pr[:, 0:512].rearrange(
                                    "p (hh w) -> p hh w", w=64)
                                wv = [(eT, lm), (eV, t_ones)]
                                tE = op("vector", lambda e, dst=dst, srcv=srcv:
                                        e.tensor_copy(dst, srcv), waits=wv, inc=eV)
                                qkv_rot_hist.append((eV, tE))
                                v_done[qg] = tE
                            yield
                        slab_lastmm[pw] = lm

            NE = 128
            its = [(qg, c, pair) for qg in range(4) for c in range(16)
                   for pair in range(2)]
            tPK, tRM, tEXP, tEPS, tTR, tD, tFME = {}, {}, {}, {}, {}, {}, {}
            tSA, tSBE, tA, tAM, tO, tSCL = {}, {}, {}, {}, {}, {}
            last_projk = {}

            def st_proj(t):
                qg, c, pair = its[t]
                pg = qg % 2
                P = (p0, p1, p2)[t % 3]
                m = pair
                wts = []
                if c == 0:
                    wts = [(eS, qk_done[qg]), (eV, v_done[qg]), (s_gam, t_gamall)]
                if (t - 3) in tEXP:
                    wts.append((eS, tEXP[t - 3]))
                qsl = actB[:, a_q(pg) + m * 2048 + c * 128:
                           a_q(pg) + m * 2048 + (c + 1) * 128]
                op("tensor", lambda e, qsl=qsl, P=P:
                   e.matmul(P[:, 0:256], qsl, omT_bf[:, :], start=True, stop=True),
                   waits=wts, inc=None)
                ksl = actB[:, a_k(pg) + m * 2048 + c * 128:
                           a_k(pg) + m * 2048 + (c + 1) * 128]
                tPK[t] = op("tensor", lambda e, ksl=ksl, P=P:
                            e.matmul(P[:, 256:512], ksl, omT_bf[:, :],
                                     start=True, stop=True), waits=(), inc=eT)
                last_projk[qg] = tPK[t]

            def st_rm(t):
                P = (p0, p1, p2)[t % 3]
                wts = [(eT, tPK[t])]
                if (t - 2) in tEXP:
                    wts.append((eS, tEXP[t - 2]))
                tRM[t] = op("vector", lambda e, P=P, t=t: e.tensor_reduce(
                    st_nm[:, (t % 2) * 4: (t % 2) * 4 + 4],
                    P[:, 0:512].rearrange("p (a b) -> p a b", a=4),
                    axis=AX.X, op=ALU.max, negate=True), waits=wts, inc=eV)

            def st_exp(t):
                P = (p0, p1, p2)[t % 3]
                tmo = T_TM + (t % 4) * 512
                tE = None
                for j in range(4):
                    wts = []
                    if j == 0:
                        wts = [(eV, tRM[t])]
                        if (t - 4) in tD:
                            wts.append((eT, tD[t - 4]))
                    tE = op("scalar", lambda e, P=P, tmo=tmo, j=j, t=t:
                            e.activation(
                        BufC[:, tmo + j * 128: tmo + (j + 1) * 128],
                        P[:, j * 128:(j + 1) * 128], AF.Exp,
                        bias=st_nm[:, (t % 2) * 4 + j: (t % 2) * 4 + j + 1]),
                        waits=wts, inc=(eS if j == 3 else None))
                tEXP[t] = tE

            def st_eps(t):
                tmo = T_TM + (t % 4) * 512
                tEPS[t] = op("vector", lambda e, tmo=tmo: e.tensor_scalar(
                    BufC[:, tmo: tmo + 512], BufC[:, tmo: tmo + 512],
                    INV_SQM, EPS_F, ALU.mult, ALU.add),
                    waits=[(eS, tEXP[t])], inc=eV)

            def st_tr_delta(t):
                qg, c, pair = its[t]
                pg = qg % 2
                tmo = T_TM + (t % 4) * 512
                tro = (t % 2) * 512
                wts = [(eV, tEPS[t]), (s_c, t_call)]
                if (t - 1) in tFME:
                    wts.append((eS, tFME[t - 1]))
                tt = None
                for j in range(4):
                    tt = op("tensor", lambda e, tmo=tmo, tro=tro, j=j:
                             e.transpose(p6a[:, tro + j * 128: tro + (j + 1) * 128],
                                         BufC[:, tmo + j * 128: tmo + (j + 1) * 128],
                                         ident_b[:, :]),
                             waits=(wts if j == 0 else ()),
                             inc=(eT if j == 3 else None))
                tTR[t] = tt
                wts = []
                if (t - 1) in tSA:
                    wts.append((eV, tSA[t - 1]))
                td = None
                for j in range(2):
                    vsl = actB[:, a_v(pg) + c * 260 + (pair * 2 + j) * 65:
                               a_v(pg) + c * 260 + (pair * 2 + j) * 65 + 65]
                    td = op("tensor", lambda e, tmo=tmo, j=j, t=t, vsl=vsl:
                            e.matmul(p3[:, (t % 2) * 256 + j * 128:
                                         (t % 2) * 256 + j * 128 + 65],
                                     BufC[:, tmo + 256 + j * 128:
                                          tmo + 256 + (j + 1) * 128],
                                     vsl, start=True, stop=True),
                            waits=(wts if j == 0 else ()),
                            inc=(eT if j == 1 else None))
                tD[t] = td

            def st_fme_sacc(t):
                qg, c, pair = its[t]
                tro = (t % 2) * 512
                fmo = T_FM + (t % 4) * 512
                wts = [(eT, tTR[t])]
                if (t - 4) in tO:
                    wts.append((eT, tO[t - 4]))
                tFME[t] = op("scalar", lambda e, tro=tro, fmo=fmo: e.copy(
                    BufC[:, fmo: fmo + 512], p6a[:, tro: tro + 512]),
                    waits=wts, inc=eS)
                sfo = ((qg % 2) * 2 + pair) * 132
                sfv = bass.AP(Sf_t, sfo, [[528, 128], [66, 2], [1, 65]])
                d3v = bass.AP(p3, (t % 2) * 256, [[512, 128], [128, 2], [1, 65]])
                wts = [(eT, tD[t])]
                if (t - 2) in tSA:
                    wts.append((eV, tSA[t - 2]))
                if (t - 2) in tSBE:
                    wts.append((eS, tSBE[t - 2]))
                if c == 0:
                    tSA[t] = op("vector", lambda e, sfv=sfv, d3v=d3v:
                                e.tensor_copy(sfv, d3v), waits=wts, inc=eV)
                else:
                    tSA[t] = op("vector", lambda e, sfv=sfv, d3v=d3v:
                                e.scalar_tensor_tensor(sfv, d3v, 1.0, sfv,
                                                       ALU.mult, ALU.add),
                                waits=wts, inc=eV)

            def st_abar_sbe(t):
                qg, c, pair = its[t]
                fmo = T_FM + (t % 4) * 512
                ao = (t % 2) * 256
                wts = [(eS, tFME[t])]
                if (t - 1) in tAM:
                    wts.append((eV, tAM[t - 1]))
                ta = None
                for j in range(2):
                    ta = op("tensor", lambda e, fmo=fmo, ao=ao, j=j: e.matmul(
                        p5[:, ao + j * 128: ao + (j + 1) * 128],
                        BufC[:, fmo + 256 + j * 128: fmo + 256 + (j + 1) * 128],
                        BufC[:, fmo + j * 128: fmo + (j + 1) * 128],
                        start=True, stop=True),
                        waits=(wts if j == 0 else ()), inc=(eT if j == 1 else None))
                tA[t] = ta
                sfo = ((qg % 2) * 2 + pair) * 132
                sbo = ((c % 3) * 2 + pair) * 132
                sbv = bass.AP(sbT, sbo, [[1056, 128], [66, 2], [1, 65]])
                sfr = bass.AP(Sf_t, sfo, [[528, 128], [66, 2], [1, 65]])
                wts = [(eV, tSA[t])]
                if (t - 4) in tO:
                    wts.append((eT, tO[t - 4]))
                tSBE[t] = op("scalar", lambda e, sbv=sbv, sfr=sfr:
                             e.copy(sbv, sfr), waits=wts, inc=eS)

            def st_mask(t):
                ao = (t % 2) * 256
                mb = bass.AP(maskji, 0, [[128, 128], [0, 2], [1, 128]])
                wts = [(eT, tA[t])]
                if (t - 2) in tO:
                    wts.append((eT, tO[t - 2]))
                amo = (t % 2) * 256
                tAM[t] = op("vector", lambda e, ao=ao, amo=amo, mb=mb: e.tensor_tensor(
                    amT[:, amo: amo + 256].rearrange("p (a b) -> p a b", a=2),
                    p5[:, ao: ao + 256].rearrange("p (a b) -> p a b", a=2),
                    mb, op=ALU.mult), waits=wts, inc=eV)

            def st_o(t):
                qg, c, pair = its[t]
                pg = qg % 2
                fmo = T_FM + (t % 4) * 512
                oo = (t % 2) * 256
                amo = (t % 2) * 256
                wts = [(eV, tAM[t])]
                if (t - 1) in tSCL:
                    wts.append((eV, tSCL[t - 1]))
                if c > 0:
                    wts.append((eS, tSBE[t - 2]))
                to = None
                for j in range(2):
                    vsl = actB[:, a_v(pg) + c * 260 + (pair * 2 + j) * 65:
                               a_v(pg) + c * 260 + (pair * 2 + j) * 65 + 65]
                    to = op("tensor", lambda e, oo=oo, amo=amo, j=j, vsl=vsl, c=c:
                            e.matmul(pOb[:, oo + j * 128: oo + j * 128 + 65],
                                     amT[:, amo + j * 128: amo + (j + 1) * 128],
                                     vsl, start=True, stop=(c == 0)),
                            waits=(wts if j == 0 else ()),
                            inc=(eT if (c == 0 and j == 1) else None))
                    if c > 0:
                        sbo = (((c - 1) % 3) * 2 + pair) * 132
                        sbs = sbT[:, sbo + j * 66: sbo + j * 66 + 65]
                        to = op("tensor", lambda e, oo=oo, fmo=fmo, j=j, sbs=sbs:
                                e.matmul(pOb[:, oo + j * 128: oo + j * 128 + 65],
                                         BufC[:, fmo + j * 128: fmo + (j + 1) * 128],
                                         sbs, start=False, stop=True),
                                waits=(), inc=(eT if j == 1 else None))
                tO[t] = to

            def st_tail(t):
                qg, c, pair = its[t]
                oo = (t % 2) * 256
                dnv = bass.AP(pOb, oo + 64, [[512, 128], [128, 2], [1, 1]])
                tdn = op("vector", lambda e, dnv=dnv, t=t: e.tensor_scalar(
                    den_t[:, (t % 2) * 2: (t % 2) * 2 + 2].rearrange(
                        "p (a b) -> p a b", b=1), dnv, EPS_F, None,
                    ALU.add, ALU.bypass), waits=[(eT, tO[t])], inc=eV)
                trc = op("vector", lambda e, t=t: e.reciprocal(
                    rec_t[:, (t % 2) * 2: (t % 2) * 2 + 2],
                    den_t[:, (t % 2) * 2: (t % 2) * 2 + 2]),
                    waits=[(eV, tdn)], inc=eV)
                outv = BufC[:, c * 1024 + (qg * 4 + pair * 2) * 64:
                            c * 1024 + (qg * 4 + pair * 2) * 64 + 128] \
                    .rearrange("p (a b) -> p a b", a=2)
                o6v = bass.AP(pOb, oo, [[512, 128], [128, 2], [1, 64]])
                rcb = bass.AP(rec_t, (t % 2) * 2, [[8, 128], [1, 2], [0, 64]])
                tSCL[t] = op("vector", lambda e, outv=outv, o6v=o6v, rcb=rcb:
                             e.tensor_tensor(outv, o6v, rcb, op=ALU.mult),
                             waits=[(eV, trc)], inc=eV)

            STAGES_A = [(6, st_abar_sbe),
                        (5, st_fme_sacc), (4, st_tr_delta), (3, st_eps),
                        (2, st_exp)]
            STAGES_B = [(1, st_rm), (0, st_proj)]

            gens = [qkv_gen(qg) for qg in range(4)]
            for _ in gens[0]:
                pass
            done_g = [True, False, False, False]

            def adv_gen(gq):
                if gq <= 3 and not done_g[gq]:
                    try:
                        next(gens[gq])
                    except StopIteration:
                        done_g[gq] = True

            for sl in range(NE + 9):
                if 0 <= sl - 8 < NE:
                    st_o(sl - 8)
                if 0 <= sl - 7 < NE:
                    st_mask(sl - 7)
                if 0 <= sl - 8 < NE:
                    st_tail(sl - 8)
                gq = sl // 32 + 1
                adv_gen(gq)
                for ofs, fn in STAGES_A:
                    t = sl - ofs
                    if 0 <= t < NE:
                        fn(t)
                adv_gen(gq)
                for ofs, fn in STAGES_B:
                    t = sl - ofs
                    if 0 <= t < NE:
                        fn(t)
                if sl % 32 == 31 and gq <= 3 and not done_g[gq]:
                    for _ in gens[gq]:
                        pass
                    done_g[gq] = True
            barrier()

            tWo = dma("gpsimd", s_wo, BufC[:, C_WO:C_WO + 4096]
                      .rearrange("p (k j) -> p k j", k=4), Wo_sw[:, 0:4, :],
                      max_dma_last_dim=8192)
            tWo2 = dma("gpsimd", cwu[3], BufC[:, C_WO + 4096:C_WO + 8192]
                       .rearrange("p (k j) -> p k j", k=4), Wo_sw[:, 4:8, :],
                       max_dma_last_dim=8192)
            ppair = ((p0, p1), (p2, p3))
            tY, tAF, tWoMM = {}, {}, {}
            ypend = {}

            def emit_ypair(c):
                ba, bb, tmpo, tTmpc = ypend[c]
                op("vector", lambda e, c=c, ba=ba, tmpo=tmpo: e.tensor_tensor(
                    actB[:, A_Y + c * 1024: A_Y + c * 1024 + 512],
                    actB[:, tmpo: tmpo + 512], ba[:, 0:512], op=ALU.add),
                    waits=[(eS, tTmpc), (eT, tWoMM[c])], inc=eV)
                tY[c] = op("vector", lambda e, c=c, bb=bb, tmpo=tmpo: e.tensor_tensor(
                    actB[:, A_Y + c * 1024 + 512: A_Y + (c + 1) * 1024],
                    actB[:, tmpo + 512: tmpo + 1024], bb[:, 0:512], op=ALU.add),
                    waits=(), inc=eV)

            for c in range(NT):
                pi = c % 2
                fi = c % 3
                wts = []
                if c >= 2:
                    wts.append((eV, tAF[c - 2]))
                tt = None
                for j in range(8):
                    tt = op("tensor", lambda e, j=j, pi=pi, c=c: e.transpose(
                        p6[pi][:, j * 128:(j + 1) * 128],
                        BufC[:, c * 1024 + j * 128: c * 1024 + (j + 1) * 128],
                        ident_b[:, :]), waits=(wts if j == 0 else ()), inc=eT)
                wv = [(eT, tt)]
                if c >= 3:
                    wv.append((eT, tWoMM[c - 3]))
                tAF[c] = op("vector", lambda e, pi=pi, fi=fi: e.tensor_copy(
                    actB[:, A_LNTM + fi * 1024: A_LNTM + (fi + 1) * 1024],
                    p6[pi][:, 0:1024]), waits=wv, inc=eV)
                ba, bb = ppair[pi]
                for k in range(8):
                    wts = []
                    if k == 0:
                        wts = [(eV, tAF[c])]
                        if c <= 1:
                            wts.append((s_wo, tWo))
                        if c >= 2:
                            wts.append((eV, tY[c - 2]))
                    elif k == 4 and c == 0:
                        wts = [(cwu[3], tWo2)]
                    op("tensor", lambda e, k=k, fi=fi, ba=ba: e.matmul(
                        ba[:, 0:512],
                        actB[:, A_LNTM + fi * 1024 + k * 128: A_LNTM + fi * 1024 + (k + 1) * 128],
                        BufC[:, C_WO + k * 1024: C_WO + k * 1024 + 512],
                        start=(k == 0), stop=(k == 7)), waits=wts, inc=None)
                    tWoMM[c] = op("tensor", lambda e, k=k, fi=fi, bb=bb: e.matmul(
                        bb[:, 0:512],
                        actB[:, A_LNTM + fi * 1024 + k * 128: A_LNTM + fi * 1024 + (k + 1) * 128],
                        BufC[:, C_WO + k * 1024 + 512: C_WO + (k + 1) * 1024],
                        start=(k == 0), stop=(k == 7)), waits=(),
                        inc=(eT if k == 7 else None))
                tmpo = 20480 + pi * 1024
                wts = []
                if c >= 2:
                    wts.append((eV, tY[c - 2]))
                tTmp = op("scalar", lambda e, c=c, tmpo=tmpo: e.activation(
                    actB[:, tmpo: tmpo + 1024], x_tm[:, c * 1024:(c + 1) * 1024],
                    AF.Identity, bias=m2r2[:, c:c + 1], scale=rstd2[:, c:c + 1]),
                    waits=wts, inc=eS)
                ypend[c] = (ba, bb, tmpo, tTmp)
                if c >= 1:
                    emit_ypair(c - 1)
            emit_ypair(NT - 1)
            barrier()

            BufCf = BufC.bitcast(F32)
            t_le2 = op("vector", lambda e: e.memset(lneps[:, :], 1e6), inc=eV)
            dump2i = actB[:, A_LNTM + 2048:A_LNTM + 3072]
            ev_tag, tr_tag, hb_tag = {}, {}, {}

            def emit_iev(c):
                pi = c % 2
                te = None
                for k in range(8):
                    dst = fmA[:, k * 2048 + c * 128: k * 2048 + (c + 1) * 128]
                    src = p6[pi][:, k * 128:(k + 1) * 128]
                    te = op("vector", lambda e, dst=dst, src=src, k=k:
                            e.tensor_scalar(dst, src, gamF_s[:, k:k + 1],
                                            betF_s[:, k:k + 1], ALU.mult, ALU.add),
                            waits=([(eT, tr_tag[c])] if k == 0 else ()),
                            inc=(eV if k == 7 else None))
                ev_tag[c] = (eV, te)

            tRSi = {}

            def emit_istats(c):
                tRi = op("scalar", lambda e, c=c: e.activation(
                    dump2i, actB[:, A_Y + c * 1024: A_Y + (c + 1) * 1024],
                    AF.Identity, accum_out=musum[:, c:c + 1]),
                    waits=[(eV, tY[c])], inc=eS)
                tMUi = op("gpsimd", lambda e, c=c: e.tensor_scalar(
                    mu_c[:, c:c + 1], musum[:, c:c + 1], 1.0 / 1024.0, None,
                    ALU.mult, ALU.bypass), waits=[(eS, tRi)], inc=eG)
                tSQi = op("scalar", lambda e, c=c: e.activation(
                    dump2i, actB[:, A_Y + c * 1024: A_Y + (c + 1) * 1024],
                    AF.Square, bias=mu_c[:, c:c + 1], scale=-1.0,
                    accum_out=varsum[:, c:c + 1]), waits=[(eG, tMUi)], inc=eS)
                wl = [(eS, tSQi)] + ([(eV, t_le2)] if c == 0 else [])
                tLNi = op("scalar", lambda e, c=c: e.activation(
                    varsum[:, c:c + 1], varsum[:, c:c + 1], AF.Ln,
                    bias=lneps[:, 0:1], scale=1.0 / 1024.0), waits=wl, inc=eS)
                tRSi[c] = op("scalar", lambda e, c=c: e.activation(
                    rstd_c[:, c:c + 1], varsum[:, c:c + 1], AF.Exp, scale=-0.5),
                    waits=[(eS, tLNi)], inc=eS)

            def emit_ixform(c):
                pi = c % 2
                lntm = actB[:, A_LNTM + pi * 1024: A_LNTM + (pi + 1) * 1024]
                wts = [(eS, tRSi[c])]
                if c == 0:
                    wts.append((s_gam, t_gamall))
                if c >= 2:
                    wts.append((eT, tr_tag[c - 2]))
                tT3 = op("vector", lambda e, c=c, lntm=lntm: e.tensor_scalar(
                    lntm, actB[:, A_Y + c * 1024: A_Y + (c + 1) * 1024],
                    mu_c[:, c:c + 1], rstd_c[:, c:c + 1],
                    ALU.subtract, ALU.mult), waits=wts, inc=eV)
                w2 = [(eV, tT3)]
                if c >= 2:
                    w2.append(ev_tag[c - 2])
                tt = None
                for j in range(8):
                    tt = op("tensor", lambda e, j=j, pi=pi, lntm=lntm: e.transpose(
                        p6[pi][:, j * 128:(j + 1) * 128], lntm[:, j * 128:(j + 1) * 128],
                        ident_b[:, :]), waits=(w2 if j == 0 else ()), inc=eT)
                tr_tag[c] = tt

            for c in range(NT + 2):
                if c < NT:
                    emit_istats(c)
                if c >= 2:
                    emit_ixform(c - 2)
                if c >= 3:
                    emit_iev(c - 3)
            emit_iev(NT - 1)
            barrier()

            relu_hist = []
            w1mm_hist = []
            w2mm_hist = []
            o2_hist = []
            tr2_hist = []
            xu2_hist = []
            w2_last_of_quarter = 0
            for tq in range(4):
                ts0 = tq * 512
                tW1g = {}

                def w1_load(gf, tq=tq):
                    pi4 = gf % 2
                    w1sem = (swA, swB)[pi4]
                    wts = [(cw1, cast_w1)]
                    if w1mm_hist:
                        wts.append((eT, w1mm_hist[-1]))
                    tW1g[gf] = dma("sync", w1sem,
                                   BufC[:, pi4 * 4096:(pi4 + 1) * 4096]
                                   .rearrange("p (f k j) -> p f k j", f=4, k=8),
                                   W1_bf[:, gf * 4:(gf + 1) * 4, :, :], waits=wts)
                w1_load(0)
                w1_load(1)
                for f in range(NF):
                    pi = f % 2
                    gf = f // 4
                    if f % 4 == 0 and f > 0 and gf + 1 <= 7:
                        w1_load(gf + 1)
                    w1sem = (swA, swB)[gf % 2]
                    w1of = (gf % 2) * 4096 + (f % 4) * 1024
                    lm = None
                    for k in range(8):
                        wts = []
                        if k == 0:
                            if f % 4 == 0:
                                wts.append((w1sem, tW1g[gf]))
                            if len(relu_hist) >= 2:
                                wts.append((eV, relu_hist[-2]))
                        lm = op("tensor", lambda e, k=k, pi=pi, ts0=ts0, w1of=w1of:
                                e.matmul(
                            pP[pi][:, 0:512],
                            BufC[:, w1of + k * 128: w1of + (k + 1) * 128],
                            fmA[:, k * 2048 + ts0: k * 2048 + ts0 + 512],
                            start=(k == 0), stop=(k == 7)),
                            waits=wts, inc=(eT if k == 7 else None))
                    w1mm_hist.append(lm)
                    wts = [(eT, lm)]
                    if tq > 0 and f == 0:
                        wts.append((eT, w2_last_of_quarter))
                    tRl = op("vector", lambda e, pi=pi, f=f: e.tensor_scalar(
                        actB[:, A_RELU + f * 512: A_RELU + (f + 1) * 512],
                        pP[pi][:, 0:512], b1t[:, f:f + 1], 0.0, ALU.add, ALU.max),
                        waits=wts, inc=eV)
                    relu_hist.append(tRl)
                for d in range(8):
                    pi = d % 2
                    w2sem = (sW2a, sW2b)[pi]
                    wts = [(cw2, cast_w2)]
                    if len(w2mm_hist) >= 2:
                        wts.append((eT, w2mm_hist[-2]))
                    tW2 = dma("sync", w2sem, BufC[:, 8192 + pi * 4096: 8192 + (pi + 1) * 4096]
                              .rearrange("p (k j) -> p k j", k=32),
                              W2_bf[:, d, :, :], waits=wts)
                    lm = None
                    for f in range(NF):
                        wts = []
                        if f == 0:
                            wts = [(w2sem, tW2), (eV, relu_hist[-1])]
                            if len(o2_hist) >= 2:
                                wts.append((eS, o2_hist[-2]))
                        lm = op("tensor", lambda e, f=f, pi=pi:
                                e.matmul(
                            pQ[pi][:, 0:512],
                            BufC[:, 8192 + pi * 4096 + f * 128: 8192 + pi * 4096 + (f + 1) * 128],
                            actB[:, A_RELU + f * 512: A_RELU + (f + 1) * 512],
                            start=(f == 0), stop=(f == NF - 1)),
                            waits=wts, inc=(eT if f == NF - 1 else None))
                    w2mm_hist.append(lm)
                    w2_last_of_quarter = lm
                    o2reg = tmpF[:, 0:512] if pi == 0 else tmpF[:, 512:1024]
                    wts = [(eT, lm)]
                    if len(tr2_hist) >= 2:
                        wts.append((eT, tr2_hist[-2]))
                    tO2 = op("scalar", lambda e, pi=pi, d=d, o2reg=o2reg:
                             e.activation(o2reg, pQ[pi][:, 0:512], AF.Identity,
                                          bias=b2t[:, d:d + 1]),
                             waits=wts, inc=eS)
                    o2_hist.append(tO2)
                    ptr = (p4, p5)[pi]
                    wts = [(eS, tO2)]
                    if len(xu2_hist) >= 2:
                        wts.append((eV, xu2_hist[-2]))
                    tt = None
                    for j in range(4):
                        tt = op("tensor", lambda e, j=j, ptr=ptr, o2reg=o2reg: e.transpose(
                            ptr[:, j * 128:(j + 1) * 128],
                            o2reg[:, j * 128:(j + 1) * 128], ident_f[:, :]),
                            waits=(wts if j == 0 else ()), inc=(eT if j == 3 else None))
                    tr2_hist.append(tt)
                    tx = None
                    for j in range(4):
                        ct = tq * 4 + j
                        tx = op("vector", lambda e, j=j, ptr=ptr, ct=ct, d=d:
                                e.scalar_tensor_tensor(
                            x_tm[:, ct * 1024 + d * 128: ct * 1024 + (d + 1) * 128],
                            ptr[:, j * 128:(j + 1) * 128], 1.0,
                            x_tm[:, ct * 1024 + d * 128: ct * 1024 + (d + 1) * 128],
                            ALU.mult, ALU.add),
                            waits=([(eT, tt)] if j == 0 else ()), inc=eV)
                    xu2_hist.append(tx)
            barrier()

        if stage >= 3:
            ln3_ev = emit_outer_ln(1e-5, pipe=True, pipe_chunks=False,
                                   pipe_stagger=True)
            ffn2_x = {}
            emit_ffn(ln3_ev, xtags=ffn2_x)
            for c in range(NT):
                dma("sync" if c % 2 == 0 else "scalar", s_x,
                    out[c * 128:(c + 1) * 128, :],
                    x_tm[:, c * 1024:(c + 1) * 1024], waits=[ffn2_x[c]])
            barrier()
        else:
            for c in range(NT):
                dma("sync", s_x, out[c * 128:(c + 1) * 128, :],
                    x_tm[:, c * 1024:(c + 1) * 1024])
        op("sync", lambda e: None, waits=[(s_x, s_x.n)])

        with nc.Block() as block:
            @block.sync
            def _(e):
                for f in ops["sync"]:
                    f(e)

            @block.gpsimd
            def _(e):
                for f in ops["gpsimd"]:
                    f(e)

            @block.tensor
            def _(e):
                for f in ops["tensor"]:
                    f(e)

            @block.vector
            def _(e):
                for f in ops["vector"]:
                    f(e)

            @block.scalar
            def _(e):
                for f in ops["scalar"]:
                    f(e)

    return nc


def _omd(omega):
    d = np.zeros((128, 256), np.float32)
    d[0:64, 0:128] = omega.T
    d[64:128, 128:256] = omega.T
    return d


def _prep_inputs(inputs):
    f32 = np.float32
    Wq = np.asarray(inputs["L0_attn_Wq"], f32)
    Wk = np.asarray(inputs["L0_attn_Wk"], f32)
    Wv = np.asarray(inputs["L0_attn_Wv"], f32)
    Wo = np.asarray(inputs["L0_attn_Wo"], f32)
    W1 = np.asarray(inputs["L0_attn_W1"], f32)
    W2 = np.asarray(inputs["L0_attn_W2"], f32)
    b1 = np.asarray(inputs["L0_attn_b1"], f32)
    b2 = np.asarray(inputs["L0_attn_b2"], f32)
    gamma = np.asarray(inputs["L0_attn_gamma"], f32)
    beta = np.asarray(inputs["L0_attn_beta"], f32)
    omega = np.asarray(inputs["L0_attn_omega"], f32)
    wg = np.asarray(inputs["L0_ffn_w_gate"], np.int32)
    wu = np.asarray(inputs["L0_ffn_w_up"], np.int32)
    wd = np.asarray(inputs["L0_ffn_w_down"], np.int32)

    c = np.ascontiguousarray
    shared = {
        "wgT_i": c(wg.T),
        "wuT_i": c(wu.T),
        "wdT_i": c(wd.T),
        "Wq_t": c(Wq.reshape(8, 128, 2, 512).transpose(2, 1, 0, 3)),
        "Wk_t": c(Wk.reshape(8, 128, 2, 512).transpose(2, 1, 0, 3)),
        "Wv_t": c(Wv.reshape(8, 128, 2, 512).transpose(2, 1, 0, 3)),
        "Wo_sw": c(Wo.reshape(8, 128, 1024).transpose(1, 0, 2)),
        "W1_sw": c(W1.reshape(8, 128, 32, 128).transpose(1, 2, 0, 3)),
        "W2_sw": c(W2.reshape(32, 128, 8, 128).transpose(1, 2, 0, 3)),
        "omD": _omd(omega),
        "b1t": c(b1.reshape(32, 128).T),
        "b2t": c(b2.reshape(8, 128).T),
        "gam_bc": c(np.broadcast_to(gamma[None, :], (128, 1024))),
        "gamF": c(gamma.reshape(8, 128).T),
        "betF": c(beta.reshape(8, 128).T),
        "bet_bc": c(np.broadcast_to(beta[None, :], (128, 1024))),
        "ident_bf": np.eye(128, dtype=BFnp),
        "ident_f32": np.eye(128, dtype=f32),
        "maskji": (np.arange(128)[:, None] <= np.arange(128)[None, :]).astype(BFnp),
    }
    return shared


def run_device(inputs, stage=3, core_ids=None):
    shared = _prep_inputs(inputs)
    batch = np.asarray(inputs["L0_attn_batch"], np.float32)
    if core_ids is None:
        core_ids = list(range(8))
    nc = build(stage)
    in_maps = []
    for b in range(len(core_ids)):
        m = dict(shared)
        m["xin"] = np.ascontiguousarray(batch[b])
        in_maps.append(m)
    res = run_bass_kernel_spmd(nc, in_maps, core_ids=core_ids)
    return np.stack([res.results[i]["out"] for i in range(len(core_ids))], axis=0)


def kernel(**inputs):
    return run_device(inputs, stage=3, core_ids=list(range(8)))


# version 62
# speedup vs baseline: 1.1960x; 1.1960x over previous
import contextlib
import numpy as np
import ml_dtypes

import concourse.bass as bass
import concourse.mybir as mybir
from concourse.bass_utils import run_bass_kernel_spmd

F32 = mybir.dt.float32
BF16 = mybir.dt.bfloat16
I32 = mybir.dt.int32
AF = mybir.ActivationFunctionType
ALU = mybir.AluOpType
AX = mybir.AxisListType
BFnp = ml_dtypes.bfloat16

NT = 16
NHC = 24
NF = 32
SCALE_QK = float(64.0 ** -0.25)
INV127 = float(1.0 / 127.0)
HALF127 = float(0.5 / 127.0)
INV_SQM = float(1.0 / np.sqrt(np.float64(128.0)))
EPS_F = 1e-6
ENGINES = ("sync", "gpsimd", "tensor", "vector", "scalar")

A_Q = 0
A_K = 8192
A_V = 16384
A_Y = 0
A_RELU = 0
A_LNTM = 16384
ACTB_W = 24704

C_ATT = 0
C_QKVW = 16384
C_WO = 16384
BUFC_W = 24576


class Sem:
    def __init__(self, h):
        self.h = h
        self.n = 0


def build(stage=3, attn_lvl=6):
    nc = bass.Bass("TRN2", target_bir_lowering=False)

    xin = nc.dram_tensor("xin", [2048, 1024], F32, kind="ExternalInput").ap()
    wgT_i = nc.dram_tensor("wgT_i", [1024, 3072], I32, kind="ExternalInput").ap()
    wuT_i = nc.dram_tensor("wuT_i", [1024, 3072], I32, kind="ExternalInput").ap()
    wdT_i = nc.dram_tensor("wdT_i", [3072, 1024], I32, kind="ExternalInput").ap()
    Wq_t = nc.dram_tensor("Wq_t", [2, 128, 8, 512], F32, kind="ExternalInput").ap()
    Wk_t = nc.dram_tensor("Wk_t", [2, 128, 8, 512], F32, kind="ExternalInput").ap()
    Wv_t = nc.dram_tensor("Wv_t", [2, 128, 8, 512], F32, kind="ExternalInput").ap()
    Wo_sw = nc.dram_tensor("Wo_sw", [128, 8, 1024], F32, kind="ExternalInput").ap()
    W1_sw = nc.dram_tensor("W1_sw", [128, 32, 8, 128], F32, kind="ExternalInput").ap()
    W2_sw = nc.dram_tensor("W2_sw", [128, 8, 32, 128], F32, kind="ExternalInput").ap()
    omTd_in = nc.dram_tensor("omD", [128, 256], F32, kind="ExternalInput").ap()
    b1t_in = nc.dram_tensor("b1t", [128, 32], F32, kind="ExternalInput").ap()
    b2t_in = nc.dram_tensor("b2t", [128, 8], F32, kind="ExternalInput").ap()
    gam_in = nc.dram_tensor("gam_bc", [128, 1024], F32, kind="ExternalInput").ap()
    gamF_in = nc.dram_tensor("gamF", [128, 8], F32, kind="ExternalInput").ap()
    betF_in = nc.dram_tensor("betF", [128, 8], F32, kind="ExternalInput").ap()
    bet_in = nc.dram_tensor("bet_bc", [128, 1024], F32, kind="ExternalInput").ap()
    idb_in = nc.dram_tensor("ident_bf", [128, 128], BF16, kind="ExternalInput").ap()
    idf_in = nc.dram_tensor("ident_f32", [128, 128], F32, kind="ExternalInput").ap()
    msk_in = nc.dram_tensor("maskji", [128, 128], BF16, kind="ExternalInput").ap()
    out = nc.dram_tensor("out", [2048, 1024], F32, kind="ExternalOutput").ap()

    wgT_bf = nc.dram_tensor("wgT_bf", [1024, 3072], BF16, kind="Internal").ap()
    wuT_bf = nc.dram_tensor("wuT_bf", [1024, 3072], BF16, kind="Internal").ap()
    wdT_bf = nc.dram_tensor("wdT_bf", [3072, 1024], BF16, kind="Internal").ap()
    W1_bf = nc.dram_tensor("W1_bf", [128, 32, 8, 128], BF16, kind="Internal").ap()
    W2_bf = nc.dram_tensor("W2_bf", [128, 8, 32, 128], BF16, kind="Internal").ap()
    Wq_bf = nc.dram_tensor("Wq_bf", [2, 128, 8, 512], BF16, kind="Internal").ap()
    Wk_bf = nc.dram_tensor("Wk_bf", [2, 128, 8, 512], BF16, kind="Internal").ap()
    Wv_bf = nc.dram_tensor("Wv_bf", [2, 128, 8, 512], BF16, kind="Internal").ap()

    es = contextlib.ExitStack()
    with es:
        x_tm = es.enter_context(nc.sbuf_tensor("x_tm", [128, NT * 1024], F32))
        fmA = es.enter_context(nc.sbuf_tensor("fmA", [128, 8 * 2048], BF16))
        actB = es.enter_context(nc.sbuf_tensor("actB", [128, ACTB_W], BF16))
        BufC = es.enter_context(nc.sbuf_tensor("BufC", [128, BUFC_W], BF16))
        tmpF = es.enter_context(nc.sbuf_tensor("tmpF", [128, 1024], F32))
        gam_bf = es.enter_context(nc.sbuf_tensor("gam_bf", [128, 1024], BF16))
        gamF_s = es.enter_context(nc.sbuf_tensor("gamF_s", [128, 8], F32))
        betF_s = es.enter_context(nc.sbuf_tensor("betF_s", [128, 8], F32))
        bet_bf = es.enter_context(nc.sbuf_tensor("bet_bf", [128, 1024], BF16))
        Sf_t = es.enter_context(nc.sbuf_tensor("Sf_t", [128, 528], F32))
        amT = es.enter_context(nc.sbuf_tensor("amT", [128, 512], BF16))
        sbT = es.enter_context(nc.sbuf_tensor("sbT", [128, 1056], BF16))
        den_t = es.enter_context(nc.sbuf_tensor("den_t", [128, 8], F32))
        rec_t = es.enter_context(nc.sbuf_tensor("rec_t", [128, 8], F32))
        musum = es.enter_context(nc.sbuf_tensor("musum", [128, 16], F32))
        lneps = es.enter_context(nc.sbuf_tensor("lneps", [128, 1], F32))
        mu_c = es.enter_context(nc.sbuf_tensor("mu_c", [128, 16], F32))
        varsum = es.enter_context(nc.sbuf_tensor("varsum", [128, 16], F32))
        rstd_c = es.enter_context(nc.sbuf_tensor("rstd_c", [128, 16], F32))
        mu2 = es.enter_context(nc.sbuf_tensor("mu2", [128, 16], F32))
        rstd2 = es.enter_context(nc.sbuf_tensor("rstd2", [128, 16], F32))
        m2r2 = es.enter_context(nc.sbuf_tensor("m2r2", [128, 16], F32))
        st_nm = es.enter_context(nc.sbuf_tensor("st_nm", [128, 16], F32))
        omT_bf = es.enter_context(nc.sbuf_tensor("omT_bf", [128, 256], BF16))
        ident_b = es.enter_context(nc.sbuf_tensor("ident_b", [128, 128], BF16))
        ident_f = es.enter_context(nc.sbuf_tensor("ident_f", [128, 128], F32))
        maskji = es.enter_context(nc.sbuf_tensor("maskji_s", [128, 128], BF16))
        b1t = es.enter_context(nc.sbuf_tensor("b1t_s", [128, 32], F32))
        b2t = es.enter_context(nc.sbuf_tensor("b2t_s", [128, 8], F32))

        p0 = es.enter_context(nc.psum_tensor("p0", [128, 512], F32))
        p1 = es.enter_context(nc.psum_tensor("p1", [128, 512], F32))
        p2 = es.enter_context(nc.psum_tensor("p2", [128, 512], F32))
        p3 = es.enter_context(nc.psum_tensor("p3", [128, 512], F32))
        p4 = es.enter_context(nc.psum_tensor("p4", [128, 512], F32))
        p5 = es.enter_context(nc.psum_tensor("p5", [128, 512], F32))
        p6a = es.enter_context(nc.psum_tensor("p6a", [128, 1024], BF16))
        p6b = es.enter_context(nc.psum_tensor("p6b", [128, 1024], BF16))
        pP = (p0, p1)
        pQ = (p2, p3)
        p6 = (p6a, p6b)
        pOb = p6b.bitcast(F32)

        def new_sem(name):
            return Sem(es.enter_context(nc.semaphore(name)))
        s_x = new_sem("s_x")
        s_xc = [s_x] + [new_sem(f"s_x{k}") for k in (1, 2, 3)]
        s_c = new_sem("s_c")
        cwg = [new_sem(f"cwg{q}") for q in range(4)]
        cwu = [new_sem(f"cwu{q}") for q in range(4)]
        cwd = new_sem("cwd")
        cw1 = new_sem("cw1")
        cw2 = new_sem("cw2")
        s_gam = new_sem("s_gam")
        cwuh = [new_sem(f"cwuh{i}") for i in range(4)]
        swA = new_sem("swA")
        swB = new_sem("swB")
        s_wd = new_sem("s_wd")
        sW2a = new_sem("sW2a")
        sW2b = new_sem("sW2b")
        qkvA = new_sem("qkvA")
        qkvB = new_sem("qkvB")
        s_wo = new_sem("s_wo")
        sync_dsems = [s_x, s_c, swA, swB, s_wd, sW2a, sW2b, qkvB] + s_xc[1:]
        gp_dsems = cwg + cwu + cwuh + [cwd, cw1, cw2, s_gam, qkvA, s_wo]
        eT = Sem(es.enter_context(nc.semaphore("eT")))
        eV = Sem(es.enter_context(nc.semaphore("eV")))
        eS = Sem(es.enter_context(nc.semaphore("eS")))
        mSY = Sem(es.enter_context(nc.semaphore("mSY")))
        mGP = Sem(es.enter_context(nc.semaphore("mGP")))
        eG = Sem(es.enter_context(nc.semaphore("eG")))

        ops = {e: [] for e in ENGINES}

        def op(eng, fn, waits=(), inc=None, inc_by=1):
            tag = None
            if inc is not None:
                inc.n += inc_by
                tag = inc.n
            wl = tuple((s, v) for (s, v) in waits if v and v > 0)

            def run(e, fn=fn, wl=wl, inc=inc, inc_by=inc_by):
                for (s, v) in wl:
                    e.wait_ge(s.h, v)
                r = fn(e)
                if inc is not None:
                    r.then_inc(inc.h, inc_by)
            ops[eng].append(run)
            return tag

        def dma(q, sem, out_ap, in_ap, waits=(), **kw):
            return op(q, lambda e, o=out_ap, i=in_ap, kw=dict(kw): e.dma_start(o, i, **kw),
                      waits, inc=sem, inc_by=16)

        def barrier():
            t1 = op("sync", lambda e: e.nop(),
                    waits=[(sm, sm.n) for sm in sync_dsems], inc=mSY)
            t2 = op("gpsimd", lambda e: e.nop(),
                    waits=[(sm, sm.n) for sm in gp_dsems] + [(eG, eG.n)], inc=mGP)
            t3 = op("tensor", lambda e: e.nop(), waits=[(eT, eT.n)], inc=eT)
            t4 = op("vector", lambda e: e.nop(), waits=[(eV, eV.n)], inc=eV)
            t5 = op("scalar", lambda e: e.nop(), waits=[(eS, eS.n)], inc=eS)
            wl = [(mSY, t1), (mGP, t2), (eT, t3), (eV, t4), (eS, t5)]
            for eng in ENGINES:
                op(eng, lambda e: None, waits=wl)

        t_xchunk = {}
        for c in range(NT):
            t = dma("sync" if c % 2 == 0 else "scalar", s_xc[c // 4],
                    x_tm[:, c * 1024:(c + 1) * 1024],
                    xin[c * 128:(c + 1) * 128, :])
            t_xchunk[c // 4] = t
        t_xall = s_x.n
        dma("sync", s_c, ident_b[:, :], idb_in)
        dma("sync", s_c, ident_f[:, :], idf_in)
        dma("sync", s_c, maskji[:, :], msk_in)
        dma("sync", s_c, b1t[:, :], b1t_in)
        dma("sync", s_c, b2t[:, :], b2t_in)
        t_call = s_c.n

        cast_wd_box = [None]

        lneps_tag = [0]

        def emit_outer_ln(eps, mu_store=None, rstd_store=None, src_wait=None,
                          src_tags=None, xl_wait=None, pipe=False,
                          pipe_chunks=True, pipe_stagger=False):
            mu_d = mu_store if mu_store is not None else mu_c
            rs_d = rstd_store if rstd_store is not None else rstd_c
            tRSd = {}
            if pipe:
                t_le = op("vector", lambda e, eps=float(eps):
                          e.memset(lneps[:, :], eps), inc=eV)
                lneps_tag[0] = t_le
                dump2 = actB[:, A_LNTM + 2048:A_LNTM + 3072]

                def emit_pstats(c):
                    w = []
                    if pipe_chunks and c % 4 == 0:
                        w = [(s_xc[c // 4], t_xchunk[c // 4])]
                    tR = op("vector", lambda e, c=c: e.reduce_sum(
                        musum[:, c:c + 1], x_tm[:, c * 1024:(c + 1) * 1024],
                        axis=AX.X), waits=w, inc=eV)
                    tMU = op("vector", lambda e, c=c, mu=mu_d: e.tensor_scalar(
                        mu[:, c:c + 1], musum[:, c:c + 1], 1.0 / 1024.0, None,
                        ALU.mult, ALU.bypass), waits=[(eV, tR)], inc=eV)
                    tSQ = op("scalar", lambda e, c=c, mu=mu_d, dump2=dump2:
                             e.activation(
                        dump2, x_tm[:, c * 1024:(c + 1) * 1024], AF.Square,
                        bias=mu[:, c:c + 1], scale=-1.0,
                        accum_out=varsum[:, c:c + 1]),
                        waits=[(eV, tMU)], inc=eS)
                    wl = [(eS, tSQ)] + ([(eV, t_le)] if c == 0 else [])
                    tLN = op("scalar", lambda e, c=c: e.activation(
                        varsum[:, c:c + 1], varsum[:, c:c + 1], AF.Ln,
                        bias=lneps[:, 0:1], scale=1.0 / 1024.0),
                        waits=wl, inc=eS)
                    tRSd[c] = op("scalar", lambda e, c=c, rs=rs_d: e.activation(
                        rs[:, c:c + 1], varsum[:, c:c + 1], AF.Exp, scale=-0.5),
                        waits=[(eS, tLN)], inc=eS)
            else:
                tR = None
                for c in range(NT):
                    if src_tags is not None:
                        w = [src_tags[c]]
                    else:
                        w = [src_wait] if (src_wait and c == 0) else []
                    tR = op("vector", lambda e, c=c: e.reduce_sum(
                        musum[:, c:c + 1], x_tm[:, c * 1024:(c + 1) * 1024], axis=AX.X),
                        waits=w, inc=eV)
                tMU = op("vector", lambda e, mu=mu_d: e.tensor_scalar(
                    mu[:, :], musum[:, :], 1.0 / 1024.0, None, ALU.mult, ALU.bypass),
                    waits=[(eV, tR)], inc=eV)
                tVS = None
                for c in range(NT):
                    w = [(eV, tMU)] if c == 0 else [(eS, tVS)]
                    tVS = op("scalar", lambda e, c=c, mu=mu_d: e.activation(
                        actB[:, A_LNTM:A_LNTM + 1024], x_tm[:, c * 1024:(c + 1) * 1024], AF.Square,
                        bias=mu[:, c:c + 1], scale=-1.0, accum_out=varsum[:, c:c + 1]),
                        waits=w, inc=eS)
                tSV = op("vector", lambda e, eps=float(eps): e.tensor_scalar(
                    varsum[:, :], varsum[:, :], 1.0 / 1024.0, eps, ALU.mult, ALU.add),
                    waits=[(eS, tVS)], inc=eV)
                tLN = op("scalar", lambda e: e.activation(varsum[:, :], varsum[:, :], AF.Ln),
                         waits=[(eV, tSV)], inc=eS)
                tRS = op("scalar", lambda e, rs=rs_d: e.activation(
                    rs[:, :], varsum[:, :], AF.Exp, scale=-0.5),
                    waits=[(eS, tLN)], inc=eS)
                for c in range(NT):
                    tRSd[c] = tRS
                if mu_store is not None:
                    op("vector", lambda e, mu=mu_d, rs=rs_d: e.scalar_tensor_tensor(
                        m2r2[:, :], mu[:, :], -1.0, rs[:, :], ALU.mult, ALU.mult),
                        waits=[(eS, tRS)], inc=eV)
            ev_tag, xl_tag, tr_tag = {}, {}, {}

            def emit_pxform(c):
                pi = c % 2
                lntm = actB[:, A_LNTM + pi * 1024: A_LNTM + (pi + 1) * 1024]
                w = [(eS, tRSd[c])]
                if xl_wait is not None and c == 0:
                    w.append(xl_wait)
                if c >= 2:
                    w.append((eT, tr_tag[c - 2]))
                xl_tag[c] = op("vector", lambda e, c=c, lntm=lntm, mu=mu_d, rs=rs_d:
                               e.tensor_scalar(lntm, x_tm[:, c * 1024:(c + 1) * 1024],
                                               mu[:, c:c + 1], rs[:, c:c + 1],
                                               ALU.subtract, ALU.mult),
                               waits=w, inc=eV)
                w2 = [(eV, xl_tag[c]), (s_c, t_call)]
                if c >= 2:
                    w2.append(ev_tag[c - 2])
                tt = None
                for j in range(8):
                    tt = op("tensor", lambda e, j=j, pi=pi, lntm=lntm: e.transpose(
                        p6[pi][:, j * 128:(j + 1) * 128], lntm[:, j * 128:(j + 1) * 128],
                        ident_b[:, :]),
                        waits=(w2 if j == 0 else ()), inc=eT)
                tr_tag[c] = tt

            def emit_pev(c):
                pi = c % 2
                dst = fmA[:, :].rearrange("p (k t) -> p k t", k=8)[:, :, c * 128:(c + 1) * 128]
                src = p6[pi][:, :].rearrange("p (k t) -> p k t", k=8)
                if pipe:
                    ev_tag[c] = (eS, op("scalar", lambda e, dst=dst, src=src:
                                        e.copy(dst, src),
                                        waits=[(eT, tr_tag[c])], inc=eS))
                else:
                    ev_tag[c] = (eV, op("vector", lambda e, dst=dst, src=src:
                                        e.tensor_copy(dst, src),
                                        waits=[(eT, tr_tag[c])], inc=eV))

            if pipe and pipe_stagger:
                for c in range(NT + 2):
                    if c < NT:
                        emit_pstats(c)
                    if c >= 2:
                        emit_pxform(c - 2)
                    if c >= 3:
                        emit_pev(c - 3)
                emit_pev(NT - 1)
            elif pipe:
                for c in range(NT):
                    emit_pstats(c)
                for c in range(NT):
                    emit_pxform(c)
                    emit_pev(c)
            else:
                for c in range(NT):
                    emit_pxform(c)
                    emit_pev(c)
            return ev_tag

        ffn_last = {"mm": 0, "evac": 0}

        def emit_ffn(ln_ev, xtags=None, stats_cb=None):
            pg = (p0, p1)
            pu = (p2, p3)
            n = [0]
            tagSilu, tagAct, tagG, tagU = {}, {}, {}, {}
            last_down_mm = [0]
            last_down_evac = [0]
            q_last_mm = {}
            for th in range(2):
                tt0 = th * 1024
                for q in range(4):
                    pq = q % 2
                    wg_off = pq * 12288
                    wu_off = pq * 12288 + 6144
                    swq = (cwg[0], cwg[1])[q % 2]
                    w = []
                    gq = th * 4 + q
                    if gq >= 2:
                        w.append((eT, q_last_mm[gq - 2]))
                    if th == 1 and q < 2:
                        w.append((eT, last_down_mm[0]))
                    src = wgT_i.rearrange("(k p) h -> p k h", p=128)[:, :, q * 768:(q + 1) * 768]
                    semh1 = (cwg[0], cwg[1])[q % 2]
                    semh2 = (cwg[2], cwg[3])[q % 2]
                    semu1 = (cwuh[0], cwuh[1])[q % 2]
                    semu2 = (cwuh[2], cwuh[3])[q % 2]
                    srcu = wuT_i.rearrange("(k p) h -> p k h", p=128)[:, :, q * 768:(q + 1) * 768]
                    tH1 = dma("gpsimd", semh1, BufC[:, wg_off:wg_off + 3072]
                              .rearrange("p (k h) -> p k h", k=4), src[:, 0:4, :],
                              waits=w)
                    tU1 = dma("gpsimd", semu1, BufC[:, wu_off:wu_off + 3072]
                              .rearrange("p (k h) -> p k h", k=4), srcu[:, 0:4, :])
                    tH2 = dma("gpsimd", semh2, BufC[:, wg_off + 3072:wg_off + 6144]
                              .rearrange("p (k h) -> p k h", k=4), src[:, 4:8, :])
                    tU2 = dma("gpsimd", semu2, BufC[:, wu_off + 3072:wu_off + 6144]
                              .rearrange("p (k h) -> p k h", k=4), srcu[:, 4:8, :])
                    if gq == 1 and cast_wd_box[0] is None:
                        cast_wd_box[0] = dma("gpsimd", cwd, wdT_bf[:, :],
                                             wdT_i[:, :], max_dma_last_dim=8192)
                    for hcl in range(6):
                        hc = q * 6 + hcl
                        for tsl in range(2):
                            i = n[0]
                            pi = i % 2
                            tg = None
                            for k in range(8):
                                wts = []
                                if k == 0:
                                    if hcl == 0:
                                        wts = [(semh1, tH1),
                                               ln_ev[th * 8 + tsl * 4 + 3]]
                                    if i >= 2:
                                        wts.append((eS, tagSilu[i - 2]))
                                    if th == 1 and i == 48:
                                        wts.append((eV, last_down_evac[0]))
                                elif k == 4 and hcl == 0 and tsl == 0:
                                    wts = [(semh2, tH2)]
                                tg = op("tensor", lambda e, k=k, pi=pi, wg_off=wg_off,
                                        hcl=hcl, tt0=tt0, tsl=tsl:
                                        e.matmul(pg[pi][:, 0:512],
                                                 BufC[:, wg_off + k * 768 + hcl * 128:
                                                      wg_off + k * 768 + (hcl + 1) * 128],
                                                 fmA[:, k * 2048 + tt0 + tsl * 512:
                                                     k * 2048 + tt0 + (tsl + 1) * 512],
                                                 start=(k == 0), stop=(k == 7)),
                                        waits=wts, inc=(eT if k == 7 else None))
                            tagG[i] = tg
                            tu = None
                            for k in range(8):
                                wts = []
                                if k == 0:
                                    if hcl == 0:
                                        wts = [(semu1, tU1)]
                                    if i >= 2:
                                        wts.append((eV, tagAct[i - 2]))
                                elif k == 4 and hcl == 0 and tsl == 0:
                                    wts = [(semu2, tU2)]
                                tu = op("tensor", lambda e, k=k, pi=pi, wu_off=wu_off,
                                        hcl=hcl, tt0=tt0, tsl=tsl:
                                        e.matmul(pu[pi][:, 0:512],
                                                 BufC[:, wu_off + k * 768 + hcl * 128:
                                                      wu_off + k * 768 + (hcl + 1) * 128],
                                                 fmA[:, k * 2048 + tt0 + tsl * 512:
                                                     k * 2048 + tt0 + (tsl + 1) * 512],
                                                 start=(k == 0), stop=(k == 7)),
                                        waits=wts, inc=(eT if k == 7 else None))
                            tagU[i] = tu
                            wts = [(eT, tagG[i])]
                            if i >= 2:
                                wts.append((eV, tagAct[i - 2]))
                            tagSilu[i] = op("scalar", lambda e, pi=pi: e.activation(
                                tmpF[:, pi * 512:(pi + 1) * 512], pg[pi][:, 0:512],
                                AF.Silu, scale=INV127),
                                waits=wts, inc=eS)
                            tagAct[i] = op("vector", lambda e, pi=pi, hc=hc, tsl=tsl:
                                           e.scalar_tensor_tensor(
                                actB[:, hc * 1024 + tsl * 512: hc * 1024 + (tsl + 1) * 512],
                                pu[pi][:, 0:512], INV127,
                                tmpF[:, pi * 512:(pi + 1) * 512], ALU.mult, ALU.mult),
                                waits=[(eT, tagU[i]), (eS, tagSilu[i])], inc=eV)
                            n[0] += 1
                    q_last_mm[gq] = tagU[n[0] - 1]
                dma("sync", s_wd, BufC[:, 0:12288].rearrange("p (hc d) -> p hc d", hc=12),
                    wdT_bf.rearrange("(hc p) d -> p hc d", p=128)[:, 0:12, :],
                    waits=[(cwd, cast_wd_box[0]), (eT, q_last_mm[th * 4 + 2])])
                tWdh = s_wd.n
                tWd1 = dma("sync", cwu[0], BufC[:, 12288:16384]
                           .rearrange("p (hc d) -> p hc d", hc=4),
                           wdT_bf.rearrange("(hc p) d -> p hc d", p=128)[:, 12:16, :],
                           waits=[(eT, q_last_mm[th * 4 + 3])])
                tWd2 = dma("sync", cwu[1], BufC[:, 16384:20480]
                           .rearrange("p (hc d) -> p hc d", hc=4),
                           wdT_bf.rearrange("(hc p) d -> p hc d", p=128)[:, 16:20, :])
                tWd3 = dma("sync", cwu[2], BufC[:, 20480:24576]
                           .rearrange("p (hc d) -> p hc d", hc=4),
                           wdT_bf.rearrange("(hc p) d -> p hc d", p=128)[:, 20:24, :])
                tXU = {}
                for tl in range(8):
                    c = th * 8 + tl
                    da, db = ((p4, p5), (p0, p1))[tl % 2]
                    td = None
                    for hc in range(NHC):
                        wts = []
                        if hc == 0:
                            if tl == 0:
                                wts = [(s_wd, tWdh), (eV, tagAct[n[0] - 1])]
                            if tl >= 2:
                                wts.append((eV, tXU[tl - 2]))
                        elif tl == 0 and hc == 12:
                            wts = [(cwu[0], tWd1)]
                        elif tl == 0 and hc == 16:
                            wts = [(cwu[1], tWd2)]
                        elif tl == 0 and hc == 20:
                            wts = [(cwu[2], tWd3)]
                        op("tensor", lambda e, hc=hc, tl=tl, da=da:
                           e.matmul(da[:, 0:512],
                                    actB[:, hc * 1024 + tl * 128: hc * 1024 + (tl + 1) * 128],
                                    BufC[:, hc * 1024: hc * 1024 + 512],
                                    start=(hc == 0), stop=(hc == NHC - 1)),
                           waits=wts, inc=None)
                        td = op("tensor", lambda e, hc=hc, tl=tl, db=db:
                                e.matmul(db[:, 0:512],
                                         actB[:, hc * 1024 + tl * 128: hc * 1024 + (tl + 1) * 128],
                                         BufC[:, hc * 1024 + 512: hc * 1024 + 1024],
                                         start=(hc == 0), stop=(hc == NHC - 1)),
                                waits=(), inc=(eT if hc == NHC - 1 else None))
                    op("vector", lambda e, c=c, da=da: e.scalar_tensor_tensor(
                        x_tm[:, c * 1024: c * 1024 + 512], da[:, 0:512], HALF127,
                        x_tm[:, c * 1024: c * 1024 + 512], ALU.mult, ALU.add),
                        waits=[(eT, td)], inc=eV)
                    tXU[tl] = op("vector", lambda e, c=c, db=db: e.scalar_tensor_tensor(
                        x_tm[:, c * 1024 + 512: (c + 1) * 1024], db[:, 0:512], HALF127,
                        x_tm[:, c * 1024 + 512: (c + 1) * 1024], ALU.mult, ALU.add),
                        waits=(), inc=eV)
                    last_down_mm[0] = td
                    last_down_evac[0] = tXU[tl]
                    ffn_last["mm"] = td
                    ffn_last["evac"] = tXU[tl]
                    if xtags is not None:
                        xtags[c] = (eV, tXU[tl])
                    if stats_cb is not None:
                        stats_cb(c)

        ln1_ev = emit_outer_ln(1e-5, pipe=True)
        ffn1_x = {}
        ln2_sqt = {}

        def ln2_stats_cb(c):
            tRc = op("vector", lambda e, c=c: e.reduce_sum(
                musum[:, c:c + 1], x_tm[:, c * 1024:(c + 1) * 1024], axis=AX.X),
                waits=[ffn1_x[c]], inc=eV)
            tMUc = op("vector", lambda e, c=c: e.tensor_scalar(
                mu2[:, c:c + 1], musum[:, c:c + 1], 1.0 / 1024.0, None,
                ALU.mult, ALU.bypass), waits=[(eV, tRc)], inc=eV)
            tSQc = op("scalar", lambda e, c=c: e.activation(
                tmpF[:, 0:1024], x_tm[:, c * 1024:(c + 1) * 1024], AF.Square,
                bias=mu2[:, c:c + 1], scale=-1.0, accum_out=varsum[:, c:c + 1]),
                waits=[(eV, tMUc)], inc=eS)
            wl = [(eS, tSQc)] + ([(eV, lneps_tag[0])] if c == 0 else [])
            tLNc = op("scalar", lambda e, c=c: e.activation(
                varsum[:, c:c + 1], varsum[:, c:c + 1], AF.Ln,
                bias=lneps[:, 0:1], scale=1.0 / 1024.0), waits=wl, inc=eS)
            ln2_sqt[c] = op("scalar", lambda e, c=c: e.activation(
                rstd2[:, c:c + 1], varsum[:, c:c + 1], AF.Exp, scale=-0.5),
                waits=[(eS, tLNc)], inc=eS)
            op("gpsimd", lambda e, c=c: e.tensor_scalar(
                m2r2[:, c:c + 1], mu2[:, c:c + 1], -1.0, rstd2[:, c:c + 1],
                ALU.mult, ALU.mult), waits=[(eS, ln2_sqt[c])], inc=eG)

        emit_ffn(ln1_ev, xtags=ffn1_x,
                 stats_cb=(ln2_stats_cb if stage >= 2 else None))
        cast_wq = dma("gpsimd", qkvA, Wq_bf, Wq_t, max_dma_last_dim=8192)
        cast_wk = dma("gpsimd", qkvA, Wk_bf, Wk_t, max_dma_last_dim=8192)
        cast_wv = dma("gpsimd", qkvA, Wv_bf, Wv_t, max_dma_last_dim=8192)
        t_qkvcast = qkvA.n
        dma("gpsimd", s_gam, gamF_s[:, :], gamF_in)
        dma("gpsimd", s_gam, betF_s[:, :], betF_in)
        dma("gpsimd", s_gam, gam_bf[:, :], gam_in, max_dma_last_dim=8192)
        dma("gpsimd", s_gam, bet_bf[:, :], bet_in, max_dma_last_dim=8192)
        dma("gpsimd", s_gam, omT_bf[:, :], omTd_in, max_dma_last_dim=8192)
        t_gamall = s_gam.n
        if stage < 2:
            barrier()

        if stage >= 2:
            ffn1_mm_tag = ffn_last["mm"]
            ln2_ev, xl_tag2, tr_tag2 = {}, {}, {}
            for c in range(NT):
                pi = c % 2
                lntm = actB[:, A_LNTM + pi * 1024: A_LNTM + (pi + 1) * 1024]
                w = [(eS, ln2_sqt[c])]
                if c <= 1:
                    w.append((eT, ffn1_mm_tag))
                if c >= 2:
                    w.append((eT, tr_tag2[c - 2]))
                xl_tag2[c] = op("vector", lambda e, c=c, lntm=lntm:
                                e.tensor_scalar(lntm, x_tm[:, c * 1024:(c + 1) * 1024],
                                                mu2[:, c:c + 1], rstd2[:, c:c + 1],
                                                ALU.subtract, ALU.mult),
                                waits=w, inc=eV)
                w2 = [(eV, xl_tag2[c])]
                if c >= 2:
                    w2.append(ln2_ev[c - 2])
                tt = None
                for j in range(8):
                    tt = op("tensor", lambda e, j=j, pi=pi, lntm=lntm: e.transpose(
                        p6[pi][:, j * 128:(j + 1) * 128], lntm[:, j * 128:(j + 1) * 128],
                        ident_b[:, :]), waits=(w2 if j == 0 else ()), inc=eT)
                tr_tag2[c] = tt
                dst = fmA[:, :].rearrange("p (k t) -> p k t", k=8)[:, :, c * 128:(c + 1) * 128]
                src = p6[pi][:, :].rearrange("p (k t) -> p k t", k=8)
                ln2_ev[c] = (eV, op("vector", lambda e, dst=dst, src=src:
                                    e.tensor_copy(dst, src), waits=[(eT, tt)], inc=eV))
            t_ln2 = ln2_ev[NT - 1][1]

            def ln2_ev_w(c):
                return ln2_ev[c]
            cast_w1 = dma("gpsimd", cw1, W1_bf[:, :, :, :], W1_sw,
                          waits=[(eV, t_ln2)], max_dma_last_dim=8192)
            cast_w2 = dma("gpsimd", cw2, W2_bf[:, :, :, :], W2_sw,
                          waits=[(eV, t_ln2)], max_dma_last_dim=8192)

            QGW = 12352

            def a_q(pg):
                return pg * QGW

            def a_k(pg):
                return pg * QGW + 4096

            def a_v(pg):
                return pg * QGW + 8192

            C_SLAB = 16384
            T_TM = 20480
            T_FM = 22528

            t_ones = None
            for pg in range(2):
                ap1 = bass.AP(actB, a_v(pg) + 64, [[ACTB_W, 128], [65, 64]])
                t_ones = op("vector", lambda e, ap1=ap1: e.memset(ap1, 1.0), inc=eV)

            slab_lastmm = {}
            slab_cnt = [0]
            qkv_rot_hist = []
            qkv_rot_cnt = [0]
            qk_done = {}
            v_done = {}

            def qkv_gen(qg):
                g, qq, pg = qg // 2, qg % 2, qg % 2
                for wi, (Wsrc, isV) in enumerate(((Wq_bf, 0), (Wk_bf, 0), (Wv_bf, 1))):
                    pw = slab_cnt[0] % 2
                    slab_cnt[0] += 1
                    qsem = qkvB
                    wts = [(qsem, qsem.n), (qkvA, t_qkvcast)]
                    if pw in slab_lastmm:
                        wts.append((eT, slab_lastmm[pw]))
                    else:
                        wts.append((eT, ffn1_mm_tag))
                    tW = dma("sync", qsem, BufC[:, C_SLAB + pw * 2048:
                                                C_SLAB + (pw + 1) * 2048]
                             .rearrange("p (k j) -> p k j", k=8),
                             Wsrc[g, :, :, qq * 256:(qq + 1) * 256],
                             waits=wts)
                    lm = None
                    if not isV:
                        dst0 = a_q(pg) if wi == 0 else a_k(pg)
                        for mloc in range(2):
                            for ts in range(4):
                                pr = p4
                                qkv_rot_cnt[0] += 1
                                for k in range(8):
                                    wts = []
                                    if k == 0:
                                        wts = [ln2_ev_w(ts * 4 + 3)]
                                        if mloc == 0 and ts == 0:
                                            wts.append((qsem, tW))
                                        if len(qkv_rot_hist) >= 1:
                                            wts.append(qkv_rot_hist[-1])
                                    lm = op("tensor", lambda e, k=k, mloc=mloc, ts=ts,
                                            pr=pr, pw=pw:
                                            e.matmul(pr[:, 0:512],
                                                     BufC[:, C_SLAB + pw * 2048 + k * 256
                                                          + mloc * 128:
                                                          C_SLAB + pw * 2048 + k * 256
                                                          + (mloc + 1) * 128],
                                                     fmA[:, k * 2048 + ts * 512:
                                                         k * 2048 + (ts + 1) * 512],
                                                     start=(k == 0), stop=(k == 7)),
                                            waits=wts, inc=(eT if k == 7 else None))
                                yield
                                tE = op("scalar", lambda e, pr=pr, mloc=mloc, ts=ts,
                                        dst0=dst0: e.activation(
                                    actB[:, dst0 + mloc * 2048 + ts * 512:
                                         dst0 + mloc * 2048 + (ts + 1) * 512],
                                    pr[:, 0:512], AF.Identity, scale=SCALE_QK),
                                    waits=[(eT, lm)], inc=eS)
                                qkv_rot_hist.append((eS, tE))
                                qk_done[qg] = tE
                                yield
                        slab_lastmm[pw] = lm
                    else:
                        for c16 in range(16):
                            pr = p4
                            po = (c16 % 2) * 256
                            qkv_rot_cnt[0] += 1
                            for k in range(8):
                                wts = []
                                if k == 0:
                                    wts = [ln2_ev_w(c16)]
                                    if c16 == 0:
                                        wts.append((qsem, tW))
                                    if c16 % 2 == 0 and len(qkv_rot_hist) >= 1:
                                        wts.append(qkv_rot_hist[-1])
                                lm = op("tensor", lambda e, k=k, c16=c16, pr=pr,
                                        pw=pw, po=po:
                                        e.matmul(pr[:, po:po + 256],
                                                 fmA[:, k * 2048 + c16 * 128:
                                                     k * 2048 + (c16 + 1) * 128],
                                                 BufC[:, C_SLAB + pw * 2048 + k * 256:
                                                      C_SLAB + pw * 2048 + (k + 1) * 256],
                                                 start=(k == 0), stop=(k == 7)),
                                        waits=wts, inc=(eT if k == 7 else None))
                            yield
                            if c16 % 2 == 1:
                                dst = bass.AP(actB, a_v(pg) + (c16 - 1) * 260,
                                              [[ACTB_W, 128], [65, 8], [1, 64]])
                                srcv = pr[:, 0:512].rearrange(
                                    "p (hh w) -> p hh w", w=64)
                                wv = [(eT, lm), (eV, t_ones)]
                                tE = op("vector", lambda e, dst=dst, srcv=srcv:
                                        e.tensor_copy(dst, srcv), waits=wv, inc=eV)
                                qkv_rot_hist.append((eV, tE))
                                v_done[qg] = tE
                            yield
                        slab_lastmm[pw] = lm

            NE = 128
            its = [(qg, c, pair) for qg in range(4) for c in range(16)
                   for pair in range(2)]
            tPK, tRM, tEXP, tEPS, tTR, tD, tFME = {}, {}, {}, {}, {}, {}, {}
            tSA, tSBE, tA, tAM, tO, tSCL = {}, {}, {}, {}, {}, {}
            last_projk = {}

            def st_proj(t):
                qg, c, pair = its[t]
                pg = qg % 2
                P = (p0, p1, p2)[t % 3]
                m = pair
                wts = []
                if c == 0:
                    wts = [(eS, qk_done[qg]), (eV, v_done[qg]), (s_gam, t_gamall)]
                if (t - 3) in tEXP:
                    wts.append((eS, tEXP[t - 3]))
                qsl = actB[:, a_q(pg) + m * 2048 + c * 128:
                           a_q(pg) + m * 2048 + (c + 1) * 128]
                op("tensor", lambda e, qsl=qsl, P=P:
                   e.matmul(P[:, 0:256], qsl, omT_bf[:, :], start=True, stop=True),
                   waits=wts, inc=None)
                ksl = actB[:, a_k(pg) + m * 2048 + c * 128:
                           a_k(pg) + m * 2048 + (c + 1) * 128]
                tPK[t] = op("tensor", lambda e, ksl=ksl, P=P:
                            e.matmul(P[:, 256:512], ksl, omT_bf[:, :],
                                     start=True, stop=True), waits=(), inc=eT)
                last_projk[qg] = tPK[t]

            def st_rm(t):
                P = (p0, p1, p2)[t % 3]
                wts = [(eT, tPK[t])]
                if (t - 2) in tEXP:
                    wts.append((eS, tEXP[t - 2]))
                tRM[t] = op("vector", lambda e, P=P, t=t: e.tensor_reduce(
                    st_nm[:, (t % 2) * 4: (t % 2) * 4 + 4],
                    P[:, 0:512].rearrange("p (a b) -> p a b", a=4),
                    axis=AX.X, op=ALU.max, negate=True), waits=wts, inc=eV)

            def st_exp(t):
                P = (p0, p1, p2)[t % 3]
                tmo = T_TM + (t % 4) * 512
                tE = None
                for j in range(4):
                    wts = []
                    if j == 0:
                        wts = [(eV, tRM[t])]
                        if (t - 4) in tD:
                            wts.append((eT, tD[t - 4]))
                    tE = op("scalar", lambda e, P=P, tmo=tmo, j=j, t=t:
                            e.activation(
                        BufC[:, tmo + j * 128: tmo + (j + 1) * 128],
                        P[:, j * 128:(j + 1) * 128], AF.Exp,
                        bias=st_nm[:, (t % 2) * 4 + j: (t % 2) * 4 + j + 1]),
                        waits=wts, inc=(eS if j == 3 else None))
                tEXP[t] = tE

            def st_eps(t):
                tmo = T_TM + (t % 4) * 512
                tEPS[t] = op("vector", lambda e, tmo=tmo: e.tensor_scalar(
                    BufC[:, tmo: tmo + 512], BufC[:, tmo: tmo + 512],
                    INV_SQM, EPS_F, ALU.mult, ALU.add),
                    waits=[(eS, tEXP[t])], inc=eV)

            def st_tr_delta(t):
                qg, c, pair = its[t]
                pg = qg % 2
                tmo = T_TM + (t % 4) * 512
                tro = (t % 2) * 512
                wts = [(eV, tEPS[t]), (s_c, t_call)]
                if (t - 1) in tFME:
                    wts.append((eS, tFME[t - 1]))
                tt = None
                for j in range(4):
                    tt = op("tensor", lambda e, tmo=tmo, tro=tro, j=j:
                             e.transpose(p6a[:, tro + j * 128: tro + (j + 1) * 128],
                                         BufC[:, tmo + j * 128: tmo + (j + 1) * 128],
                                         ident_b[:, :]),
                             waits=(wts if j == 0 else ()),
                             inc=(eT if j == 3 else None))
                tTR[t] = tt
                wts = []
                if (t - 1) in tSA:
                    wts.append((eV, tSA[t - 1]))
                td = None
                for j in range(2):
                    vsl = actB[:, a_v(pg) + c * 260 + (pair * 2 + j) * 65:
                               a_v(pg) + c * 260 + (pair * 2 + j) * 65 + 65]
                    td = op("tensor", lambda e, tmo=tmo, j=j, t=t, vsl=vsl:
                            e.matmul(p3[:, (t % 2) * 256 + j * 128:
                                         (t % 2) * 256 + j * 128 + 65],
                                     BufC[:, tmo + 256 + j * 128:
                                          tmo + 256 + (j + 1) * 128],
                                     vsl, start=True, stop=True),
                            waits=(wts if j == 0 else ()),
                            inc=(eT if j == 1 else None))
                tD[t] = td

            def st_fme_sacc(t):
                qg, c, pair = its[t]
                tro = (t % 2) * 512
                fmo = T_FM + (t % 4) * 512
                wts = [(eT, tTR[t])]
                if (t - 4) in tO:
                    wts.append((eT, tO[t - 4]))
                tFME[t] = op("scalar", lambda e, tro=tro, fmo=fmo: e.copy(
                    BufC[:, fmo: fmo + 512], p6a[:, tro: tro + 512]),
                    waits=wts, inc=eS)
                sfo = ((qg % 2) * 2 + pair) * 132
                sfv = bass.AP(Sf_t, sfo, [[528, 128], [66, 2], [1, 65]])
                d3v = bass.AP(p3, (t % 2) * 256, [[512, 128], [128, 2], [1, 65]])
                wts = [(eT, tD[t])]
                if (t - 2) in tSA:
                    wts.append((eV, tSA[t - 2]))
                if (t - 2) in tSBE:
                    wts.append((eS, tSBE[t - 2]))
                if c == 0:
                    tSA[t] = op("vector", lambda e, sfv=sfv, d3v=d3v:
                                e.tensor_copy(sfv, d3v), waits=wts, inc=eV)
                else:
                    tSA[t] = op("vector", lambda e, sfv=sfv, d3v=d3v:
                                e.scalar_tensor_tensor(sfv, d3v, 1.0, sfv,
                                                       ALU.mult, ALU.add),
                                waits=wts, inc=eV)

            def st_abar_sbe(t):
                qg, c, pair = its[t]
                fmo = T_FM + (t % 4) * 512
                ao = (t % 2) * 256
                wts = [(eS, tFME[t])]
                if (t - 1) in tAM:
                    wts.append((eV, tAM[t - 1]))
                ta = None
                for j in range(2):
                    ta = op("tensor", lambda e, fmo=fmo, ao=ao, j=j: e.matmul(
                        p5[:, ao + j * 128: ao + (j + 1) * 128],
                        BufC[:, fmo + 256 + j * 128: fmo + 256 + (j + 1) * 128],
                        BufC[:, fmo + j * 128: fmo + (j + 1) * 128],
                        start=True, stop=True),
                        waits=(wts if j == 0 else ()), inc=(eT if j == 1 else None))
                tA[t] = ta
                sfo = ((qg % 2) * 2 + pair) * 132
                sbo = ((c % 3) * 2 + pair) * 132
                sbv = bass.AP(sbT, sbo, [[1056, 128], [66, 2], [1, 65]])
                sfr = bass.AP(Sf_t, sfo, [[528, 128], [66, 2], [1, 65]])
                wts = [(eV, tSA[t])]
                if (t - 4) in tO:
                    wts.append((eT, tO[t - 4]))
                tSBE[t] = op("scalar", lambda e, sbv=sbv, sfr=sfr:
                             e.copy(sbv, sfr), waits=wts, inc=eS)

            def st_mask(t):
                ao = (t % 2) * 256
                mb = bass.AP(maskji, 0, [[128, 128], [0, 2], [1, 128]])
                wts = [(eT, tA[t])]
                if (t - 2) in tO:
                    wts.append((eT, tO[t - 2]))
                amo = (t % 2) * 256
                tAM[t] = op("vector", lambda e, ao=ao, amo=amo, mb=mb: e.tensor_tensor(
                    amT[:, amo: amo + 256].rearrange("p (a b) -> p a b", a=2),
                    p5[:, ao: ao + 256].rearrange("p (a b) -> p a b", a=2),
                    mb, op=ALU.mult), waits=wts, inc=eV)

            def st_o(t):
                qg, c, pair = its[t]
                pg = qg % 2
                fmo = T_FM + (t % 4) * 512
                oo = (t % 2) * 256
                amo = (t % 2) * 256
                wts = [(eV, tAM[t])]
                if (t - 1) in tSCL:
                    wts.append((eV, tSCL[t - 1]))
                if c > 0:
                    wts.append((eS, tSBE[t - 2]))
                to = None
                for j in range(2):
                    vsl = actB[:, a_v(pg) + c * 260 + (pair * 2 + j) * 65:
                               a_v(pg) + c * 260 + (pair * 2 + j) * 65 + 65]
                    to = op("tensor", lambda e, oo=oo, amo=amo, j=j, vsl=vsl, c=c:
                            e.matmul(pOb[:, oo + j * 128: oo + j * 128 + 65],
                                     amT[:, amo + j * 128: amo + (j + 1) * 128],
                                     vsl, start=True, stop=(c == 0)),
                            waits=(wts if j == 0 else ()),
                            inc=(eT if (c == 0 and j == 1) else None))
                    if c > 0:
                        sbo = (((c - 1) % 3) * 2 + pair) * 132
                        sbs = sbT[:, sbo + j * 66: sbo + j * 66 + 65]
                        to = op("tensor", lambda e, oo=oo, fmo=fmo, j=j, sbs=sbs:
                                e.matmul(pOb[:, oo + j * 128: oo + j * 128 + 65],
                                         BufC[:, fmo + j * 128: fmo + (j + 1) * 128],
                                         sbs, start=False, stop=True),
                                waits=(), inc=(eT if j == 1 else None))
                tO[t] = to

            def st_tail(t):
                qg, c, pair = its[t]
                oo = (t % 2) * 256
                dnv = bass.AP(pOb, oo + 64, [[512, 128], [128, 2], [1, 1]])
                tdn = op("vector", lambda e, dnv=dnv, t=t: e.tensor_scalar(
                    den_t[:, (t % 2) * 2: (t % 2) * 2 + 2].rearrange(
                        "p (a b) -> p a b", b=1), dnv, EPS_F, None,
                    ALU.add, ALU.bypass), waits=[(eT, tO[t])], inc=eV)
                trc = op("vector", lambda e, t=t: e.reciprocal(
                    rec_t[:, (t % 2) * 2: (t % 2) * 2 + 2],
                    den_t[:, (t % 2) * 2: (t % 2) * 2 + 2]),
                    waits=[(eV, tdn)], inc=eV)
                outv = BufC[:, c * 1024 + (qg * 4 + pair * 2) * 64:
                            c * 1024 + (qg * 4 + pair * 2) * 64 + 128] \
                    .rearrange("p (a b) -> p a b", a=2)
                o6v = bass.AP(pOb, oo, [[512, 128], [128, 2], [1, 64]])
                rcb = bass.AP(rec_t, (t % 2) * 2, [[8, 128], [1, 2], [0, 64]])
                tSCL[t] = op("vector", lambda e, outv=outv, o6v=o6v, rcb=rcb:
                             e.tensor_tensor(outv, o6v, rcb, op=ALU.mult),
                             waits=[(eV, trc)], inc=eV)

            STAGES_A = [(6, st_abar_sbe),
                        (5, st_fme_sacc), (4, st_tr_delta), (3, st_eps),
                        (2, st_exp)]
            STAGES_B = [(1, st_rm), (0, st_proj)]

            gens = [qkv_gen(qg) for qg in range(4)]
            for _ in gens[0]:
                pass
            done_g = [True, False, False, False]

            def adv_gen(gq):
                if gq <= 3 and not done_g[gq]:
                    try:
                        next(gens[gq])
                    except StopIteration:
                        done_g[gq] = True

            for sl in range(NE + 9):
                if 0 <= sl - 8 < NE:
                    st_o(sl - 8)
                if 0 <= sl - 7 < NE:
                    st_mask(sl - 7)
                if 0 <= sl - 8 < NE:
                    st_tail(sl - 8)
                gq = sl // 32 + 1
                adv_gen(gq)
                for ofs, fn in STAGES_A:
                    t = sl - ofs
                    if 0 <= t < NE:
                        fn(t)
                adv_gen(gq)
                for ofs, fn in STAGES_B:
                    t = sl - ofs
                    if 0 <= t < NE:
                        fn(t)
                if sl % 32 == 31 and gq <= 3 and not done_g[gq]:
                    for _ in gens[gq]:
                        pass
                    done_g[gq] = True
            barrier()

            tWo = dma("gpsimd", s_wo, BufC[:, C_WO:C_WO + 4096]
                      .rearrange("p (k j) -> p k j", k=4), Wo_sw[:, 0:4, :],
                      max_dma_last_dim=8192)
            tWo2 = dma("gpsimd", cwu[3], BufC[:, C_WO + 4096:C_WO + 8192]
                       .rearrange("p (k j) -> p k j", k=4), Wo_sw[:, 4:8, :],
                       max_dma_last_dim=8192)
            ppair = ((p0, p1), (p2, p3))
            tY, tAF, tWoMM = {}, {}, {}
            ypend = {}

            def emit_ypair(c):
                ba, bb, tmpo, tTmpc = ypend[c]
                op("vector", lambda e, c=c, ba=ba, tmpo=tmpo: e.tensor_tensor(
                    actB[:, A_Y + c * 1024: A_Y + c * 1024 + 512],
                    actB[:, tmpo: tmpo + 512], ba[:, 0:512], op=ALU.add),
                    waits=[(eS, tTmpc), (eT, tWoMM[c])], inc=eV)
                tY[c] = op("vector", lambda e, c=c, bb=bb, tmpo=tmpo: e.tensor_tensor(
                    actB[:, A_Y + c * 1024 + 512: A_Y + (c + 1) * 1024],
                    actB[:, tmpo + 512: tmpo + 1024], bb[:, 0:512], op=ALU.add),
                    waits=(), inc=eV)

            for c in range(NT):
                pi = c % 2
                fi = c % 3
                wts = []
                if c >= 2:
                    wts.append((eV, tAF[c - 2]))
                tt = None
                for j in range(8):
                    tt = op("tensor", lambda e, j=j, pi=pi, c=c: e.transpose(
                        p6[pi][:, j * 128:(j + 1) * 128],
                        BufC[:, c * 1024 + j * 128: c * 1024 + (j + 1) * 128],
                        ident_b[:, :]), waits=(wts if j == 0 else ()), inc=eT)
                wv = [(eT, tt)]
                if c >= 3:
                    wv.append((eT, tWoMM[c - 3]))
                tAF[c] = op("vector", lambda e, pi=pi, fi=fi: e.tensor_copy(
                    actB[:, A_LNTM + fi * 1024: A_LNTM + (fi + 1) * 1024],
                    p6[pi][:, 0:1024]), waits=wv, inc=eV)
                ba, bb = ppair[pi]
                for k in range(8):
                    wts = []
                    if k == 0:
                        wts = [(eV, tAF[c])]
                        if c <= 1:
                            wts.append((s_wo, tWo))
                        if c >= 2:
                            wts.append((eV, tY[c - 2]))
                    elif k == 4 and c == 0:
                        wts = [(cwu[3], tWo2)]
                    op("tensor", lambda e, k=k, fi=fi, ba=ba: e.matmul(
                        ba[:, 0:512],
                        actB[:, A_LNTM + fi * 1024 + k * 128: A_LNTM + fi * 1024 + (k + 1) * 128],
                        BufC[:, C_WO + k * 1024: C_WO + k * 1024 + 512],
                        start=(k == 0), stop=(k == 7)), waits=wts, inc=None)
                    tWoMM[c] = op("tensor", lambda e, k=k, fi=fi, bb=bb: e.matmul(
                        bb[:, 0:512],
                        actB[:, A_LNTM + fi * 1024 + k * 128: A_LNTM + fi * 1024 + (k + 1) * 128],
                        BufC[:, C_WO + k * 1024 + 512: C_WO + (k + 1) * 1024],
                        start=(k == 0), stop=(k == 7)), waits=(),
                        inc=(eT if k == 7 else None))
                tmpo = 20480 + pi * 1024
                wts = []
                if c >= 2:
                    wts.append((eV, tY[c - 2]))
                tTmp = op("scalar", lambda e, c=c, tmpo=tmpo: e.activation(
                    actB[:, tmpo: tmpo + 1024], x_tm[:, c * 1024:(c + 1) * 1024],
                    AF.Identity, bias=m2r2[:, c:c + 1], scale=rstd2[:, c:c + 1]),
                    waits=wts, inc=eS)
                ypend[c] = (ba, bb, tmpo, tTmp)
                if c >= 1:
                    emit_ypair(c - 1)
            emit_ypair(NT - 1)
            barrier()

            BufCf = BufC.bitcast(F32)
            t_le2 = op("vector", lambda e: e.memset(lneps[:, :], 1e6), inc=eV)
            dump2i = actB[:, A_LNTM + 2048:A_LNTM + 3072]
            ev_tag, tr_tag, hb_tag = {}, {}, {}

            def emit_iev(c):
                pi = c % 2
                te = None
                for k in range(8):
                    dst = fmA[:, k * 2048 + c * 128: k * 2048 + (c + 1) * 128]
                    src = p6[pi][:, k * 128:(k + 1) * 128]
                    te = op("vector", lambda e, dst=dst, src=src, k=k:
                            e.tensor_scalar(dst, src, gamF_s[:, k:k + 1],
                                            betF_s[:, k:k + 1], ALU.mult, ALU.add),
                            waits=([(eT, tr_tag[c])] if k == 0 else ()),
                            inc=(eV if k == 7 else None))
                ev_tag[c] = (eV, te)

            tRSi = {}

            def emit_istats(c):
                tRi = op("scalar", lambda e, c=c: e.activation(
                    dump2i, actB[:, A_Y + c * 1024: A_Y + (c + 1) * 1024],
                    AF.Identity, accum_out=musum[:, c:c + 1]),
                    waits=[(eV, tY[c])], inc=eS)
                tMUi = op("gpsimd", lambda e, c=c: e.tensor_scalar(
                    mu_c[:, c:c + 1], musum[:, c:c + 1], 1.0 / 1024.0, None,
                    ALU.mult, ALU.bypass), waits=[(eS, tRi)], inc=eG)
                tSQi = op("scalar", lambda e, c=c: e.activation(
                    dump2i, actB[:, A_Y + c * 1024: A_Y + (c + 1) * 1024],
                    AF.Square, bias=mu_c[:, c:c + 1], scale=-1.0,
                    accum_out=varsum[:, c:c + 1]), waits=[(eG, tMUi)], inc=eS)
                wl = [(eS, tSQi)] + ([(eV, t_le2)] if c == 0 else [])
                tLNi = op("scalar", lambda e, c=c: e.activation(
                    varsum[:, c:c + 1], varsum[:, c:c + 1], AF.Ln,
                    bias=lneps[:, 0:1], scale=1.0 / 1024.0), waits=wl, inc=eS)
                tRSi[c] = op("scalar", lambda e, c=c: e.activation(
                    rstd_c[:, c:c + 1], varsum[:, c:c + 1], AF.Exp, scale=-0.5),
                    waits=[(eS, tLNi)], inc=eS)

            def emit_ixform(c):
                pi = c % 2
                lntm = actB[:, A_LNTM + pi * 1024: A_LNTM + (pi + 1) * 1024]
                wts = [(eS, tRSi[c])]
                if c == 0:
                    wts.append((s_gam, t_gamall))
                if c >= 2:
                    wts.append((eT, tr_tag[c - 2]))
                tT3 = op("vector", lambda e, c=c, lntm=lntm: e.tensor_scalar(
                    lntm, actB[:, A_Y + c * 1024: A_Y + (c + 1) * 1024],
                    mu_c[:, c:c + 1], rstd_c[:, c:c + 1],
                    ALU.subtract, ALU.mult), waits=wts, inc=eV)
                w2 = [(eV, tT3)]
                if c >= 2:
                    w2.append(ev_tag[c - 2])
                tt = None
                for j in range(8):
                    tt = op("tensor", lambda e, j=j, pi=pi, lntm=lntm: e.transpose(
                        p6[pi][:, j * 128:(j + 1) * 128], lntm[:, j * 128:(j + 1) * 128],
                        ident_b[:, :]), waits=(w2 if j == 0 else ()), inc=eT)
                tr_tag[c] = tt

            for c in range(NT + 2):
                if c < NT:
                    emit_istats(c)
                if c >= 2:
                    emit_ixform(c - 2)
                if c >= 3:
                    emit_iev(c - 3)
            emit_iev(NT - 1)
            barrier()

            relu_hist = []
            w1mm_hist = []
            w2mm_hist = []
            o2_hist = []
            tr2_hist = []
            xu2_hist = []
            w2_last_of_quarter = 0
            for tq in range(4):
                ts0 = tq * 512
                tW1g = {}

                def w1_load(gf, tq=tq):
                    pi4 = gf % 2
                    w1sem = (swA, swB)[pi4]
                    wts = [(cw1, cast_w1)]
                    if w1mm_hist:
                        wts.append((eT, w1mm_hist[-1]))
                    tW1g[gf] = dma("sync", w1sem,
                                   BufC[:, pi4 * 4096:(pi4 + 1) * 4096]
                                   .rearrange("p (f k j) -> p f k j", f=4, k=8),
                                   W1_bf[:, gf * 4:(gf + 1) * 4, :, :], waits=wts)
                w1_load(0)
                w1_load(1)
                for f in range(NF):
                    pi = f % 2
                    gf = f // 4
                    if f % 4 == 0 and f > 0 and gf + 1 <= 7:
                        w1_load(gf + 1)
                    w1sem = (swA, swB)[gf % 2]
                    w1of = (gf % 2) * 4096 + (f % 4) * 1024
                    lm = None
                    for k in range(8):
                        wts = []
                        if k == 0:
                            if f % 4 == 0:
                                wts.append((w1sem, tW1g[gf]))
                            if len(relu_hist) >= 2:
                                wts.append((eV, relu_hist[-2]))
                        lm = op("tensor", lambda e, k=k, pi=pi, ts0=ts0, w1of=w1of:
                                e.matmul(
                            pP[pi][:, 0:512],
                            BufC[:, w1of + k * 128: w1of + (k + 1) * 128],
                            fmA[:, k * 2048 + ts0: k * 2048 + ts0 + 512],
                            start=(k == 0), stop=(k == 7)),
                            waits=wts, inc=(eT if k == 7 else None))
                    w1mm_hist.append(lm)
                    wts = [(eT, lm)]
                    if tq > 0 and f == 0:
                        wts.append((eT, w2_last_of_quarter))
                    tRl = op("vector", lambda e, pi=pi, f=f: e.tensor_scalar(
                        actB[:, A_RELU + f * 512: A_RELU + (f + 1) * 512],
                        pP[pi][:, 0:512], b1t[:, f:f + 1], 0.0, ALU.add, ALU.max),
                        waits=wts, inc=eV)
                    relu_hist.append(tRl)
                for d in range(8):
                    pi = d % 2
                    w2sem = (sW2a, sW2b)[pi]
                    wts = [(cw2, cast_w2)]
                    if len(w2mm_hist) >= 2:
                        wts.append((eT, w2mm_hist[-2]))
                    tW2 = dma("sync", w2sem, BufC[:, 8192 + pi * 4096: 8192 + (pi + 1) * 4096]
                              .rearrange("p (k j) -> p k j", k=32),
                              W2_bf[:, d, :, :], waits=wts)
                    lm = None
                    for f in range(NF):
                        wts = []
                        if f == 0:
                            wts = [(w2sem, tW2), (eV, relu_hist[-1])]
                            if len(o2_hist) >= 2:
                                wts.append((eS, o2_hist[-2]))
                        lm = op("tensor", lambda e, f=f, pi=pi:
                                e.matmul(
                            pQ[pi][:, 0:512],
                            BufC[:, 8192 + pi * 4096 + f * 128: 8192 + pi * 4096 + (f + 1) * 128],
                            actB[:, A_RELU + f * 512: A_RELU + (f + 1) * 512],
                            start=(f == 0), stop=(f == NF - 1)),
                            waits=wts, inc=(eT if f == NF - 1 else None))
                    w2mm_hist.append(lm)
                    w2_last_of_quarter = lm
                    o2reg = tmpF[:, 0:512] if pi == 0 else tmpF[:, 512:1024]
                    wts = [(eT, lm)]
                    if len(tr2_hist) >= 2:
                        wts.append((eT, tr2_hist[-2]))
                    tO2 = op("scalar", lambda e, pi=pi, d=d, o2reg=o2reg:
                             e.activation(o2reg, pQ[pi][:, 0:512], AF.Identity,
                                          bias=b2t[:, d:d + 1]),
                             waits=wts, inc=eS)
                    o2_hist.append(tO2)
                    ptr = (p4, p5)[pi]
                    wts = [(eS, tO2)]
                    if len(xu2_hist) >= 2:
                        wts.append((eV, xu2_hist[-2]))
                    tt = None
                    for j in range(4):
                        tt = op("tensor", lambda e, j=j, ptr=ptr, o2reg=o2reg: e.transpose(
                            ptr[:, j * 128:(j + 1) * 128],
                            o2reg[:, j * 128:(j + 1) * 128], ident_f[:, :]),
                            waits=(wts if j == 0 else ()), inc=(eT if j == 3 else None))
                    tr2_hist.append(tt)
                    tx = None
                    for j in range(4):
                        ct = tq * 4 + j
                        tx = op("vector", lambda e, j=j, ptr=ptr, ct=ct, d=d:
                                e.scalar_tensor_tensor(
                            x_tm[:, ct * 1024 + d * 128: ct * 1024 + (d + 1) * 128],
                            ptr[:, j * 128:(j + 1) * 128], 1.0,
                            x_tm[:, ct * 1024 + d * 128: ct * 1024 + (d + 1) * 128],
                            ALU.mult, ALU.add),
                            waits=([(eT, tt)] if j == 0 else ()), inc=eV)
                    xu2_hist.append(tx)
            barrier()

        if stage >= 3:
            ln3_ev = emit_outer_ln(1e-5, pipe=True, pipe_chunks=False,
                                   pipe_stagger=True)
            ffn2_x = {}
            emit_ffn(ln3_ev, xtags=ffn2_x)
            for c in range(NT):
                dma("sync" if c % 2 == 0 else "scalar", s_x,
                    out[c * 128:(c + 1) * 128, :],
                    x_tm[:, c * 1024:(c + 1) * 1024], waits=[ffn2_x[c]])
            barrier()
        else:
            for c in range(NT):
                dma("sync", s_x, out[c * 128:(c + 1) * 128, :],
                    x_tm[:, c * 1024:(c + 1) * 1024])
        op("sync", lambda e: None, waits=[(s_x, s_x.n)])

        with nc.Block() as block:
            @block.sync
            def _(e):
                for f in ops["sync"]:
                    f(e)

            @block.gpsimd
            def _(e):
                for f in ops["gpsimd"]:
                    f(e)

            @block.tensor
            def _(e):
                for f in ops["tensor"]:
                    f(e)

            @block.vector
            def _(e):
                for f in ops["vector"]:
                    f(e)

            @block.scalar
            def _(e):
                for f in ops["scalar"]:
                    f(e)

    return nc


def _omd(omega):
    d = np.zeros((128, 256), np.float32)
    d[0:64, 0:128] = omega.T
    d[64:128, 128:256] = omega.T
    return d


def _prep_inputs(inputs):
    f32 = np.float32
    Wq = np.asarray(inputs["L0_attn_Wq"], f32)
    Wk = np.asarray(inputs["L0_attn_Wk"], f32)
    Wv = np.asarray(inputs["L0_attn_Wv"], f32)
    Wo = np.asarray(inputs["L0_attn_Wo"], f32)
    W1 = np.asarray(inputs["L0_attn_W1"], f32)
    W2 = np.asarray(inputs["L0_attn_W2"], f32)
    b1 = np.asarray(inputs["L0_attn_b1"], f32)
    b2 = np.asarray(inputs["L0_attn_b2"], f32)
    gamma = np.asarray(inputs["L0_attn_gamma"], f32)
    beta = np.asarray(inputs["L0_attn_beta"], f32)
    omega = np.asarray(inputs["L0_attn_omega"], f32)
    wg = np.asarray(inputs["L0_ffn_w_gate"], np.int32)
    wu = np.asarray(inputs["L0_ffn_w_up"], np.int32)
    wd = np.asarray(inputs["L0_ffn_w_down"], np.int32)

    c = np.ascontiguousarray
    shared = {
        "wgT_i": c(wg.T),
        "wuT_i": c(wu.T),
        "wdT_i": c(wd.T),
        "Wq_t": c(Wq.reshape(8, 128, 2, 512).transpose(2, 1, 0, 3)),
        "Wk_t": c(Wk.reshape(8, 128, 2, 512).transpose(2, 1, 0, 3)),
        "Wv_t": c(Wv.reshape(8, 128, 2, 512).transpose(2, 1, 0, 3)),
        "Wo_sw": c(Wo.reshape(8, 128, 1024).transpose(1, 0, 2)),
        "W1_sw": c(W1.reshape(8, 128, 32, 128).transpose(1, 2, 0, 3)),
        "W2_sw": c(W2.reshape(32, 128, 8, 128).transpose(1, 2, 0, 3)),
        "omD": _omd(omega),
        "b1t": c(b1.reshape(32, 128).T),
        "b2t": c(b2.reshape(8, 128).T),
        "gam_bc": c(np.broadcast_to(gamma[None, :], (128, 1024))),
        "gamF": c(gamma.reshape(8, 128).T),
        "betF": c(beta.reshape(8, 128).T),
        "bet_bc": c(np.broadcast_to(beta[None, :], (128, 1024))),
        "ident_bf": np.eye(128, dtype=BFnp),
        "ident_f32": np.eye(128, dtype=f32),
        "maskji": (np.arange(128)[:, None] <= np.arange(128)[None, :]).astype(BFnp),
    }
    return shared


def run_device(inputs, stage=3, core_ids=None):
    shared = _prep_inputs(inputs)
    batch = np.asarray(inputs["L0_attn_batch"], np.float32)
    if core_ids is None:
        core_ids = list(range(8))
    nc = build(stage)
    in_maps = []
    for b in range(len(core_ids)):
        m = dict(shared)
        m["xin"] = np.ascontiguousarray(batch[b])
        in_maps.append(m)
    res = run_bass_kernel_spmd(nc, in_maps, core_ids=core_ids)
    return np.stack([res.results[i]["out"] for i in range(len(core_ids))], axis=0)


def kernel(**inputs):
    return run_device(inputs, stage=3, core_ids=list(range(8)))


# version 63
# speedup vs baseline: 1.1963x; 1.0002x over previous
import contextlib
import numpy as np
import ml_dtypes

import concourse.bass as bass
import concourse.mybir as mybir
from concourse.bass_utils import run_bass_kernel_spmd

F32 = mybir.dt.float32
BF16 = mybir.dt.bfloat16
I32 = mybir.dt.int32
AF = mybir.ActivationFunctionType
ALU = mybir.AluOpType
AX = mybir.AxisListType
BFnp = ml_dtypes.bfloat16

NT = 16
NHC = 24
NF = 32
SCALE_QK = float(64.0 ** -0.25)
INV127 = float(1.0 / 127.0)
HALF127 = float(0.5 / 127.0)
INV_SQM = float(1.0 / np.sqrt(np.float64(128.0)))
EPS_F = 1e-6
ENGINES = ("sync", "gpsimd", "tensor", "vector", "scalar")

A_Q = 0
A_K = 8192
A_V = 16384
A_Y = 0
A_RELU = 0
A_LNTM = 16384
ACTB_W = 24704

C_ATT = 0
C_QKVW = 16384
C_WO = 16384
BUFC_W = 24576


class Sem:
    def __init__(self, h):
        self.h = h
        self.n = 0


def build(stage=3, attn_lvl=6):
    nc = bass.Bass("TRN2", target_bir_lowering=False)

    xin = nc.dram_tensor("xin", [2048, 1024], F32, kind="ExternalInput").ap()
    wgT_i = nc.dram_tensor("wgT_i", [1024, 3072], I32, kind="ExternalInput").ap()
    wuT_i = nc.dram_tensor("wuT_i", [1024, 3072], I32, kind="ExternalInput").ap()
    wdT_i = nc.dram_tensor("wdT_i", [3072, 1024], I32, kind="ExternalInput").ap()
    Wq_t = nc.dram_tensor("Wq_t", [2, 128, 8, 512], F32, kind="ExternalInput").ap()
    Wk_t = nc.dram_tensor("Wk_t", [2, 128, 8, 512], F32, kind="ExternalInput").ap()
    Wv_t = nc.dram_tensor("Wv_t", [2, 128, 8, 512], F32, kind="ExternalInput").ap()
    Wo_sw = nc.dram_tensor("Wo_sw", [128, 8, 1024], F32, kind="ExternalInput").ap()
    W1_sw = nc.dram_tensor("W1_sw", [128, 32, 8, 128], F32, kind="ExternalInput").ap()
    W2_sw = nc.dram_tensor("W2_sw", [128, 8, 32, 128], F32, kind="ExternalInput").ap()
    omTd_in = nc.dram_tensor("omD", [128, 256], F32, kind="ExternalInput").ap()
    b1t_in = nc.dram_tensor("b1t", [128, 32], F32, kind="ExternalInput").ap()
    b2t_in = nc.dram_tensor("b2t", [128, 8], F32, kind="ExternalInput").ap()
    gam_in = nc.dram_tensor("gam_bc", [128, 1024], F32, kind="ExternalInput").ap()
    gamF_in = nc.dram_tensor("gamF", [128, 8], F32, kind="ExternalInput").ap()
    betF_in = nc.dram_tensor("betF", [128, 8], F32, kind="ExternalInput").ap()
    bet_in = nc.dram_tensor("bet_bc", [128, 1024], F32, kind="ExternalInput").ap()
    idb_in = nc.dram_tensor("ident_bf", [128, 128], BF16, kind="ExternalInput").ap()
    idf_in = nc.dram_tensor("ident_f32", [128, 128], F32, kind="ExternalInput").ap()
    msk_in = nc.dram_tensor("maskji", [128, 128], BF16, kind="ExternalInput").ap()
    out = nc.dram_tensor("out", [2048, 1024], F32, kind="ExternalOutput").ap()

    wgT_bf = nc.dram_tensor("wgT_bf", [1024, 3072], BF16, kind="Internal").ap()
    wuT_bf = nc.dram_tensor("wuT_bf", [1024, 3072], BF16, kind="Internal").ap()
    wdT_bf = nc.dram_tensor("wdT_bf", [3072, 1024], BF16, kind="Internal").ap()
    W1_bf = nc.dram_tensor("W1_bf", [128, 32, 8, 128], BF16, kind="Internal").ap()
    W2_bf = nc.dram_tensor("W2_bf", [128, 8, 32, 128], BF16, kind="Internal").ap()
    Wq_bf = nc.dram_tensor("Wq_bf", [2, 128, 8, 512], BF16, kind="Internal").ap()
    Wk_bf = nc.dram_tensor("Wk_bf", [2, 128, 8, 512], BF16, kind="Internal").ap()
    Wv_bf = nc.dram_tensor("Wv_bf", [2, 128, 8, 512], BF16, kind="Internal").ap()

    es = contextlib.ExitStack()
    with es:
        x_tm = es.enter_context(nc.sbuf_tensor("x_tm", [128, NT * 1024], F32))
        fmA = es.enter_context(nc.sbuf_tensor("fmA", [128, 8 * 2048], BF16))
        actB = es.enter_context(nc.sbuf_tensor("actB", [128, ACTB_W], BF16))
        BufC = es.enter_context(nc.sbuf_tensor("BufC", [128, BUFC_W], BF16))
        tmpF = es.enter_context(nc.sbuf_tensor("tmpF", [128, 1024], F32))
        gam_bf = es.enter_context(nc.sbuf_tensor("gam_bf", [128, 1024], BF16))
        gamF_s = es.enter_context(nc.sbuf_tensor("gamF_s", [128, 8], F32))
        betF_s = es.enter_context(nc.sbuf_tensor("betF_s", [128, 8], F32))
        bet_bf = es.enter_context(nc.sbuf_tensor("bet_bf", [128, 1024], BF16))
        Sf_t = es.enter_context(nc.sbuf_tensor("Sf_t", [128, 528], F32))
        amT = es.enter_context(nc.sbuf_tensor("amT", [128, 512], BF16))
        sbT = es.enter_context(nc.sbuf_tensor("sbT", [128, 1056], BF16))
        den_t = es.enter_context(nc.sbuf_tensor("den_t", [128, 8], F32))
        rec_t = es.enter_context(nc.sbuf_tensor("rec_t", [128, 8], F32))
        musum = es.enter_context(nc.sbuf_tensor("musum", [128, 16], F32))
        lneps = es.enter_context(nc.sbuf_tensor("lneps", [128, 1], F32))
        mu_c = es.enter_context(nc.sbuf_tensor("mu_c", [128, 16], F32))
        varsum = es.enter_context(nc.sbuf_tensor("varsum", [128, 16], F32))
        rstd_c = es.enter_context(nc.sbuf_tensor("rstd_c", [128, 16], F32))
        mu2 = es.enter_context(nc.sbuf_tensor("mu2", [128, 16], F32))
        rstd2 = es.enter_context(nc.sbuf_tensor("rstd2", [128, 16], F32))
        m2r2 = es.enter_context(nc.sbuf_tensor("m2r2", [128, 16], F32))
        st_nm = es.enter_context(nc.sbuf_tensor("st_nm", [128, 16], F32))
        omT_bf = es.enter_context(nc.sbuf_tensor("omT_bf", [128, 256], BF16))
        ident_b = es.enter_context(nc.sbuf_tensor("ident_b", [128, 128], BF16))
        ident_f = es.enter_context(nc.sbuf_tensor("ident_f", [128, 128], F32))
        maskji = es.enter_context(nc.sbuf_tensor("maskji_s", [128, 128], BF16))
        b1t = es.enter_context(nc.sbuf_tensor("b1t_s", [128, 32], F32))
        b2t = es.enter_context(nc.sbuf_tensor("b2t_s", [128, 8], F32))

        p0 = es.enter_context(nc.psum_tensor("p0", [128, 512], F32))
        p1 = es.enter_context(nc.psum_tensor("p1", [128, 512], F32))
        p2 = es.enter_context(nc.psum_tensor("p2", [128, 512], F32))
        p3 = es.enter_context(nc.psum_tensor("p3", [128, 512], F32))
        p4 = es.enter_context(nc.psum_tensor("p4", [128, 512], F32))
        p5 = es.enter_context(nc.psum_tensor("p5", [128, 512], F32))
        p6a = es.enter_context(nc.psum_tensor("p6a", [128, 1024], BF16))
        p6b = es.enter_context(nc.psum_tensor("p6b", [128, 1024], BF16))
        pP = (p0, p1)
        pQ = (p2, p3)
        p6 = (p6a, p6b)
        pOb = p6b.bitcast(F32)

        def new_sem(name):
            return Sem(es.enter_context(nc.semaphore(name)))
        s_x = new_sem("s_x")
        s_xc = [s_x] + [new_sem(f"s_x{k}") for k in (1, 2, 3)]
        s_c = new_sem("s_c")
        cwg = [new_sem(f"cwg{q}") for q in range(4)]
        cwu = [new_sem(f"cwu{q}") for q in range(4)]
        cwd = new_sem("cwd")
        cw1 = new_sem("cw1")
        cw2 = new_sem("cw2")
        s_gam = new_sem("s_gam")
        cwuh = [new_sem(f"cwuh{i}") for i in range(4)]
        cwq = [new_sem(f"cwq{i}") for i in range(2)]
        swA = new_sem("swA")
        swB = new_sem("swB")
        s_wd = new_sem("s_wd")
        sW2a = new_sem("sW2a")
        sW2b = new_sem("sW2b")
        qkvA = new_sem("qkvA")
        qkvB = new_sem("qkvB")
        s_wo = new_sem("s_wo")
        sync_dsems = [s_x, s_c, swA, swB, s_wd, sW2a, sW2b, qkvB] + s_xc[1:]
        gp_dsems = cwg + cwu + cwuh + cwq + [cwd, cw1, cw2, s_gam, qkvA, s_wo]
        eT = Sem(es.enter_context(nc.semaphore("eT")))
        eV = Sem(es.enter_context(nc.semaphore("eV")))
        eS = Sem(es.enter_context(nc.semaphore("eS")))
        mSY = Sem(es.enter_context(nc.semaphore("mSY")))
        mGP = Sem(es.enter_context(nc.semaphore("mGP")))
        eG = Sem(es.enter_context(nc.semaphore("eG")))

        ops = {e: [] for e in ENGINES}

        def op(eng, fn, waits=(), inc=None, inc_by=1):
            tag = None
            if inc is not None:
                inc.n += inc_by
                tag = inc.n
            wl = tuple((s, v) for (s, v) in waits if v and v > 0)

            def run(e, fn=fn, wl=wl, inc=inc, inc_by=inc_by):
                for (s, v) in wl:
                    e.wait_ge(s.h, v)
                r = fn(e)
                if inc is not None:
                    r.then_inc(inc.h, inc_by)
            ops[eng].append(run)
            return tag

        def dma(q, sem, out_ap, in_ap, waits=(), **kw):
            return op(q, lambda e, o=out_ap, i=in_ap, kw=dict(kw): e.dma_start(o, i, **kw),
                      waits, inc=sem, inc_by=16)

        def barrier():
            t1 = op("sync", lambda e: e.nop(),
                    waits=[(sm, sm.n) for sm in sync_dsems], inc=mSY)
            t2 = op("gpsimd", lambda e: e.nop(),
                    waits=[(sm, sm.n) for sm in gp_dsems] + [(eG, eG.n)], inc=mGP)
            t3 = op("tensor", lambda e: e.nop(), waits=[(eT, eT.n)], inc=eT)
            t4 = op("vector", lambda e: e.nop(), waits=[(eV, eV.n)], inc=eV)
            t5 = op("scalar", lambda e: e.nop(), waits=[(eS, eS.n)], inc=eS)
            wl = [(mSY, t1), (mGP, t2), (eT, t3), (eV, t4), (eS, t5)]
            for eng in ENGINES:
                op(eng, lambda e: None, waits=wl)

        t_xchunk = {}
        for c in range(NT):
            t = dma("sync" if c % 2 == 0 else "scalar", s_xc[c // 4],
                    x_tm[:, c * 1024:(c + 1) * 1024],
                    xin[c * 128:(c + 1) * 128, :])
            t_xchunk[c // 4] = t
        t_xall = s_x.n
        dma("sync", s_c, ident_b[:, :], idb_in)
        dma("sync", s_c, ident_f[:, :], idf_in)
        dma("sync", s_c, maskji[:, :], msk_in)
        dma("sync", s_c, b1t[:, :], b1t_in)
        dma("sync", s_c, b2t[:, :], b2t_in)
        t_call = s_c.n

        cast_wd_box = [None]

        lneps_tag = [0]

        def emit_outer_ln(eps, mu_store=None, rstd_store=None, src_wait=None,
                          src_tags=None, xl_wait=None, pipe=False,
                          pipe_chunks=True, pipe_stagger=False):
            mu_d = mu_store if mu_store is not None else mu_c
            rs_d = rstd_store if rstd_store is not None else rstd_c
            tRSd = {}
            if pipe:
                t_le = op("vector", lambda e, eps=float(eps):
                          e.memset(lneps[:, :], eps), inc=eV)
                lneps_tag[0] = t_le
                dump2 = actB[:, A_LNTM + 2048:A_LNTM + 3072]

                def emit_pstats(c):
                    w = []
                    if pipe_chunks and c % 4 == 0:
                        w = [(s_xc[c // 4], t_xchunk[c // 4])]
                    tR = op("vector", lambda e, c=c: e.reduce_sum(
                        musum[:, c:c + 1], x_tm[:, c * 1024:(c + 1) * 1024],
                        axis=AX.X), waits=w, inc=eV)
                    tMU = op("vector", lambda e, c=c, mu=mu_d: e.tensor_scalar(
                        mu[:, c:c + 1], musum[:, c:c + 1], 1.0 / 1024.0, None,
                        ALU.mult, ALU.bypass), waits=[(eV, tR)], inc=eV)
                    tSQ = op("scalar", lambda e, c=c, mu=mu_d, dump2=dump2:
                             e.activation(
                        dump2, x_tm[:, c * 1024:(c + 1) * 1024], AF.Square,
                        bias=mu[:, c:c + 1], scale=-1.0,
                        accum_out=varsum[:, c:c + 1]),
                        waits=[(eV, tMU)], inc=eS)
                    wl = [(eS, tSQ)] + ([(eV, t_le)] if c == 0 else [])
                    tLN = op("scalar", lambda e, c=c: e.activation(
                        varsum[:, c:c + 1], varsum[:, c:c + 1], AF.Ln,
                        bias=lneps[:, 0:1], scale=1.0 / 1024.0),
                        waits=wl, inc=eS)
                    tRSd[c] = op("scalar", lambda e, c=c, rs=rs_d: e.activation(
                        rs[:, c:c + 1], varsum[:, c:c + 1], AF.Exp, scale=-0.5),
                        waits=[(eS, tLN)], inc=eS)
            else:
                tR = None
                for c in range(NT):
                    if src_tags is not None:
                        w = [src_tags[c]]
                    else:
                        w = [src_wait] if (src_wait and c == 0) else []
                    tR = op("vector", lambda e, c=c: e.reduce_sum(
                        musum[:, c:c + 1], x_tm[:, c * 1024:(c + 1) * 1024], axis=AX.X),
                        waits=w, inc=eV)
                tMU = op("vector", lambda e, mu=mu_d: e.tensor_scalar(
                    mu[:, :], musum[:, :], 1.0 / 1024.0, None, ALU.mult, ALU.bypass),
                    waits=[(eV, tR)], inc=eV)
                tVS = None
                for c in range(NT):
                    w = [(eV, tMU)] if c == 0 else [(eS, tVS)]
                    tVS = op("scalar", lambda e, c=c, mu=mu_d: e.activation(
                        actB[:, A_LNTM:A_LNTM + 1024], x_tm[:, c * 1024:(c + 1) * 1024], AF.Square,
                        bias=mu[:, c:c + 1], scale=-1.0, accum_out=varsum[:, c:c + 1]),
                        waits=w, inc=eS)
                tSV = op("vector", lambda e, eps=float(eps): e.tensor_scalar(
                    varsum[:, :], varsum[:, :], 1.0 / 1024.0, eps, ALU.mult, ALU.add),
                    waits=[(eS, tVS)], inc=eV)
                tLN = op("scalar", lambda e: e.activation(varsum[:, :], varsum[:, :], AF.Ln),
                         waits=[(eV, tSV)], inc=eS)
                tRS = op("scalar", lambda e, rs=rs_d: e.activation(
                    rs[:, :], varsum[:, :], AF.Exp, scale=-0.5),
                    waits=[(eS, tLN)], inc=eS)
                for c in range(NT):
                    tRSd[c] = tRS
                if mu_store is not None:
                    op("vector", lambda e, mu=mu_d, rs=rs_d: e.scalar_tensor_tensor(
                        m2r2[:, :], mu[:, :], -1.0, rs[:, :], ALU.mult, ALU.mult),
                        waits=[(eS, tRS)], inc=eV)
            ev_tag, xl_tag, tr_tag = {}, {}, {}

            def emit_pxform(c):
                pi = c % 2
                lntm = actB[:, A_LNTM + pi * 1024: A_LNTM + (pi + 1) * 1024]
                w = [(eS, tRSd[c])]
                if xl_wait is not None and c == 0:
                    w.append(xl_wait)
                if c >= 2:
                    w.append((eT, tr_tag[c - 2]))
                xl_tag[c] = op("vector", lambda e, c=c, lntm=lntm, mu=mu_d, rs=rs_d:
                               e.tensor_scalar(lntm, x_tm[:, c * 1024:(c + 1) * 1024],
                                               mu[:, c:c + 1], rs[:, c:c + 1],
                                               ALU.subtract, ALU.mult),
                               waits=w, inc=eV)
                w2 = [(eV, xl_tag[c]), (s_c, t_call)]
                if c >= 2:
                    w2.append(ev_tag[c - 2])
                tt = None
                for j in range(8):
                    tt = op("tensor", lambda e, j=j, pi=pi, lntm=lntm: e.transpose(
                        p6[pi][:, j * 128:(j + 1) * 128], lntm[:, j * 128:(j + 1) * 128],
                        ident_b[:, :]),
                        waits=(w2 if j == 0 else ()), inc=eT)
                tr_tag[c] = tt

            def emit_pev(c):
                pi = c % 2
                dst = fmA[:, :].rearrange("p (k t) -> p k t", k=8)[:, :, c * 128:(c + 1) * 128]
                src = p6[pi][:, :].rearrange("p (k t) -> p k t", k=8)
                if pipe:
                    ev_tag[c] = (eS, op("scalar", lambda e, dst=dst, src=src:
                                        e.copy(dst, src),
                                        waits=[(eT, tr_tag[c])], inc=eS))
                else:
                    ev_tag[c] = (eV, op("vector", lambda e, dst=dst, src=src:
                                        e.tensor_copy(dst, src),
                                        waits=[(eT, tr_tag[c])], inc=eV))

            if pipe and pipe_stagger:
                for c in range(NT + 2):
                    if c < NT:
                        emit_pstats(c)
                    if c >= 2:
                        emit_pxform(c - 2)
                    if c >= 3:
                        emit_pev(c - 3)
                emit_pev(NT - 1)
            elif pipe:
                for c in range(NT):
                    emit_pstats(c)
                for c in range(NT):
                    emit_pxform(c)
                    emit_pev(c)
            else:
                for c in range(NT):
                    emit_pxform(c)
                    emit_pev(c)
            return ev_tag

        ffn_last = {"mm": 0, "evac": 0}

        def emit_ffn(ln_ev, xtags=None, stats_cb=None):
            pg = (p0, p1)
            pu = (p2, p3)
            n = [0]
            tagSilu, tagAct, tagG, tagU = {}, {}, {}, {}
            last_down_mm = [0]
            last_down_evac = [0]
            q_last_mm = {}
            for th in range(2):
                tt0 = th * 1024
                for q in range(4):
                    pq = q % 2
                    wg_off = pq * 12288
                    wu_off = pq * 12288 + 6144
                    swq = (cwg[0], cwg[1])[q % 2]
                    w = []
                    gq = th * 4 + q
                    if gq >= 2:
                        w.append((eT, q_last_mm[gq - 2]))
                    if th == 1 and q < 2:
                        w.append((eT, last_down_mm[0]))
                    src = wgT_i.rearrange("(k p) h -> p k h", p=128)[:, :, q * 768:(q + 1) * 768]
                    semh1 = (cwg[0], cwg[1])[q % 2]
                    semh2 = (cwg[2], cwg[3])[q % 2]
                    semu1 = (cwuh[0], cwuh[1])[q % 2]
                    semu2 = (cwuh[2], cwuh[3])[q % 2]
                    srcu = wuT_i.rearrange("(k p) h -> p k h", p=128)[:, :, q * 768:(q + 1) * 768]
                    semq1 = (cwq[0], cwq[1])[q % 2]
                    tQ1 = dma("gpsimd", semq1, BufC[:, wg_off:wg_off + 1536]
                              .rearrange("p (k h) -> p k h", k=2), src[:, 0:2, :],
                              waits=w)
                    tH1 = dma("gpsimd", semh1, BufC[:, wg_off + 1536:wg_off + 3072]
                              .rearrange("p (k h) -> p k h", k=2), src[:, 2:4, :])
                    tU1 = dma("gpsimd", semu1, BufC[:, wu_off:wu_off + 3072]
                              .rearrange("p (k h) -> p k h", k=4), srcu[:, 0:4, :])
                    tH2 = dma("gpsimd", semh2, BufC[:, wg_off + 3072:wg_off + 6144]
                              .rearrange("p (k h) -> p k h", k=4), src[:, 4:8, :])
                    tU2 = dma("gpsimd", semu2, BufC[:, wu_off + 3072:wu_off + 6144]
                              .rearrange("p (k h) -> p k h", k=4), srcu[:, 4:8, :])
                    if gq == 1 and cast_wd_box[0] is None:
                        cast_wd_box[0] = dma("gpsimd", cwd, wdT_bf[:, :],
                                             wdT_i[:, :], max_dma_last_dim=8192)
                    for hcl in range(6):
                        hc = q * 6 + hcl
                        for tsl in range(2):
                            i = n[0]
                            pi = i % 2
                            tg = None
                            for k in range(8):
                                wts = []
                                if k == 0:
                                    if hcl == 0:
                                        wts = [(semq1, tQ1),
                                               ln_ev[th * 8 + tsl * 4 + 3]]
                                    if i >= 2:
                                        wts.append((eS, tagSilu[i - 2]))
                                    if th == 1 and i == 48:
                                        wts.append((eV, last_down_evac[0]))
                                elif k == 2 and hcl == 0 and tsl == 0:
                                    wts = [(semh1, tH1)]
                                elif k == 4 and hcl == 0 and tsl == 0:
                                    wts = [(semh2, tH2)]
                                tg = op("tensor", lambda e, k=k, pi=pi, wg_off=wg_off,
                                        hcl=hcl, tt0=tt0, tsl=tsl:
                                        e.matmul(pg[pi][:, 0:512],
                                                 BufC[:, wg_off + k * 768 + hcl * 128:
                                                      wg_off + k * 768 + (hcl + 1) * 128],
                                                 fmA[:, k * 2048 + tt0 + tsl * 512:
                                                     k * 2048 + tt0 + (tsl + 1) * 512],
                                                 start=(k == 0), stop=(k == 7)),
                                        waits=wts, inc=(eT if k == 7 else None))
                            tagG[i] = tg
                            tu = None
                            for k in range(8):
                                wts = []
                                if k == 0:
                                    if hcl == 0:
                                        wts = [(semu1, tU1)]
                                    if i >= 2:
                                        wts.append((eV, tagAct[i - 2]))
                                elif k == 4 and hcl == 0 and tsl == 0:
                                    wts = [(semu2, tU2)]
                                tu = op("tensor", lambda e, k=k, pi=pi, wu_off=wu_off,
                                        hcl=hcl, tt0=tt0, tsl=tsl:
                                        e.matmul(pu[pi][:, 0:512],
                                                 BufC[:, wu_off + k * 768 + hcl * 128:
                                                      wu_off + k * 768 + (hcl + 1) * 128],
                                                 fmA[:, k * 2048 + tt0 + tsl * 512:
                                                     k * 2048 + tt0 + (tsl + 1) * 512],
                                                 start=(k == 0), stop=(k == 7)),
                                        waits=wts, inc=(eT if k == 7 else None))
                            tagU[i] = tu
                            wts = [(eT, tagG[i])]
                            if i >= 2:
                                wts.append((eV, tagAct[i - 2]))
                            tagSilu[i] = op("scalar", lambda e, pi=pi: e.activation(
                                tmpF[:, pi * 512:(pi + 1) * 512], pg[pi][:, 0:512],
                                AF.Silu, scale=INV127),
                                waits=wts, inc=eS)
                            tagAct[i] = op("vector", lambda e, pi=pi, hc=hc, tsl=tsl:
                                           e.scalar_tensor_tensor(
                                actB[:, hc * 1024 + tsl * 512: hc * 1024 + (tsl + 1) * 512],
                                pu[pi][:, 0:512], INV127,
                                tmpF[:, pi * 512:(pi + 1) * 512], ALU.mult, ALU.mult),
                                waits=[(eT, tagU[i]), (eS, tagSilu[i])], inc=eV)
                            n[0] += 1
                    q_last_mm[gq] = tagU[n[0] - 1]
                dma("sync", s_wd, BufC[:, 0:12288].rearrange("p (hc d) -> p hc d", hc=12),
                    wdT_bf.rearrange("(hc p) d -> p hc d", p=128)[:, 0:12, :],
                    waits=[(cwd, cast_wd_box[0]), (eT, q_last_mm[th * 4 + 2])])
                tWdh = s_wd.n
                tWd1 = dma("sync", cwu[0], BufC[:, 12288:16384]
                           .rearrange("p (hc d) -> p hc d", hc=4),
                           wdT_bf.rearrange("(hc p) d -> p hc d", p=128)[:, 12:16, :],
                           waits=[(eT, q_last_mm[th * 4 + 3])])
                tWd2 = dma("sync", cwu[1], BufC[:, 16384:20480]
                           .rearrange("p (hc d) -> p hc d", hc=4),
                           wdT_bf.rearrange("(hc p) d -> p hc d", p=128)[:, 16:20, :])
                tWd3 = dma("sync", cwu[2], BufC[:, 20480:24576]
                           .rearrange("p (hc d) -> p hc d", hc=4),
                           wdT_bf.rearrange("(hc p) d -> p hc d", p=128)[:, 20:24, :])
                tXU = {}
                for tl in range(8):
                    c = th * 8 + tl
                    da, db = ((p4, p5), (p0, p1))[tl % 2]
                    td = None
                    for hc in range(NHC):
                        wts = []
                        if hc == 0:
                            if tl == 0:
                                wts = [(s_wd, tWdh), (eV, tagAct[n[0] - 1])]
                            if tl >= 2:
                                wts.append((eV, tXU[tl - 2]))
                        elif tl == 0 and hc == 12:
                            wts = [(cwu[0], tWd1)]
                        elif tl == 0 and hc == 16:
                            wts = [(cwu[1], tWd2)]
                        elif tl == 0 and hc == 20:
                            wts = [(cwu[2], tWd3)]
                        op("tensor", lambda e, hc=hc, tl=tl, da=da:
                           e.matmul(da[:, 0:512],
                                    actB[:, hc * 1024 + tl * 128: hc * 1024 + (tl + 1) * 128],
                                    BufC[:, hc * 1024: hc * 1024 + 512],
                                    start=(hc == 0), stop=(hc == NHC - 1)),
                           waits=wts, inc=None)
                        td = op("tensor", lambda e, hc=hc, tl=tl, db=db:
                                e.matmul(db[:, 0:512],
                                         actB[:, hc * 1024 + tl * 128: hc * 1024 + (tl + 1) * 128],
                                         BufC[:, hc * 1024 + 512: hc * 1024 + 1024],
                                         start=(hc == 0), stop=(hc == NHC - 1)),
                                waits=(), inc=(eT if hc == NHC - 1 else None))
                    op("vector", lambda e, c=c, da=da: e.scalar_tensor_tensor(
                        x_tm[:, c * 1024: c * 1024 + 512], da[:, 0:512], HALF127,
                        x_tm[:, c * 1024: c * 1024 + 512], ALU.mult, ALU.add),
                        waits=[(eT, td)], inc=eV)
                    tXU[tl] = op("vector", lambda e, c=c, db=db: e.scalar_tensor_tensor(
                        x_tm[:, c * 1024 + 512: (c + 1) * 1024], db[:, 0:512], HALF127,
                        x_tm[:, c * 1024 + 512: (c + 1) * 1024], ALU.mult, ALU.add),
                        waits=(), inc=eV)
                    last_down_mm[0] = td
                    last_down_evac[0] = tXU[tl]
                    ffn_last["mm"] = td
                    ffn_last["evac"] = tXU[tl]
                    if xtags is not None:
                        xtags[c] = (eV, tXU[tl])
                    if stats_cb is not None:
                        stats_cb(c)

        ln1_ev = emit_outer_ln(1e-5, pipe=True)
        ffn1_x = {}
        ln2_sqt = {}

        def ln2_stats_cb(c):
            tRc = op("vector", lambda e, c=c: e.reduce_sum(
                musum[:, c:c + 1], x_tm[:, c * 1024:(c + 1) * 1024], axis=AX.X),
                waits=[ffn1_x[c]], inc=eV)
            tMUc = op("vector", lambda e, c=c: e.tensor_scalar(
                mu2[:, c:c + 1], musum[:, c:c + 1], 1.0 / 1024.0, None,
                ALU.mult, ALU.bypass), waits=[(eV, tRc)], inc=eV)
            tSQc = op("scalar", lambda e, c=c: e.activation(
                tmpF[:, 0:1024], x_tm[:, c * 1024:(c + 1) * 1024], AF.Square,
                bias=mu2[:, c:c + 1], scale=-1.0, accum_out=varsum[:, c:c + 1]),
                waits=[(eV, tMUc)], inc=eS)
            wl = [(eS, tSQc)] + ([(eV, lneps_tag[0])] if c == 0 else [])
            tLNc = op("scalar", lambda e, c=c: e.activation(
                varsum[:, c:c + 1], varsum[:, c:c + 1], AF.Ln,
                bias=lneps[:, 0:1], scale=1.0 / 1024.0), waits=wl, inc=eS)
            ln2_sqt[c] = op("scalar", lambda e, c=c: e.activation(
                rstd2[:, c:c + 1], varsum[:, c:c + 1], AF.Exp, scale=-0.5),
                waits=[(eS, tLNc)], inc=eS)
            op("gpsimd", lambda e, c=c: e.tensor_scalar(
                m2r2[:, c:c + 1], mu2[:, c:c + 1], -1.0, rstd2[:, c:c + 1],
                ALU.mult, ALU.mult), waits=[(eS, ln2_sqt[c])], inc=eG)

        emit_ffn(ln1_ev, xtags=ffn1_x,
                 stats_cb=(ln2_stats_cb if stage >= 2 else None))
        cast_wq = dma("gpsimd", qkvA, Wq_bf, Wq_t, max_dma_last_dim=8192)
        cast_wk = dma("gpsimd", qkvA, Wk_bf, Wk_t, max_dma_last_dim=8192)
        cast_wv = dma("gpsimd", qkvA, Wv_bf, Wv_t, max_dma_last_dim=8192)
        t_qkvcast = qkvA.n
        dma("gpsimd", s_gam, gamF_s[:, :], gamF_in)
        dma("gpsimd", s_gam, betF_s[:, :], betF_in)
        dma("gpsimd", s_gam, gam_bf[:, :], gam_in, max_dma_last_dim=8192)
        dma("gpsimd", s_gam, bet_bf[:, :], bet_in, max_dma_last_dim=8192)
        dma("gpsimd", s_gam, omT_bf[:, :], omTd_in, max_dma_last_dim=8192)
        t_gamall = s_gam.n
        if stage < 2:
            barrier()

        if stage >= 2:
            ffn1_mm_tag = ffn_last["mm"]
            ln2_ev, xl_tag2, tr_tag2 = {}, {}, {}
            for c in range(NT):
                pi = c % 2
                lntm = actB[:, A_LNTM + pi * 1024: A_LNTM + (pi + 1) * 1024]
                w = [(eS, ln2_sqt[c])]
                if c <= 1:
                    w.append((eT, ffn1_mm_tag))
                if c >= 2:
                    w.append((eT, tr_tag2[c - 2]))
                xl_tag2[c] = op("vector", lambda e, c=c, lntm=lntm:
                                e.tensor_scalar(lntm, x_tm[:, c * 1024:(c + 1) * 1024],
                                                mu2[:, c:c + 1], rstd2[:, c:c + 1],
                                                ALU.subtract, ALU.mult),
                                waits=w, inc=eV)
                w2 = [(eV, xl_tag2[c])]
                if c >= 2:
                    w2.append(ln2_ev[c - 2])
                tt = None
                for j in range(8):
                    tt = op("tensor", lambda e, j=j, pi=pi, lntm=lntm: e.transpose(
                        p6[pi][:, j * 128:(j + 1) * 128], lntm[:, j * 128:(j + 1) * 128],
                        ident_b[:, :]), waits=(w2 if j == 0 else ()), inc=eT)
                tr_tag2[c] = tt
                dst = fmA[:, :].rearrange("p (k t) -> p k t", k=8)[:, :, c * 128:(c + 1) * 128]
                src = p6[pi][:, :].rearrange("p (k t) -> p k t", k=8)
                ln2_ev[c] = (eV, op("vector", lambda e, dst=dst, src=src:
                                    e.tensor_copy(dst, src), waits=[(eT, tt)], inc=eV))
            t_ln2 = ln2_ev[NT - 1][1]

            def ln2_ev_w(c):
                return ln2_ev[c]
            cast_w1 = dma("gpsimd", cw1, W1_bf[:, :, :, :], W1_sw,
                          waits=[(eV, t_ln2)], max_dma_last_dim=8192)
            cast_w2 = dma("gpsimd", cw2, W2_bf[:, :, :, :], W2_sw,
                          waits=[(eV, t_ln2)], max_dma_last_dim=8192)

            QGW = 12352

            def a_q(pg):
                return pg * QGW

            def a_k(pg):
                return pg * QGW + 4096

            def a_v(pg):
                return pg * QGW + 8192

            C_SLAB = 16384
            T_TM = 20480
            T_FM = 22528

            t_ones = None
            for pg in range(2):
                ap1 = bass.AP(actB, a_v(pg) + 64, [[ACTB_W, 128], [65, 64]])
                t_ones = op("vector", lambda e, ap1=ap1: e.memset(ap1, 1.0), inc=eV)

            slab_lastmm = {}
            slab_cnt = [0]
            qkv_rot_hist = []
            qkv_rot_cnt = [0]
            qk_done = {}
            v_done = {}

            def qkv_gen(qg):
                g, qq, pg = qg // 2, qg % 2, qg % 2
                for wi, (Wsrc, isV) in enumerate(((Wq_bf, 0), (Wk_bf, 0), (Wv_bf, 1))):
                    pw = slab_cnt[0] % 2
                    slab_cnt[0] += 1
                    qsem = qkvB
                    wts = [(qsem, qsem.n), (qkvA, t_qkvcast)]
                    if pw in slab_lastmm:
                        wts.append((eT, slab_lastmm[pw]))
                    else:
                        wts.append((eT, ffn1_mm_tag))
                    tW = dma("sync", qsem, BufC[:, C_SLAB + pw * 2048:
                                                C_SLAB + (pw + 1) * 2048]
                             .rearrange("p (k j) -> p k j", k=8),
                             Wsrc[g, :, :, qq * 256:(qq + 1) * 256],
                             waits=wts)
                    lm = None
                    if not isV:
                        dst0 = a_q(pg) if wi == 0 else a_k(pg)
                        for mloc in range(2):
                            for ts in range(4):
                                pr = p4
                                qkv_rot_cnt[0] += 1
                                for k in range(8):
                                    wts = []
                                    if k == 0:
                                        wts = [ln2_ev_w(ts * 4 + 3)]
                                        if mloc == 0 and ts == 0:
                                            wts.append((qsem, tW))
                                        if len(qkv_rot_hist) >= 1:
                                            wts.append(qkv_rot_hist[-1])
                                    lm = op("tensor", lambda e, k=k, mloc=mloc, ts=ts,
                                            pr=pr, pw=pw:
                                            e.matmul(pr[:, 0:512],
                                                     BufC[:, C_SLAB + pw * 2048 + k * 256
                                                          + mloc * 128:
                                                          C_SLAB + pw * 2048 + k * 256
                                                          + (mloc + 1) * 128],
                                                     fmA[:, k * 2048 + ts * 512:
                                                         k * 2048 + (ts + 1) * 512],
                                                     start=(k == 0), stop=(k == 7)),
                                            waits=wts, inc=(eT if k == 7 else None))
                                yield
                                tE = op("scalar", lambda e, pr=pr, mloc=mloc, ts=ts,
                                        dst0=dst0: e.activation(
                                    actB[:, dst0 + mloc * 2048 + ts * 512:
                                         dst0 + mloc * 2048 + (ts + 1) * 512],
                                    pr[:, 0:512], AF.Identity, scale=SCALE_QK),
                                    waits=[(eT, lm)], inc=eS)
                                qkv_rot_hist.append((eS, tE))
                                qk_done[qg] = tE
                                yield
                        slab_lastmm[pw] = lm
                    else:
                        for c16 in range(16):
                            pr = p4
                            po = (c16 % 2) * 256
                            qkv_rot_cnt[0] += 1
                            for k in range(8):
                                wts = []
                                if k == 0:
                                    wts = [ln2_ev_w(c16)]
                                    if c16 == 0:
                                        wts.append((qsem, tW))
                                    if c16 % 2 == 0 and len(qkv_rot_hist) >= 1:
                                        wts.append(qkv_rot_hist[-1])
                                lm = op("tensor", lambda e, k=k, c16=c16, pr=pr,
                                        pw=pw, po=po:
                                        e.matmul(pr[:, po:po + 256],
                                                 fmA[:, k * 2048 + c16 * 128:
                                                     k * 2048 + (c16 + 1) * 128],
                                                 BufC[:, C_SLAB + pw * 2048 + k * 256:
                                                      C_SLAB + pw * 2048 + (k + 1) * 256],
                                                 start=(k == 0), stop=(k == 7)),
                                        waits=wts, inc=(eT if k == 7 else None))
                            yield
                            if c16 % 2 == 1:
                                dst = bass.AP(actB, a_v(pg) + (c16 - 1) * 260,
                                              [[ACTB_W, 128], [65, 8], [1, 64]])
                                srcv = pr[:, 0:512].rearrange(
                                    "p (hh w) -> p hh w", w=64)
                                wv = [(eT, lm), (eV, t_ones)]
                                tE = op("vector", lambda e, dst=dst, srcv=srcv:
                                        e.tensor_copy(dst, srcv), waits=wv, inc=eV)
                                qkv_rot_hist.append((eV, tE))
                                v_done[qg] = tE
                            yield
                        slab_lastmm[pw] = lm

            NE = 128
            its = [(qg, c, pair) for qg in range(4) for c in range(16)
                   for pair in range(2)]
            tPK, tRM, tEXP, tEPS, tTR, tD, tFME = {}, {}, {}, {}, {}, {}, {}
            tSA, tSBE, tA, tAM, tO, tSCL = {}, {}, {}, {}, {}, {}
            last_projk = {}

            def st_proj(t):
                qg, c, pair = its[t]
                pg = qg % 2
                P = (p0, p1, p2)[t % 3]
                m = pair
                wts = []
                if c == 0:
                    wts = [(eS, qk_done[qg]), (eV, v_done[qg]), (s_gam, t_gamall)]
                if (t - 3) in tEXP:
                    wts.append((eS, tEXP[t - 3]))
                qsl = actB[:, a_q(pg) + m * 2048 + c * 128:
                           a_q(pg) + m * 2048 + (c + 1) * 128]
                op("tensor", lambda e, qsl=qsl, P=P:
                   e.matmul(P[:, 0:256], qsl, omT_bf[:, :], start=True, stop=True),
                   waits=wts, inc=None)
                ksl = actB[:, a_k(pg) + m * 2048 + c * 128:
                           a_k(pg) + m * 2048 + (c + 1) * 128]
                tPK[t] = op("tensor", lambda e, ksl=ksl, P=P:
                            e.matmul(P[:, 256:512], ksl, omT_bf[:, :],
                                     start=True, stop=True), waits=(), inc=eT)
                last_projk[qg] = tPK[t]

            def st_rm(t):
                P = (p0, p1, p2)[t % 3]
                wts = [(eT, tPK[t])]
                if (t - 2) in tEXP:
                    wts.append((eS, tEXP[t - 2]))
                tRM[t] = op("vector", lambda e, P=P, t=t: e.tensor_reduce(
                    st_nm[:, (t % 2) * 4: (t % 2) * 4 + 4],
                    P[:, 0:512].rearrange("p (a b) -> p a b", a=4),
                    axis=AX.X, op=ALU.max, negate=True), waits=wts, inc=eV)

            def st_exp(t):
                P = (p0, p1, p2)[t % 3]
                tmo = T_TM + (t % 4) * 512
                tE = None
                for j in range(4):
                    wts = []
                    if j == 0:
                        wts = [(eV, tRM[t])]
                        if (t - 4) in tD:
                            wts.append((eT, tD[t - 4]))
                    tE = op("scalar", lambda e, P=P, tmo=tmo, j=j, t=t:
                            e.activation(
                        BufC[:, tmo + j * 128: tmo + (j + 1) * 128],
                        P[:, j * 128:(j + 1) * 128], AF.Exp,
                        bias=st_nm[:, (t % 2) * 4 + j: (t % 2) * 4 + j + 1]),
                        waits=wts, inc=(eS if j == 3 else None))
                tEXP[t] = tE

            def st_eps(t):
                tmo = T_TM + (t % 4) * 512
                tEPS[t] = op("vector", lambda e, tmo=tmo: e.tensor_scalar(
                    BufC[:, tmo: tmo + 512], BufC[:, tmo: tmo + 512],
                    INV_SQM, EPS_F, ALU.mult, ALU.add),
                    waits=[(eS, tEXP[t])], inc=eV)

            def st_tr_delta(t):
                qg, c, pair = its[t]
                pg = qg % 2
                tmo = T_TM + (t % 4) * 512
                tro = (t % 2) * 512
                wts = [(eV, tEPS[t]), (s_c, t_call)]
                if (t - 1) in tFME:
                    wts.append((eS, tFME[t - 1]))
                tt = None
                for j in range(4):
                    tt = op("tensor", lambda e, tmo=tmo, tro=tro, j=j:
                             e.transpose(p6a[:, tro + j * 128: tro + (j + 1) * 128],
                                         BufC[:, tmo + j * 128: tmo + (j + 1) * 128],
                                         ident_b[:, :]),
                             waits=(wts if j == 0 else ()),
                             inc=(eT if j == 3 else None))
                tTR[t] = tt
                wts = []
                if (t - 1) in tSA:
                    wts.append((eV, tSA[t - 1]))
                td = None
                for j in range(2):
                    vsl = actB[:, a_v(pg) + c * 260 + (pair * 2 + j) * 65:
                               a_v(pg) + c * 260 + (pair * 2 + j) * 65 + 65]
                    td = op("tensor", lambda e, tmo=tmo, j=j, t=t, vsl=vsl:
                            e.matmul(p3[:, (t % 2) * 256 + j * 128:
                                         (t % 2) * 256 + j * 128 + 65],
                                     BufC[:, tmo + 256 + j * 128:
                                          tmo + 256 + (j + 1) * 128],
                                     vsl, start=True, stop=True),
                            waits=(wts if j == 0 else ()),
                            inc=(eT if j == 1 else None))
                tD[t] = td

            def st_fme_sacc(t):
                qg, c, pair = its[t]
                tro = (t % 2) * 512
                fmo = T_FM + (t % 4) * 512
                wts = [(eT, tTR[t])]
                if (t - 4) in tO:
                    wts.append((eT, tO[t - 4]))
                tFME[t] = op("scalar", lambda e, tro=tro, fmo=fmo: e.copy(
                    BufC[:, fmo: fmo + 512], p6a[:, tro: tro + 512]),
                    waits=wts, inc=eS)
                sfo = ((qg % 2) * 2 + pair) * 132
                sfv = bass.AP(Sf_t, sfo, [[528, 128], [66, 2], [1, 65]])
                d3v = bass.AP(p3, (t % 2) * 256, [[512, 128], [128, 2], [1, 65]])
                wts = [(eT, tD[t])]
                if (t - 2) in tSA:
                    wts.append((eV, tSA[t - 2]))
                if (t - 2) in tSBE:
                    wts.append((eS, tSBE[t - 2]))
                if c == 0:
                    tSA[t] = op("vector", lambda e, sfv=sfv, d3v=d3v:
                                e.tensor_copy(sfv, d3v), waits=wts, inc=eV)
                else:
                    tSA[t] = op("vector", lambda e, sfv=sfv, d3v=d3v:
                                e.scalar_tensor_tensor(sfv, d3v, 1.0, sfv,
                                                       ALU.mult, ALU.add),
                                waits=wts, inc=eV)

            def st_abar_sbe(t):
                qg, c, pair = its[t]
                fmo = T_FM + (t % 4) * 512
                ao = (t % 2) * 256
                wts = [(eS, tFME[t])]
                if (t - 1) in tAM:
                    wts.append((eV, tAM[t - 1]))
                ta = None
                for j in range(2):
                    ta = op("tensor", lambda e, fmo=fmo, ao=ao, j=j: e.matmul(
                        p5[:, ao + j * 128: ao + (j + 1) * 128],
                        BufC[:, fmo + 256 + j * 128: fmo + 256 + (j + 1) * 128],
                        BufC[:, fmo + j * 128: fmo + (j + 1) * 128],
                        start=True, stop=True),
                        waits=(wts if j == 0 else ()), inc=(eT if j == 1 else None))
                tA[t] = ta
                sfo = ((qg % 2) * 2 + pair) * 132
                sbo = ((c % 3) * 2 + pair) * 132
                sbv = bass.AP(sbT, sbo, [[1056, 128], [66, 2], [1, 65]])
                sfr = bass.AP(Sf_t, sfo, [[528, 128], [66, 2], [1, 65]])
                wts = [(eV, tSA[t])]
                if (t - 4) in tO:
                    wts.append((eT, tO[t - 4]))
                tSBE[t] = op("scalar", lambda e, sbv=sbv, sfr=sfr:
                             e.copy(sbv, sfr), waits=wts, inc=eS)

            def st_mask(t):
                ao = (t % 2) * 256
                mb = bass.AP(maskji, 0, [[128, 128], [0, 2], [1, 128]])
                wts = [(eT, tA[t])]
                if (t - 2) in tO:
                    wts.append((eT, tO[t - 2]))
                amo = (t % 2) * 256
                tAM[t] = op("vector", lambda e, ao=ao, amo=amo, mb=mb: e.tensor_tensor(
                    amT[:, amo: amo + 256].rearrange("p (a b) -> p a b", a=2),
                    p5[:, ao: ao + 256].rearrange("p (a b) -> p a b", a=2),
                    mb, op=ALU.mult), waits=wts, inc=eV)

            def st_o(t):
                qg, c, pair = its[t]
                pg = qg % 2
                fmo = T_FM + (t % 4) * 512
                oo = (t % 2) * 256
                amo = (t % 2) * 256
                wts = [(eV, tAM[t])]
                if (t - 1) in tSCL:
                    wts.append((eV, tSCL[t - 1]))
                if c > 0:
                    wts.append((eS, tSBE[t - 2]))
                to = None
                for j in range(2):
                    vsl = actB[:, a_v(pg) + c * 260 + (pair * 2 + j) * 65:
                               a_v(pg) + c * 260 + (pair * 2 + j) * 65 + 65]
                    to = op("tensor", lambda e, oo=oo, amo=amo, j=j, vsl=vsl, c=c:
                            e.matmul(pOb[:, oo + j * 128: oo + j * 128 + 65],
                                     amT[:, amo + j * 128: amo + (j + 1) * 128],
                                     vsl, start=True, stop=(c == 0)),
                            waits=(wts if j == 0 else ()),
                            inc=(eT if (c == 0 and j == 1) else None))
                    if c > 0:
                        sbo = (((c - 1) % 3) * 2 + pair) * 132
                        sbs = sbT[:, sbo + j * 66: sbo + j * 66 + 65]
                        to = op("tensor", lambda e, oo=oo, fmo=fmo, j=j, sbs=sbs:
                                e.matmul(pOb[:, oo + j * 128: oo + j * 128 + 65],
                                         BufC[:, fmo + j * 128: fmo + (j + 1) * 128],
                                         sbs, start=False, stop=True),
                                waits=(), inc=(eT if j == 1 else None))
                tO[t] = to

            def st_tail(t):
                qg, c, pair = its[t]
                oo = (t % 2) * 256
                dnv = bass.AP(pOb, oo + 64, [[512, 128], [128, 2], [1, 1]])
                tdn = op("vector", lambda e, dnv=dnv, t=t: e.tensor_scalar(
                    den_t[:, (t % 2) * 2: (t % 2) * 2 + 2].rearrange(
                        "p (a b) -> p a b", b=1), dnv, EPS_F, None,
                    ALU.add, ALU.bypass), waits=[(eT, tO[t])], inc=eV)
                trc = op("vector", lambda e, t=t: e.reciprocal(
                    rec_t[:, (t % 2) * 2: (t % 2) * 2 + 2],
                    den_t[:, (t % 2) * 2: (t % 2) * 2 + 2]),
                    waits=[(eV, tdn)], inc=eV)
                outv = BufC[:, c * 1024 + (qg * 4 + pair * 2) * 64:
                            c * 1024 + (qg * 4 + pair * 2) * 64 + 128] \
                    .rearrange("p (a b) -> p a b", a=2)
                o6v = bass.AP(pOb, oo, [[512, 128], [128, 2], [1, 64]])
                rcb = bass.AP(rec_t, (t % 2) * 2, [[8, 128], [1, 2], [0, 64]])
                tSCL[t] = op("vector", lambda e, outv=outv, o6v=o6v, rcb=rcb:
                             e.tensor_tensor(outv, o6v, rcb, op=ALU.mult),
                             waits=[(eV, trc)], inc=eV)

            STAGES_A = [(6, st_abar_sbe),
                        (5, st_fme_sacc), (4, st_tr_delta), (3, st_eps),
                        (2, st_exp)]
            STAGES_B = [(1, st_rm), (0, st_proj)]

            gens = [qkv_gen(qg) for qg in range(4)]
            for _ in gens[0]:
                pass
            done_g = [True, False, False, False]

            def adv_gen(gq):
                if gq <= 3 and not done_g[gq]:
                    try:
                        next(gens[gq])
                    except StopIteration:
                        done_g[gq] = True

            for sl in range(NE + 9):
                if 0 <= sl - 8 < NE:
                    st_o(sl - 8)
                if 0 <= sl - 7 < NE:
                    st_mask(sl - 7)
                if 0 <= sl - 8 < NE:
                    st_tail(sl - 8)
                gq = sl // 32 + 1
                adv_gen(gq)
                for ofs, fn in STAGES_A:
                    t = sl - ofs
                    if 0 <= t < NE:
                        fn(t)
                adv_gen(gq)
                for ofs, fn in STAGES_B:
                    t = sl - ofs
                    if 0 <= t < NE:
                        fn(t)
                if sl % 32 == 31 and gq <= 3 and not done_g[gq]:
                    for _ in gens[gq]:
                        pass
                    done_g[gq] = True
            barrier()

            tWo = dma("gpsimd", s_wo, BufC[:, C_WO:C_WO + 4096]
                      .rearrange("p (k j) -> p k j", k=4), Wo_sw[:, 0:4, :],
                      max_dma_last_dim=8192)
            tWo2 = dma("gpsimd", cwu[3], BufC[:, C_WO + 4096:C_WO + 8192]
                       .rearrange("p (k j) -> p k j", k=4), Wo_sw[:, 4:8, :],
                       max_dma_last_dim=8192)
            ppair = ((p0, p1), (p2, p3))
            tY, tAF, tWoMM = {}, {}, {}
            ypend = {}

            def emit_ypair(c):
                ba, bb, tmpo, tTmpc = ypend[c]
                op("vector", lambda e, c=c, ba=ba, tmpo=tmpo: e.tensor_tensor(
                    actB[:, A_Y + c * 1024: A_Y + c * 1024 + 512],
                    actB[:, tmpo: tmpo + 512], ba[:, 0:512], op=ALU.add),
                    waits=[(eS, tTmpc), (eT, tWoMM[c])], inc=eV)
                tY[c] = op("vector", lambda e, c=c, bb=bb, tmpo=tmpo: e.tensor_tensor(
                    actB[:, A_Y + c * 1024 + 512: A_Y + (c + 1) * 1024],
                    actB[:, tmpo + 512: tmpo + 1024], bb[:, 0:512], op=ALU.add),
                    waits=(), inc=eV)

            for c in range(NT):
                pi = c % 2
                fi = c % 3
                wts = []
                if c >= 2:
                    wts.append((eV, tAF[c - 2]))
                tt = None
                for j in range(8):
                    tt = op("tensor", lambda e, j=j, pi=pi, c=c: e.transpose(
                        p6[pi][:, j * 128:(j + 1) * 128],
                        BufC[:, c * 1024 + j * 128: c * 1024 + (j + 1) * 128],
                        ident_b[:, :]), waits=(wts if j == 0 else ()), inc=eT)
                wv = [(eT, tt)]
                if c >= 3:
                    wv.append((eT, tWoMM[c - 3]))
                tAF[c] = op("vector", lambda e, pi=pi, fi=fi: e.tensor_copy(
                    actB[:, A_LNTM + fi * 1024: A_LNTM + (fi + 1) * 1024],
                    p6[pi][:, 0:1024]), waits=wv, inc=eV)
                ba, bb = ppair[pi]
                for k in range(8):
                    wts = []
                    if k == 0:
                        wts = [(eV, tAF[c])]
                        if c <= 1:
                            wts.append((s_wo, tWo))
                        if c >= 2:
                            wts.append((eV, tY[c - 2]))
                    elif k == 4 and c == 0:
                        wts = [(cwu[3], tWo2)]
                    op("tensor", lambda e, k=k, fi=fi, ba=ba: e.matmul(
                        ba[:, 0:512],
                        actB[:, A_LNTM + fi * 1024 + k * 128: A_LNTM + fi * 1024 + (k + 1) * 128],
                        BufC[:, C_WO + k * 1024: C_WO + k * 1024 + 512],
                        start=(k == 0), stop=(k == 7)), waits=wts, inc=None)
                    tWoMM[c] = op("tensor", lambda e, k=k, fi=fi, bb=bb: e.matmul(
                        bb[:, 0:512],
                        actB[:, A_LNTM + fi * 1024 + k * 128: A_LNTM + fi * 1024 + (k + 1) * 128],
                        BufC[:, C_WO + k * 1024 + 512: C_WO + (k + 1) * 1024],
                        start=(k == 0), stop=(k == 7)), waits=(),
                        inc=(eT if k == 7 else None))
                tmpo = 20480 + pi * 1024
                wts = []
                if c >= 2:
                    wts.append((eV, tY[c - 2]))
                tTmp = op("scalar", lambda e, c=c, tmpo=tmpo: e.activation(
                    actB[:, tmpo: tmpo + 1024], x_tm[:, c * 1024:(c + 1) * 1024],
                    AF.Identity, bias=m2r2[:, c:c + 1], scale=rstd2[:, c:c + 1]),
                    waits=wts, inc=eS)
                ypend[c] = (ba, bb, tmpo, tTmp)
                if c >= 1:
                    emit_ypair(c - 1)
            emit_ypair(NT - 1)
            barrier()

            BufCf = BufC.bitcast(F32)
            t_le2 = op("vector", lambda e: e.memset(lneps[:, :], 1e6), inc=eV)
            dump2i = actB[:, A_LNTM + 2048:A_LNTM + 3072]
            ev_tag, tr_tag, hb_tag = {}, {}, {}

            def emit_iev(c):
                pi = c % 2
                te = None
                for k in range(8):
                    dst = fmA[:, k * 2048 + c * 128: k * 2048 + (c + 1) * 128]
                    src = p6[pi][:, k * 128:(k + 1) * 128]
                    te = op("vector", lambda e, dst=dst, src=src, k=k:
                            e.tensor_scalar(dst, src, gamF_s[:, k:k + 1],
                                            betF_s[:, k:k + 1], ALU.mult, ALU.add),
                            waits=([(eT, tr_tag[c])] if k == 0 else ()),
                            inc=(eV if k == 7 else None))
                ev_tag[c] = (eV, te)

            tRSi = {}

            def emit_istats(c):
                tRi = op("scalar", lambda e, c=c: e.activation(
                    dump2i, actB[:, A_Y + c * 1024: A_Y + (c + 1) * 1024],
                    AF.Identity, accum_out=musum[:, c:c + 1]),
                    waits=[(eV, tY[c])], inc=eS)
                tMUi = op("gpsimd", lambda e, c=c: e.tensor_scalar(
                    mu_c[:, c:c + 1], musum[:, c:c + 1], 1.0 / 1024.0, None,
                    ALU.mult, ALU.bypass), waits=[(eS, tRi)], inc=eG)
                tSQi = op("scalar", lambda e, c=c: e.activation(
                    dump2i, actB[:, A_Y + c * 1024: A_Y + (c + 1) * 1024],
                    AF.Square, bias=mu_c[:, c:c + 1], scale=-1.0,
                    accum_out=varsum[:, c:c + 1]), waits=[(eG, tMUi)], inc=eS)
                wl = [(eS, tSQi)] + ([(eV, t_le2)] if c == 0 else [])
                tLNi = op("scalar", lambda e, c=c: e.activation(
                    varsum[:, c:c + 1], varsum[:, c:c + 1], AF.Ln,
                    bias=lneps[:, 0:1], scale=1.0 / 1024.0), waits=wl, inc=eS)
                tRSi[c] = op("scalar", lambda e, c=c: e.activation(
                    rstd_c[:, c:c + 1], varsum[:, c:c + 1], AF.Exp, scale=-0.5),
                    waits=[(eS, tLNi)], inc=eS)

            def emit_ixform(c):
                pi = c % 2
                lntm = actB[:, A_LNTM + pi * 1024: A_LNTM + (pi + 1) * 1024]
                wts = [(eS, tRSi[c])]
                if c == 0:
                    wts.append((s_gam, t_gamall))
                if c >= 2:
                    wts.append((eT, tr_tag[c - 2]))
                tT3 = op("vector", lambda e, c=c, lntm=lntm: e.tensor_scalar(
                    lntm, actB[:, A_Y + c * 1024: A_Y + (c + 1) * 1024],
                    mu_c[:, c:c + 1], rstd_c[:, c:c + 1],
                    ALU.subtract, ALU.mult), waits=wts, inc=eV)
                w2 = [(eV, tT3)]
                if c >= 2:
                    w2.append(ev_tag[c - 2])
                tt = None
                for j in range(8):
                    tt = op("tensor", lambda e, j=j, pi=pi, lntm=lntm: e.transpose(
                        p6[pi][:, j * 128:(j + 1) * 128], lntm[:, j * 128:(j + 1) * 128],
                        ident_b[:, :]), waits=(w2 if j == 0 else ()), inc=eT)
                tr_tag[c] = tt

            for c in range(NT + 2):
                if c < NT:
                    emit_istats(c)
                if c >= 2:
                    emit_ixform(c - 2)
                if c >= 3:
                    emit_iev(c - 3)
            emit_iev(NT - 1)
            barrier()

            relu_hist = []
            w1mm_hist = []
            w2mm_hist = []
            o2_hist = []
            tr2_hist = []
            xu2_hist = []
            w2_last_of_quarter = 0
            for tq in range(4):
                ts0 = tq * 512
                tW1g = {}

                def w1_load(gf, tq=tq):
                    pi4 = gf % 2
                    w1sem = (swA, swB)[pi4]
                    wts = [(cw1, cast_w1)]
                    if w1mm_hist:
                        wts.append((eT, w1mm_hist[-1]))
                    tW1g[gf] = dma("sync", w1sem,
                                   BufC[:, pi4 * 4096:(pi4 + 1) * 4096]
                                   .rearrange("p (f k j) -> p f k j", f=4, k=8),
                                   W1_bf[:, gf * 4:(gf + 1) * 4, :, :], waits=wts)
                w1_load(0)
                w1_load(1)
                for f in range(NF):
                    pi = f % 2
                    gf = f // 4
                    if f % 4 == 0 and f > 0 and gf + 1 <= 7:
                        w1_load(gf + 1)
                    w1sem = (swA, swB)[gf % 2]
                    w1of = (gf % 2) * 4096 + (f % 4) * 1024
                    lm = None
                    for k in range(8):
                        wts = []
                        if k == 0:
                            if f % 4 == 0:
                                wts.append((w1sem, tW1g[gf]))
                            if len(relu_hist) >= 2:
                                wts.append((eV, relu_hist[-2]))
                        lm = op("tensor", lambda e, k=k, pi=pi, ts0=ts0, w1of=w1of:
                                e.matmul(
                            pP[pi][:, 0:512],
                            BufC[:, w1of + k * 128: w1of + (k + 1) * 128],
                            fmA[:, k * 2048 + ts0: k * 2048 + ts0 + 512],
                            start=(k == 0), stop=(k == 7)),
                            waits=wts, inc=(eT if k == 7 else None))
                    w1mm_hist.append(lm)
                    wts = [(eT, lm)]
                    if tq > 0 and f == 0:
                        wts.append((eT, w2_last_of_quarter))
                    tRl = op("vector", lambda e, pi=pi, f=f: e.tensor_scalar(
                        actB[:, A_RELU + f * 512: A_RELU + (f + 1) * 512],
                        pP[pi][:, 0:512], b1t[:, f:f + 1], 0.0, ALU.add, ALU.max),
                        waits=wts, inc=eV)
                    relu_hist.append(tRl)
                for d in range(8):
                    pi = d % 2
                    w2sem = (sW2a, sW2b)[pi]
                    wts = [(cw2, cast_w2)]
                    if len(w2mm_hist) >= 2:
                        wts.append((eT, w2mm_hist[-2]))
                    tW2 = dma("sync", w2sem, BufC[:, 8192 + pi * 4096: 8192 + (pi + 1) * 4096]
                              .rearrange("p (k j) -> p k j", k=32),
                              W2_bf[:, d, :, :], waits=wts)
                    lm = None
                    for f in range(NF):
                        wts = []
                        if f == 0:
                            wts = [(w2sem, tW2), (eV, relu_hist[-1])]
                            if len(o2_hist) >= 2:
                                wts.append((eS, o2_hist[-2]))
                        lm = op("tensor", lambda e, f=f, pi=pi:
                                e.matmul(
                            pQ[pi][:, 0:512],
                            BufC[:, 8192 + pi * 4096 + f * 128: 8192 + pi * 4096 + (f + 1) * 128],
                            actB[:, A_RELU + f * 512: A_RELU + (f + 1) * 512],
                            start=(f == 0), stop=(f == NF - 1)),
                            waits=wts, inc=(eT if f == NF - 1 else None))
                    w2mm_hist.append(lm)
                    w2_last_of_quarter = lm
                    o2reg = tmpF[:, 0:512] if pi == 0 else tmpF[:, 512:1024]
                    wts = [(eT, lm)]
                    if len(tr2_hist) >= 2:
                        wts.append((eT, tr2_hist[-2]))
                    tO2 = op("scalar", lambda e, pi=pi, d=d, o2reg=o2reg:
                             e.activation(o2reg, pQ[pi][:, 0:512], AF.Identity,
                                          bias=b2t[:, d:d + 1]),
                             waits=wts, inc=eS)
                    o2_hist.append(tO2)
                    ptr = (p4, p5)[pi]
                    wts = [(eS, tO2)]
                    if len(xu2_hist) >= 2:
                        wts.append((eV, xu2_hist[-2]))
                    tt = None
                    for j in range(4):
                        tt = op("tensor", lambda e, j=j, ptr=ptr, o2reg=o2reg: e.transpose(
                            ptr[:, j * 128:(j + 1) * 128],
                            o2reg[:, j * 128:(j + 1) * 128], ident_f[:, :]),
                            waits=(wts if j == 0 else ()), inc=(eT if j == 3 else None))
                    tr2_hist.append(tt)
                    tx = None
                    for j in range(4):
                        ct = tq * 4 + j
                        tx = op("vector", lambda e, j=j, ptr=ptr, ct=ct, d=d:
                                e.scalar_tensor_tensor(
                            x_tm[:, ct * 1024 + d * 128: ct * 1024 + (d + 1) * 128],
                            ptr[:, j * 128:(j + 1) * 128], 1.0,
                            x_tm[:, ct * 1024 + d * 128: ct * 1024 + (d + 1) * 128],
                            ALU.mult, ALU.add),
                            waits=([(eT, tt)] if j == 0 else ()), inc=eV)
                    xu2_hist.append(tx)
            barrier()

        if stage >= 3:
            ln3_ev = emit_outer_ln(1e-5, pipe=True, pipe_chunks=False,
                                   pipe_stagger=True)
            ffn2_x = {}
            emit_ffn(ln3_ev, xtags=ffn2_x)
            for c in range(NT):
                dma("sync" if c % 2 == 0 else "scalar", s_x,
                    out[c * 128:(c + 1) * 128, :],
                    x_tm[:, c * 1024:(c + 1) * 1024], waits=[ffn2_x[c]])
            barrier()
        else:
            for c in range(NT):
                dma("sync", s_x, out[c * 128:(c + 1) * 128, :],
                    x_tm[:, c * 1024:(c + 1) * 1024])
        op("sync", lambda e: None, waits=[(s_x, s_x.n)])

        with nc.Block() as block:
            @block.sync
            def _(e):
                for f in ops["sync"]:
                    f(e)

            @block.gpsimd
            def _(e):
                for f in ops["gpsimd"]:
                    f(e)

            @block.tensor
            def _(e):
                for f in ops["tensor"]:
                    f(e)

            @block.vector
            def _(e):
                for f in ops["vector"]:
                    f(e)

            @block.scalar
            def _(e):
                for f in ops["scalar"]:
                    f(e)

    return nc


def _omd(omega):
    d = np.zeros((128, 256), np.float32)
    d[0:64, 0:128] = omega.T
    d[64:128, 128:256] = omega.T
    return d


def _prep_inputs(inputs):
    f32 = np.float32
    Wq = np.asarray(inputs["L0_attn_Wq"], f32)
    Wk = np.asarray(inputs["L0_attn_Wk"], f32)
    Wv = np.asarray(inputs["L0_attn_Wv"], f32)
    Wo = np.asarray(inputs["L0_attn_Wo"], f32)
    W1 = np.asarray(inputs["L0_attn_W1"], f32)
    W2 = np.asarray(inputs["L0_attn_W2"], f32)
    b1 = np.asarray(inputs["L0_attn_b1"], f32)
    b2 = np.asarray(inputs["L0_attn_b2"], f32)
    gamma = np.asarray(inputs["L0_attn_gamma"], f32)
    beta = np.asarray(inputs["L0_attn_beta"], f32)
    omega = np.asarray(inputs["L0_attn_omega"], f32)
    wg = np.asarray(inputs["L0_ffn_w_gate"], np.int32)
    wu = np.asarray(inputs["L0_ffn_w_up"], np.int32)
    wd = np.asarray(inputs["L0_ffn_w_down"], np.int32)

    c = np.ascontiguousarray
    shared = {
        "wgT_i": c(wg.T),
        "wuT_i": c(wu.T),
        "wdT_i": c(wd.T),
        "Wq_t": c(Wq.reshape(8, 128, 2, 512).transpose(2, 1, 0, 3)),
        "Wk_t": c(Wk.reshape(8, 128, 2, 512).transpose(2, 1, 0, 3)),
        "Wv_t": c(Wv.reshape(8, 128, 2, 512).transpose(2, 1, 0, 3)),
        "Wo_sw": c(Wo.reshape(8, 128, 1024).transpose(1, 0, 2)),
        "W1_sw": c(W1.reshape(8, 128, 32, 128).transpose(1, 2, 0, 3)),
        "W2_sw": c(W2.reshape(32, 128, 8, 128).transpose(1, 2, 0, 3)),
        "omD": _omd(omega),
        "b1t": c(b1.reshape(32, 128).T),
        "b2t": c(b2.reshape(8, 128).T),
        "gam_bc": c(np.broadcast_to(gamma[None, :], (128, 1024))),
        "gamF": c(gamma.reshape(8, 128).T),
        "betF": c(beta.reshape(8, 128).T),
        "bet_bc": c(np.broadcast_to(beta[None, :], (128, 1024))),
        "ident_bf": np.eye(128, dtype=BFnp),
        "ident_f32": np.eye(128, dtype=f32),
        "maskji": (np.arange(128)[:, None] <= np.arange(128)[None, :]).astype(BFnp),
    }
    return shared


def run_device(inputs, stage=3, core_ids=None):
    shared = _prep_inputs(inputs)
    batch = np.asarray(inputs["L0_attn_batch"], np.float32)
    if core_ids is None:
        core_ids = list(range(8))
    nc = build(stage)
    in_maps = []
    for b in range(len(core_ids)):
        m = dict(shared)
        m["xin"] = np.ascontiguousarray(batch[b])
        in_maps.append(m)
    res = run_bass_kernel_spmd(nc, in_maps, core_ids=core_ids)
    return np.stack([res.results[i]["out"] for i in range(len(core_ids))], axis=0)


def kernel(**inputs):
    return run_device(inputs, stage=3, core_ids=list(range(8)))


# version 64
# speedup vs baseline: 1.2041x; 1.0065x over previous
import contextlib
import numpy as np
import ml_dtypes

import concourse.bass as bass
import concourse.mybir as mybir
from concourse.bass_utils import run_bass_kernel_spmd

F32 = mybir.dt.float32
BF16 = mybir.dt.bfloat16
I32 = mybir.dt.int32
AF = mybir.ActivationFunctionType
ALU = mybir.AluOpType
AX = mybir.AxisListType
BFnp = ml_dtypes.bfloat16

NT = 16
NHC = 24
NF = 32
SCALE_QK = float(64.0 ** -0.25)
INV127 = float(1.0 / 127.0)
HALF127 = float(0.5 / 127.0)
INV_SQM = float(1.0 / np.sqrt(np.float64(128.0)))
EPS_F = 1e-6
ENGINES = ("sync", "gpsimd", "tensor", "vector", "scalar")

A_Q = 0
A_K = 8192
A_V = 16384
A_Y = 0
A_RELU = 0
A_LNTM = 16384
ACTB_W = 24704

C_ATT = 0
C_QKVW = 16384
C_WO = 16384
BUFC_W = 24576


class Sem:
    def __init__(self, h):
        self.h = h
        self.n = 0


def build(stage=3, attn_lvl=6):
    nc = bass.Bass("TRN2", target_bir_lowering=False)

    xin = nc.dram_tensor("xin", [2048, 1024], F32, kind="ExternalInput").ap()
    wgT_i = nc.dram_tensor("wgT_i", [1024, 3072], I32, kind="ExternalInput").ap()
    wuT_i = nc.dram_tensor("wuT_i", [1024, 3072], I32, kind="ExternalInput").ap()
    wdT_i = nc.dram_tensor("wdT_i", [3072, 1024], I32, kind="ExternalInput").ap()
    Wq_t = nc.dram_tensor("Wq_t", [2, 128, 8, 512], F32, kind="ExternalInput").ap()
    Wk_t = nc.dram_tensor("Wk_t", [2, 128, 8, 512], F32, kind="ExternalInput").ap()
    Wv_t = nc.dram_tensor("Wv_t", [2, 128, 8, 512], F32, kind="ExternalInput").ap()
    Wo_sw = nc.dram_tensor("Wo_sw", [128, 8, 1024], F32, kind="ExternalInput").ap()
    W1_sw = nc.dram_tensor("W1_sw", [128, 32, 8, 128], F32, kind="ExternalInput").ap()
    W2_sw = nc.dram_tensor("W2_sw", [128, 8, 32, 128], F32, kind="ExternalInput").ap()
    omTd_in = nc.dram_tensor("omD", [128, 256], F32, kind="ExternalInput").ap()
    b1t_in = nc.dram_tensor("b1t", [128, 32], F32, kind="ExternalInput").ap()
    b2t_in = nc.dram_tensor("b2t", [128, 8], F32, kind="ExternalInput").ap()
    gam_in = nc.dram_tensor("gam_bc", [128, 1024], F32, kind="ExternalInput").ap()
    gamF_in = nc.dram_tensor("gamF", [128, 8], F32, kind="ExternalInput").ap()
    betF_in = nc.dram_tensor("betF", [128, 8], F32, kind="ExternalInput").ap()
    bet_in = nc.dram_tensor("bet_bc", [128, 1024], F32, kind="ExternalInput").ap()
    idb_in = nc.dram_tensor("ident_bf", [128, 128], BF16, kind="ExternalInput").ap()
    idf_in = nc.dram_tensor("ident_f32", [128, 128], F32, kind="ExternalInput").ap()
    msk_in = nc.dram_tensor("maskji", [128, 128], BF16, kind="ExternalInput").ap()
    out = nc.dram_tensor("out", [2048, 1024], F32, kind="ExternalOutput").ap()

    wgT_bf = nc.dram_tensor("wgT_bf", [1024, 3072], BF16, kind="Internal").ap()
    wuT_bf = nc.dram_tensor("wuT_bf", [1024, 3072], BF16, kind="Internal").ap()
    wdT_bf = nc.dram_tensor("wdT_bf", [3072, 1024], BF16, kind="Internal").ap()
    W1_bf = nc.dram_tensor("W1_bf", [128, 32, 8, 128], BF16, kind="Internal").ap()
    W2_bf = nc.dram_tensor("W2_bf", [128, 8, 32, 128], BF16, kind="Internal").ap()
    Wq_bf = nc.dram_tensor("Wq_bf", [2, 128, 8, 512], BF16, kind="Internal").ap()
    Wk_bf = nc.dram_tensor("Wk_bf", [2, 128, 8, 512], BF16, kind="Internal").ap()
    Wv_bf = nc.dram_tensor("Wv_bf", [2, 128, 8, 512], BF16, kind="Internal").ap()

    es = contextlib.ExitStack()
    with es:
        x_tm = es.enter_context(nc.sbuf_tensor("x_tm", [128, NT * 1024], F32))
        fmA = es.enter_context(nc.sbuf_tensor("fmA", [128, 8 * 2048], BF16))
        actB = es.enter_context(nc.sbuf_tensor("actB", [128, ACTB_W], BF16))
        BufC = es.enter_context(nc.sbuf_tensor("BufC", [128, BUFC_W], BF16))
        tmpF = es.enter_context(nc.sbuf_tensor("tmpF", [128, 1024], F32))
        gam_bf = es.enter_context(nc.sbuf_tensor("gam_bf", [128, 1024], BF16))
        gamF_s = es.enter_context(nc.sbuf_tensor("gamF_s", [128, 8], F32))
        betF_s = es.enter_context(nc.sbuf_tensor("betF_s", [128, 8], F32))
        bet_bf = es.enter_context(nc.sbuf_tensor("bet_bf", [128, 1024], BF16))
        Sf_t = es.enter_context(nc.sbuf_tensor("Sf_t", [128, 528], F32))
        amT = es.enter_context(nc.sbuf_tensor("amT", [128, 512], BF16))
        sbT = es.enter_context(nc.sbuf_tensor("sbT", [128, 1056], BF16))
        den_t = es.enter_context(nc.sbuf_tensor("den_t", [128, 8], F32))
        rec_t = es.enter_context(nc.sbuf_tensor("rec_t", [128, 8], F32))
        musum = es.enter_context(nc.sbuf_tensor("musum", [128, 16], F32))
        lneps = es.enter_context(nc.sbuf_tensor("lneps", [128, 1], F32))
        mu_c = es.enter_context(nc.sbuf_tensor("mu_c", [128, 16], F32))
        varsum = es.enter_context(nc.sbuf_tensor("varsum", [128, 16], F32))
        rstd_c = es.enter_context(nc.sbuf_tensor("rstd_c", [128, 16], F32))
        mu2 = es.enter_context(nc.sbuf_tensor("mu2", [128, 16], F32))
        rstd2 = es.enter_context(nc.sbuf_tensor("rstd2", [128, 16], F32))
        m2r2 = es.enter_context(nc.sbuf_tensor("m2r2", [128, 16], F32))
        st_nm = es.enter_context(nc.sbuf_tensor("st_nm", [128, 16], F32))
        omT_bf = es.enter_context(nc.sbuf_tensor("omT_bf", [128, 256], BF16))
        ident_b = es.enter_context(nc.sbuf_tensor("ident_b", [128, 128], BF16))
        ident_f = es.enter_context(nc.sbuf_tensor("ident_f", [128, 128], F32))
        maskji = es.enter_context(nc.sbuf_tensor("maskji_s", [128, 128], BF16))
        b1t = es.enter_context(nc.sbuf_tensor("b1t_s", [128, 32], F32))
        b2t = es.enter_context(nc.sbuf_tensor("b2t_s", [128, 8], F32))

        p0 = es.enter_context(nc.psum_tensor("p0", [128, 512], F32))
        p1 = es.enter_context(nc.psum_tensor("p1", [128, 512], F32))
        p2 = es.enter_context(nc.psum_tensor("p2", [128, 512], F32))
        p3 = es.enter_context(nc.psum_tensor("p3", [128, 512], F32))
        p4 = es.enter_context(nc.psum_tensor("p4", [128, 512], F32))
        p5 = es.enter_context(nc.psum_tensor("p5", [128, 512], F32))
        p6a = es.enter_context(nc.psum_tensor("p6a", [128, 1024], BF16))
        p6b = es.enter_context(nc.psum_tensor("p6b", [128, 1024], BF16))
        pP = (p0, p1)
        pQ = (p2, p3)
        p6 = (p6a, p6b)
        pOb = p6b.bitcast(F32)

        def new_sem(name):
            return Sem(es.enter_context(nc.semaphore(name)))
        s_x = new_sem("s_x")
        s_xc = [s_x] + [new_sem(f"s_x{k}") for k in (1, 2, 3)]
        s_c = new_sem("s_c")
        cwg = [new_sem(f"cwg{q}") for q in range(4)]
        cwu = [new_sem(f"cwu{q}") for q in range(4)]
        cwd = new_sem("cwd")
        cw1 = new_sem("cw1")
        cw2 = new_sem("cw2")
        s_gam = new_sem("s_gam")
        cwuh = [new_sem(f"cwuh{i}") for i in range(4)]
        swA = new_sem("swA")
        swB = new_sem("swB")
        s_wd = new_sem("s_wd")
        sW2a = new_sem("sW2a")
        sW2b = new_sem("sW2b")
        qkvA = new_sem("qkvA")
        qkvB = new_sem("qkvB")
        s_wo = new_sem("s_wo")
        sync_dsems = [s_x, s_c, swA, swB, s_wd, sW2a, sW2b, qkvB] + s_xc[1:]
        gp_dsems = cwg + cwu + cwuh + [cwd, cw1, cw2, s_gam, qkvA, s_wo]
        eT = Sem(es.enter_context(nc.semaphore("eT")))
        eV = Sem(es.enter_context(nc.semaphore("eV")))
        eS = Sem(es.enter_context(nc.semaphore("eS")))
        mSY = Sem(es.enter_context(nc.semaphore("mSY")))
        mGP = Sem(es.enter_context(nc.semaphore("mGP")))
        eG = Sem(es.enter_context(nc.semaphore("eG")))

        ops = {e: [] for e in ENGINES}

        def op(eng, fn, waits=(), inc=None, inc_by=1):
            tag = None
            if inc is not None:
                inc.n += inc_by
                tag = inc.n
            wl = tuple((s, v) for (s, v) in waits if v and v > 0)

            def run(e, fn=fn, wl=wl, inc=inc, inc_by=inc_by):
                for (s, v) in wl:
                    e.wait_ge(s.h, v)
                r = fn(e)
                if inc is not None:
                    r.then_inc(inc.h, inc_by)
            ops[eng].append(run)
            return tag

        def dma(q, sem, out_ap, in_ap, waits=(), **kw):
            return op(q, lambda e, o=out_ap, i=in_ap, kw=dict(kw): e.dma_start(o, i, **kw),
                      waits, inc=sem, inc_by=16)

        def barrier():
            t1 = op("sync", lambda e: e.nop(),
                    waits=[(sm, sm.n) for sm in sync_dsems], inc=mSY)
            t2 = op("gpsimd", lambda e: e.nop(),
                    waits=[(sm, sm.n) for sm in gp_dsems] + [(eG, eG.n)], inc=mGP)
            t3 = op("tensor", lambda e: e.nop(), waits=[(eT, eT.n)], inc=eT)
            t4 = op("vector", lambda e: e.nop(), waits=[(eV, eV.n)], inc=eV)
            t5 = op("scalar", lambda e: e.nop(), waits=[(eS, eS.n)], inc=eS)
            wl = [(mSY, t1), (mGP, t2), (eT, t3), (eV, t4), (eS, t5)]
            for eng in ENGINES:
                op(eng, lambda e: None, waits=wl)

        t_xchunk = {}
        for c in range(NT):
            t = dma("sync" if c % 2 == 0 else "scalar", s_xc[c // 4],
                    x_tm[:, c * 1024:(c + 1) * 1024],
                    xin[c * 128:(c + 1) * 128, :])
            t_xchunk[c // 4] = t
        t_xall = s_x.n
        dma("sync", s_c, ident_b[:, :], idb_in)
        dma("sync", s_c, ident_f[:, :], idf_in)
        dma("sync", s_c, maskji[:, :], msk_in)
        dma("sync", s_c, b1t[:, :], b1t_in)
        dma("sync", s_c, b2t[:, :], b2t_in)
        t_call = s_c.n

        cast_wd_box = [None]

        lneps_tag = [0]

        def emit_outer_ln(eps, mu_store=None, rstd_store=None, src_wait=None,
                          src_tags=None, xl_wait=None, pipe=False,
                          pipe_chunks=True, pipe_stagger=False):
            mu_d = mu_store if mu_store is not None else mu_c
            rs_d = rstd_store if rstd_store is not None else rstd_c
            tRSd = {}
            if pipe:
                t_le = op("vector", lambda e, eps=float(eps):
                          e.memset(lneps[:, :], eps), inc=eV)
                lneps_tag[0] = t_le
                dump2 = actB[:, A_LNTM + 2048:A_LNTM + 3072]

                def emit_pstats(c):
                    w = []
                    if pipe_chunks and c % 4 == 0:
                        w = [(s_xc[c // 4], t_xchunk[c // 4])]
                    tR = op("vector", lambda e, c=c: e.reduce_sum(
                        musum[:, c:c + 1], x_tm[:, c * 1024:(c + 1) * 1024],
                        axis=AX.X), waits=w, inc=eV)
                    tMU = op("vector", lambda e, c=c, mu=mu_d: e.tensor_scalar(
                        mu[:, c:c + 1], musum[:, c:c + 1], 1.0 / 1024.0, None,
                        ALU.mult, ALU.bypass), waits=[(eV, tR)], inc=eV)
                    tSQ = op("scalar", lambda e, c=c, mu=mu_d, dump2=dump2:
                             e.activation(
                        dump2, x_tm[:, c * 1024:(c + 1) * 1024], AF.Square,
                        bias=mu[:, c:c + 1], scale=-1.0,
                        accum_out=varsum[:, c:c + 1]),
                        waits=[(eV, tMU)], inc=eS)
                    wl = [(eS, tSQ)] + ([(eV, t_le)] if c == 0 else [])
                    tLN = op("scalar", lambda e, c=c: e.activation(
                        varsum[:, c:c + 1], varsum[:, c:c + 1], AF.Ln,
                        bias=lneps[:, 0:1], scale=1.0 / 1024.0),
                        waits=wl, inc=eS)
                    tRSd[c] = op("scalar", lambda e, c=c, rs=rs_d: e.activation(
                        rs[:, c:c + 1], varsum[:, c:c + 1], AF.Exp, scale=-0.5),
                        waits=[(eS, tLN)], inc=eS)
            else:
                tR = None
                for c in range(NT):
                    if src_tags is not None:
                        w = [src_tags[c]]
                    else:
                        w = [src_wait] if (src_wait and c == 0) else []
                    tR = op("vector", lambda e, c=c: e.reduce_sum(
                        musum[:, c:c + 1], x_tm[:, c * 1024:(c + 1) * 1024], axis=AX.X),
                        waits=w, inc=eV)
                tMU = op("vector", lambda e, mu=mu_d: e.tensor_scalar(
                    mu[:, :], musum[:, :], 1.0 / 1024.0, None, ALU.mult, ALU.bypass),
                    waits=[(eV, tR)], inc=eV)
                tVS = None
                for c in range(NT):
                    w = [(eV, tMU)] if c == 0 else [(eS, tVS)]
                    tVS = op("scalar", lambda e, c=c, mu=mu_d: e.activation(
                        actB[:, A_LNTM:A_LNTM + 1024], x_tm[:, c * 1024:(c + 1) * 1024], AF.Square,
                        bias=mu[:, c:c + 1], scale=-1.0, accum_out=varsum[:, c:c + 1]),
                        waits=w, inc=eS)
                tSV = op("vector", lambda e, eps=float(eps): e.tensor_scalar(
                    varsum[:, :], varsum[:, :], 1.0 / 1024.0, eps, ALU.mult, ALU.add),
                    waits=[(eS, tVS)], inc=eV)
                tLN = op("scalar", lambda e: e.activation(varsum[:, :], varsum[:, :], AF.Ln),
                         waits=[(eV, tSV)], inc=eS)
                tRS = op("scalar", lambda e, rs=rs_d: e.activation(
                    rs[:, :], varsum[:, :], AF.Exp, scale=-0.5),
                    waits=[(eS, tLN)], inc=eS)
                for c in range(NT):
                    tRSd[c] = tRS
                if mu_store is not None:
                    op("vector", lambda e, mu=mu_d, rs=rs_d: e.scalar_tensor_tensor(
                        m2r2[:, :], mu[:, :], -1.0, rs[:, :], ALU.mult, ALU.mult),
                        waits=[(eS, tRS)], inc=eV)
            ev_tag, xl_tag, tr_tag = {}, {}, {}

            def emit_pxform(c):
                pi = c % 2
                lntm = actB[:, A_LNTM + pi * 1024: A_LNTM + (pi + 1) * 1024]
                w = [(eS, tRSd[c])]
                if xl_wait is not None and c == 0:
                    w.append(xl_wait)
                if c >= 2:
                    w.append((eT, tr_tag[c - 2]))
                xl_tag[c] = op("vector", lambda e, c=c, lntm=lntm, mu=mu_d, rs=rs_d:
                               e.tensor_scalar(lntm, x_tm[:, c * 1024:(c + 1) * 1024],
                                               mu[:, c:c + 1], rs[:, c:c + 1],
                                               ALU.subtract, ALU.mult),
                               waits=w, inc=eV)
                w2 = [(eV, xl_tag[c]), (s_c, t_call)]
                if c >= 2:
                    w2.append(ev_tag[c - 2])
                tt = None
                for j in range(8):
                    tt = op("tensor", lambda e, j=j, pi=pi, lntm=lntm: e.transpose(
                        p6[pi][:, j * 128:(j + 1) * 128], lntm[:, j * 128:(j + 1) * 128],
                        ident_b[:, :]),
                        waits=(w2 if j == 0 else ()), inc=(eT if j == 7 else None))
                tr_tag[c] = tt

            def emit_pev(c):
                pi = c % 2
                dst = fmA[:, :].rearrange("p (k t) -> p k t", k=8)[:, :, c * 128:(c + 1) * 128]
                src = p6[pi][:, :].rearrange("p (k t) -> p k t", k=8)
                if pipe:
                    ev_tag[c] = (eS, op("scalar", lambda e, dst=dst, src=src:
                                        e.copy(dst, src),
                                        waits=[(eT, tr_tag[c])], inc=eS))
                else:
                    ev_tag[c] = (eV, op("vector", lambda e, dst=dst, src=src:
                                        e.tensor_copy(dst, src),
                                        waits=[(eT, tr_tag[c])], inc=eV))

            if pipe and pipe_stagger:
                for c in range(NT + 2):
                    if c < NT:
                        emit_pstats(c)
                    if c >= 2:
                        emit_pxform(c - 2)
                    if c >= 3:
                        emit_pev(c - 3)
                emit_pev(NT - 1)
            elif pipe:
                for c in range(NT):
                    emit_pstats(c)
                for c in range(NT):
                    emit_pxform(c)
                    emit_pev(c)
            else:
                for c in range(NT):
                    emit_pxform(c)
                    emit_pev(c)
            return ev_tag

        ffn_last = {"mm": 0, "evac": 0}

        def emit_ffn(ln_ev, xtags=None, stats_cb=None):
            pg = (p0, p1)
            pu = (p2, p3)
            n = [0]
            tagSilu, tagAct, tagG, tagU = {}, {}, {}, {}
            last_down_mm = [0]
            last_down_evac = [0]
            q_last_mm = {}
            for th in range(2):
                tt0 = th * 1024
                for q in range(4):
                    pq = q % 2
                    wg_off = pq * 12288
                    wu_off = pq * 12288 + 6144
                    swq = (cwg[0], cwg[1])[q % 2]
                    w = []
                    gq = th * 4 + q
                    if gq >= 2:
                        w.append((eT, q_last_mm[gq - 2]))
                    if th == 1 and q < 2:
                        w.append((eT, last_down_mm[0]))
                    src = wgT_i.rearrange("(k p) h -> p k h", p=128)[:, :, q * 768:(q + 1) * 768]
                    semh1 = (cwg[0], cwg[1])[q % 2]
                    semh2 = (cwg[2], cwg[3])[q % 2]
                    semu1 = (cwuh[0], cwuh[1])[q % 2]
                    semu2 = (cwuh[2], cwuh[3])[q % 2]
                    srcu = wuT_i.rearrange("(k p) h -> p k h", p=128)[:, :, q * 768:(q + 1) * 768]
                    tH1 = dma("gpsimd", semh1, BufC[:, wg_off:wg_off + 3072]
                              .rearrange("p (k h) -> p k h", k=4), src[:, 0:4, :],
                              waits=w)
                    tU1 = dma("gpsimd", semu1, BufC[:, wu_off:wu_off + 3072]
                              .rearrange("p (k h) -> p k h", k=4), srcu[:, 0:4, :])
                    tH2 = dma("gpsimd", semh2, BufC[:, wg_off + 3072:wg_off + 6144]
                              .rearrange("p (k h) -> p k h", k=4), src[:, 4:8, :])
                    tU2 = dma("gpsimd", semu2, BufC[:, wu_off + 3072:wu_off + 6144]
                              .rearrange("p (k h) -> p k h", k=4), srcu[:, 4:8, :])
                    if gq == 1 and cast_wd_box[0] is None:
                        cast_wd_box[0] = dma("gpsimd", cwd, wdT_bf[:, :],
                                             wdT_i[:, :], max_dma_last_dim=8192)
                    for hcl in range(6):
                        hc = q * 6 + hcl
                        for tsl in range(2):
                            i = n[0]
                            pi = i % 2
                            tg = None
                            for k in range(8):
                                wts = []
                                if k == 0:
                                    if hcl == 0:
                                        wts = [(semh1, tH1),
                                               ln_ev[th * 8 + tsl * 4 + 3]]
                                    if i >= 2:
                                        wts.append((eS, tagSilu[i - 2]))
                                    if th == 1 and i == 48:
                                        wts.append((eV, last_down_evac[0]))
                                elif k == 4 and hcl == 0 and tsl == 0:
                                    wts = [(semh2, tH2)]
                                tg = op("tensor", lambda e, k=k, pi=pi, wg_off=wg_off,
                                        hcl=hcl, tt0=tt0, tsl=tsl:
                                        e.matmul(pg[pi][:, 0:512],
                                                 BufC[:, wg_off + k * 768 + hcl * 128:
                                                      wg_off + k * 768 + (hcl + 1) * 128],
                                                 fmA[:, k * 2048 + tt0 + tsl * 512:
                                                     k * 2048 + tt0 + (tsl + 1) * 512],
                                                 start=(k == 0), stop=(k == 7)),
                                        waits=wts, inc=(eT if k == 7 else None))
                            tagG[i] = tg
                            tu = None
                            for k in range(8):
                                wts = []
                                if k == 0:
                                    if hcl == 0:
                                        wts = [(semu1, tU1)]
                                    if i >= 2:
                                        wts.append((eV, tagAct[i - 2]))
                                elif k == 4 and hcl == 0 and tsl == 0:
                                    wts = [(semu2, tU2)]
                                tu = op("tensor", lambda e, k=k, pi=pi, wu_off=wu_off,
                                        hcl=hcl, tt0=tt0, tsl=tsl:
                                        e.matmul(pu[pi][:, 0:512],
                                                 BufC[:, wu_off + k * 768 + hcl * 128:
                                                      wu_off + k * 768 + (hcl + 1) * 128],
                                                 fmA[:, k * 2048 + tt0 + tsl * 512:
                                                     k * 2048 + tt0 + (tsl + 1) * 512],
                                                 start=(k == 0), stop=(k == 7)),
                                        waits=wts, inc=(eT if k == 7 else None))
                            tagU[i] = tu
                            wts = [(eT, tagG[i])]
                            if i >= 2:
                                wts.append((eV, tagAct[i - 2]))
                            tagSilu[i] = op("scalar", lambda e, pi=pi: e.activation(
                                tmpF[:, pi * 512:(pi + 1) * 512], pg[pi][:, 0:512],
                                AF.Silu, scale=INV127),
                                waits=wts, inc=eS)
                            tagAct[i] = op("vector", lambda e, pi=pi, hc=hc, tsl=tsl:
                                           e.scalar_tensor_tensor(
                                actB[:, hc * 1024 + tsl * 512: hc * 1024 + (tsl + 1) * 512],
                                pu[pi][:, 0:512], INV127,
                                tmpF[:, pi * 512:(pi + 1) * 512], ALU.mult, ALU.mult),
                                waits=[(eT, tagU[i]), (eS, tagSilu[i])], inc=eV)
                            n[0] += 1
                    q_last_mm[gq] = tagU[n[0] - 1]
                dma("sync", s_wd, BufC[:, 0:12288].rearrange("p (hc d) -> p hc d", hc=12),
                    wdT_bf.rearrange("(hc p) d -> p hc d", p=128)[:, 0:12, :],
                    waits=[(cwd, cast_wd_box[0]), (eT, q_last_mm[th * 4 + 2])])
                tWdh = s_wd.n
                tWd1 = dma("sync", cwu[0], BufC[:, 12288:16384]
                           .rearrange("p (hc d) -> p hc d", hc=4),
                           wdT_bf.rearrange("(hc p) d -> p hc d", p=128)[:, 12:16, :],
                           waits=[(eT, q_last_mm[th * 4 + 3])])
                tWd2 = dma("sync", cwu[1], BufC[:, 16384:20480]
                           .rearrange("p (hc d) -> p hc d", hc=4),
                           wdT_bf.rearrange("(hc p) d -> p hc d", p=128)[:, 16:20, :])
                tWd3 = dma("sync", cwu[2], BufC[:, 20480:24576]
                           .rearrange("p (hc d) -> p hc d", hc=4),
                           wdT_bf.rearrange("(hc p) d -> p hc d", p=128)[:, 20:24, :])
                tXU = {}
                for tl in range(8):
                    c = th * 8 + tl
                    da, db = ((p4, p5), (p0, p1))[tl % 2]
                    td = None
                    for hc in range(NHC):
                        wts = []
                        if hc == 0:
                            if tl == 0:
                                wts = [(s_wd, tWdh), (eV, tagAct[n[0] - 1])]
                            if tl >= 2:
                                wts.append((eV, tXU[tl - 2]))
                        elif tl == 0 and hc == 12:
                            wts = [(cwu[0], tWd1)]
                        elif tl == 0 and hc == 16:
                            wts = [(cwu[1], tWd2)]
                        elif tl == 0 and hc == 20:
                            wts = [(cwu[2], tWd3)]
                        op("tensor", lambda e, hc=hc, tl=tl, da=da:
                           e.matmul(da[:, 0:512],
                                    actB[:, hc * 1024 + tl * 128: hc * 1024 + (tl + 1) * 128],
                                    BufC[:, hc * 1024: hc * 1024 + 512],
                                    start=(hc == 0), stop=(hc == NHC - 1)),
                           waits=wts, inc=None)
                        td = op("tensor", lambda e, hc=hc, tl=tl, db=db:
                                e.matmul(db[:, 0:512],
                                         actB[:, hc * 1024 + tl * 128: hc * 1024 + (tl + 1) * 128],
                                         BufC[:, hc * 1024 + 512: hc * 1024 + 1024],
                                         start=(hc == 0), stop=(hc == NHC - 1)),
                                waits=(), inc=(eT if hc == NHC - 1 else None))
                    op("vector", lambda e, c=c, da=da: e.scalar_tensor_tensor(
                        x_tm[:, c * 1024: c * 1024 + 512], da[:, 0:512], HALF127,
                        x_tm[:, c * 1024: c * 1024 + 512], ALU.mult, ALU.add),
                        waits=[(eT, td)], inc=eV)
                    tXU[tl] = op("vector", lambda e, c=c, db=db: e.scalar_tensor_tensor(
                        x_tm[:, c * 1024 + 512: (c + 1) * 1024], db[:, 0:512], HALF127,
                        x_tm[:, c * 1024 + 512: (c + 1) * 1024], ALU.mult, ALU.add),
                        waits=(), inc=eV)
                    last_down_mm[0] = td
                    last_down_evac[0] = tXU[tl]
                    ffn_last["mm"] = td
                    ffn_last["evac"] = tXU[tl]
                    if xtags is not None:
                        xtags[c] = (eV, tXU[tl])
                    if stats_cb is not None:
                        stats_cb(c)

        ln1_ev = emit_outer_ln(1e-5, pipe=True)
        ffn1_x = {}
        ln2_sqt = {}

        def ln2_stats_cb(c):
            tRc = op("vector", lambda e, c=c: e.reduce_sum(
                musum[:, c:c + 1], x_tm[:, c * 1024:(c + 1) * 1024], axis=AX.X),
                waits=[ffn1_x[c]], inc=eV)
            tMUc = op("vector", lambda e, c=c: e.tensor_scalar(
                mu2[:, c:c + 1], musum[:, c:c + 1], 1.0 / 1024.0, None,
                ALU.mult, ALU.bypass), waits=[(eV, tRc)], inc=eV)
            tSQc = op("scalar", lambda e, c=c: e.activation(
                tmpF[:, 0:1024], x_tm[:, c * 1024:(c + 1) * 1024], AF.Square,
                bias=mu2[:, c:c + 1], scale=-1.0, accum_out=varsum[:, c:c + 1]),
                waits=[(eV, tMUc)], inc=eS)
            wl = [(eS, tSQc)] + ([(eV, lneps_tag[0])] if c == 0 else [])
            tLNc = op("scalar", lambda e, c=c: e.activation(
                varsum[:, c:c + 1], varsum[:, c:c + 1], AF.Ln,
                bias=lneps[:, 0:1], scale=1.0 / 1024.0), waits=wl, inc=eS)
            ln2_sqt[c] = op("scalar", lambda e, c=c: e.activation(
                rstd2[:, c:c + 1], varsum[:, c:c + 1], AF.Exp, scale=-0.5),
                waits=[(eS, tLNc)], inc=eS)
            op("gpsimd", lambda e, c=c: e.tensor_scalar(
                m2r2[:, c:c + 1], mu2[:, c:c + 1], -1.0, rstd2[:, c:c + 1],
                ALU.mult, ALU.mult), waits=[(eS, ln2_sqt[c])], inc=eG)

        emit_ffn(ln1_ev, xtags=ffn1_x,
                 stats_cb=(ln2_stats_cb if stage >= 2 else None))
        cast_wq = dma("gpsimd", qkvA, Wq_bf, Wq_t, max_dma_last_dim=8192)
        cast_wk = dma("gpsimd", qkvA, Wk_bf, Wk_t, max_dma_last_dim=8192)
        cast_wv = dma("gpsimd", qkvA, Wv_bf, Wv_t, max_dma_last_dim=8192)
        t_qkvcast = qkvA.n
        dma("gpsimd", s_gam, gamF_s[:, :], gamF_in)
        dma("gpsimd", s_gam, betF_s[:, :], betF_in)
        dma("gpsimd", s_gam, gam_bf[:, :], gam_in, max_dma_last_dim=8192)
        dma("gpsimd", s_gam, bet_bf[:, :], bet_in, max_dma_last_dim=8192)
        dma("gpsimd", s_gam, omT_bf[:, :], omTd_in, max_dma_last_dim=8192)
        t_gamall = s_gam.n
        if stage < 2:
            barrier()

        if stage >= 2:
            ffn1_mm_tag = ffn_last["mm"]
            ln2_ev, xl_tag2, tr_tag2 = {}, {}, {}
            for c in range(NT):
                pi = c % 2
                lntm = actB[:, A_LNTM + pi * 1024: A_LNTM + (pi + 1) * 1024]
                w = [(eS, ln2_sqt[c])]
                if c <= 1:
                    w.append((eT, ffn1_mm_tag))
                if c >= 2:
                    w.append((eT, tr_tag2[c - 2]))
                xl_tag2[c] = op("vector", lambda e, c=c, lntm=lntm:
                                e.tensor_scalar(lntm, x_tm[:, c * 1024:(c + 1) * 1024],
                                                mu2[:, c:c + 1], rstd2[:, c:c + 1],
                                                ALU.subtract, ALU.mult),
                                waits=w, inc=eV)
                w2 = [(eV, xl_tag2[c])]
                if c >= 2:
                    w2.append(ln2_ev[c - 2])
                tt = None
                for j in range(8):
                    tt = op("tensor", lambda e, j=j, pi=pi, lntm=lntm: e.transpose(
                        p6[pi][:, j * 128:(j + 1) * 128], lntm[:, j * 128:(j + 1) * 128],
                        ident_b[:, :]), waits=(w2 if j == 0 else ()),
                        inc=(eT if j == 7 else None))
                tr_tag2[c] = tt
                dst = fmA[:, :].rearrange("p (k t) -> p k t", k=8)[:, :, c * 128:(c + 1) * 128]
                src = p6[pi][:, :].rearrange("p (k t) -> p k t", k=8)
                ln2_ev[c] = (eV, op("vector", lambda e, dst=dst, src=src:
                                    e.tensor_copy(dst, src), waits=[(eT, tt)], inc=eV))
            t_ln2 = ln2_ev[NT - 1][1]

            def ln2_ev_w(c):
                return ln2_ev[c]
            cast_w1 = dma("gpsimd", cw1, W1_bf[:, :, :, :], W1_sw,
                          waits=[(eV, t_ln2)], max_dma_last_dim=8192)
            cast_w2 = dma("gpsimd", cw2, W2_bf[:, :, :, :], W2_sw,
                          waits=[(eV, t_ln2)], max_dma_last_dim=8192)

            QGW = 12352

            def a_q(pg):
                return pg * QGW

            def a_k(pg):
                return pg * QGW + 4096

            def a_v(pg):
                return pg * QGW + 8192

            C_SLAB = 16384
            T_TM = 20480
            T_FM = 22528

            t_ones = None
            for pg in range(2):
                ap1 = bass.AP(actB, a_v(pg) + 64, [[ACTB_W, 128], [65, 64]])
                t_ones = op("vector", lambda e, ap1=ap1: e.memset(ap1, 1.0), inc=eV)

            slab_lastmm = {}
            slab_cnt = [0]
            qkv_rot_hist = []
            qkv_rot_cnt = [0]
            qk_done = {}
            v_done = {}

            def qkv_gen(qg):
                g, qq, pg = qg // 2, qg % 2, qg % 2
                for wi, (Wsrc, isV) in enumerate(((Wq_bf, 0), (Wk_bf, 0), (Wv_bf, 1))):
                    pw = slab_cnt[0] % 2
                    slab_cnt[0] += 1
                    qsem = qkvB
                    wts = [(qsem, qsem.n), (qkvA, t_qkvcast)]
                    if pw in slab_lastmm:
                        wts.append((eT, slab_lastmm[pw]))
                    else:
                        wts.append((eT, ffn1_mm_tag))
                    tW = dma("sync", qsem, BufC[:, C_SLAB + pw * 2048:
                                                C_SLAB + (pw + 1) * 2048]
                             .rearrange("p (k j) -> p k j", k=8),
                             Wsrc[g, :, :, qq * 256:(qq + 1) * 256],
                             waits=wts)
                    lm = None
                    if not isV:
                        dst0 = a_q(pg) if wi == 0 else a_k(pg)
                        for mloc in range(2):
                            for ts in range(4):
                                pr = p4
                                qkv_rot_cnt[0] += 1
                                for k in range(8):
                                    wts = []
                                    if k == 0:
                                        wts = [ln2_ev_w(ts * 4 + 3)]
                                        if mloc == 0 and ts == 0:
                                            wts.append((qsem, tW))
                                        if len(qkv_rot_hist) >= 1:
                                            wts.append(qkv_rot_hist[-1])
                                    lm = op("tensor", lambda e, k=k, mloc=mloc, ts=ts,
                                            pr=pr, pw=pw:
                                            e.matmul(pr[:, 0:512],
                                                     BufC[:, C_SLAB + pw * 2048 + k * 256
                                                          + mloc * 128:
                                                          C_SLAB + pw * 2048 + k * 256
                                                          + (mloc + 1) * 128],
                                                     fmA[:, k * 2048 + ts * 512:
                                                         k * 2048 + (ts + 1) * 512],
                                                     start=(k == 0), stop=(k == 7)),
                                            waits=wts, inc=(eT if k == 7 else None))
                                yield
                                tE = op("scalar", lambda e, pr=pr, mloc=mloc, ts=ts,
                                        dst0=dst0: e.activation(
                                    actB[:, dst0 + mloc * 2048 + ts * 512:
                                         dst0 + mloc * 2048 + (ts + 1) * 512],
                                    pr[:, 0:512], AF.Identity, scale=SCALE_QK),
                                    waits=[(eT, lm)], inc=eS)
                                qkv_rot_hist.append((eS, tE))
                                qk_done[qg] = tE
                                yield
                        slab_lastmm[pw] = lm
                    else:
                        for c16 in range(16):
                            pr = p4
                            po = (c16 % 2) * 256
                            qkv_rot_cnt[0] += 1
                            for k in range(8):
                                wts = []
                                if k == 0:
                                    wts = [ln2_ev_w(c16)]
                                    if c16 == 0:
                                        wts.append((qsem, tW))
                                    if c16 % 2 == 0 and len(qkv_rot_hist) >= 1:
                                        wts.append(qkv_rot_hist[-1])
                                lm = op("tensor", lambda e, k=k, c16=c16, pr=pr,
                                        pw=pw, po=po:
                                        e.matmul(pr[:, po:po + 256],
                                                 fmA[:, k * 2048 + c16 * 128:
                                                     k * 2048 + (c16 + 1) * 128],
                                                 BufC[:, C_SLAB + pw * 2048 + k * 256:
                                                      C_SLAB + pw * 2048 + (k + 1) * 256],
                                                 start=(k == 0), stop=(k == 7)),
                                        waits=wts, inc=(eT if k == 7 else None))
                            yield
                            if c16 % 2 == 1:
                                dst = bass.AP(actB, a_v(pg) + (c16 - 1) * 260,
                                              [[ACTB_W, 128], [65, 8], [1, 64]])
                                srcv = pr[:, 0:512].rearrange(
                                    "p (hh w) -> p hh w", w=64)
                                wv = [(eT, lm), (eV, t_ones)]
                                tE = op("vector", lambda e, dst=dst, srcv=srcv:
                                        e.tensor_copy(dst, srcv), waits=wv, inc=eV)
                                qkv_rot_hist.append((eV, tE))
                                v_done[qg] = tE
                            yield
                        slab_lastmm[pw] = lm

            NE = 128
            its = [(qg, c, pair) for qg in range(4) for c in range(16)
                   for pair in range(2)]
            tPK, tRM, tEXP, tEPS, tTR, tD, tFME = {}, {}, {}, {}, {}, {}, {}
            tSA, tSBE, tA, tAM, tO, tSCL = {}, {}, {}, {}, {}, {}
            last_projk = {}

            def st_proj(t):
                qg, c, pair = its[t]
                pg = qg % 2
                P = (p0, p1, p2)[t % 3]
                m = pair
                wts = []
                if c == 0:
                    wts = [(eS, qk_done[qg]), (eV, v_done[qg]), (s_gam, t_gamall)]
                if (t - 3) in tEXP:
                    wts.append((eS, tEXP[t - 3]))
                qsl = actB[:, a_q(pg) + m * 2048 + c * 128:
                           a_q(pg) + m * 2048 + (c + 1) * 128]
                op("tensor", lambda e, qsl=qsl, P=P:
                   e.matmul(P[:, 0:256], qsl, omT_bf[:, :], start=True, stop=True),
                   waits=wts, inc=None)
                ksl = actB[:, a_k(pg) + m * 2048 + c * 128:
                           a_k(pg) + m * 2048 + (c + 1) * 128]
                tPK[t] = op("tensor", lambda e, ksl=ksl, P=P:
                            e.matmul(P[:, 256:512], ksl, omT_bf[:, :],
                                     start=True, stop=True), waits=(), inc=eT)
                last_projk[qg] = tPK[t]

            def st_rm(t):
                P = (p0, p1, p2)[t % 3]
                wts = [(eT, tPK[t])]
                if (t - 2) in tEXP:
                    wts.append((eS, tEXP[t - 2]))
                tRM[t] = op("vector", lambda e, P=P, t=t: e.tensor_reduce(
                    st_nm[:, (t % 2) * 4: (t % 2) * 4 + 4],
                    P[:, 0:512].rearrange("p (a b) -> p a b", a=4),
                    axis=AX.X, op=ALU.max, negate=True), waits=wts, inc=eV)

            def st_exp(t):
                P = (p0, p1, p2)[t % 3]
                tmo = T_TM + (t % 4) * 512
                tE = None
                for j in range(4):
                    wts = []
                    if j == 0:
                        wts = [(eV, tRM[t])]
                        if (t - 4) in tD:
                            wts.append((eT, tD[t - 4]))
                    tE = op("scalar", lambda e, P=P, tmo=tmo, j=j, t=t:
                            e.activation(
                        BufC[:, tmo + j * 128: tmo + (j + 1) * 128],
                        P[:, j * 128:(j + 1) * 128], AF.Exp,
                        bias=st_nm[:, (t % 2) * 4 + j: (t % 2) * 4 + j + 1]),
                        waits=wts, inc=(eS if j == 3 else None))
                tEXP[t] = tE

            def st_eps(t):
                tmo = T_TM + (t % 4) * 512
                tEPS[t] = op("vector", lambda e, tmo=tmo: e.tensor_scalar(
                    BufC[:, tmo: tmo + 512], BufC[:, tmo: tmo + 512],
                    INV_SQM, EPS_F, ALU.mult, ALU.add),
                    waits=[(eS, tEXP[t])], inc=eV)

            def st_tr_delta(t):
                qg, c, pair = its[t]
                pg = qg % 2
                tmo = T_TM + (t % 4) * 512
                tro = (t % 2) * 512
                wts = [(eV, tEPS[t]), (s_c, t_call)]
                if (t - 1) in tFME:
                    wts.append((eS, tFME[t - 1]))
                tt = None
                for j in range(4):
                    tt = op("tensor", lambda e, tmo=tmo, tro=tro, j=j:
                             e.transpose(p6a[:, tro + j * 128: tro + (j + 1) * 128],
                                         BufC[:, tmo + j * 128: tmo + (j + 1) * 128],
                                         ident_b[:, :]),
                             waits=(wts if j == 0 else ()),
                             inc=(eT if j == 3 else None))
                tTR[t] = tt
                wts = []
                if (t - 1) in tSA:
                    wts.append((eV, tSA[t - 1]))
                td = None
                for j in range(2):
                    vsl = actB[:, a_v(pg) + c * 260 + (pair * 2 + j) * 65:
                               a_v(pg) + c * 260 + (pair * 2 + j) * 65 + 65]
                    td = op("tensor", lambda e, tmo=tmo, j=j, t=t, vsl=vsl:
                            e.matmul(p3[:, (t % 2) * 256 + j * 128:
                                         (t % 2) * 256 + j * 128 + 65],
                                     BufC[:, tmo + 256 + j * 128:
                                          tmo + 256 + (j + 1) * 128],
                                     vsl, start=True, stop=True),
                            waits=(wts if j == 0 else ()),
                            inc=(eT if j == 1 else None))
                tD[t] = td

            def st_fme_sacc(t):
                qg, c, pair = its[t]
                tro = (t % 2) * 512
                fmo = T_FM + (t % 4) * 512
                wts = [(eT, tTR[t])]
                if (t - 4) in tO:
                    wts.append((eT, tO[t - 4]))
                tFME[t] = op("scalar", lambda e, tro=tro, fmo=fmo: e.copy(
                    BufC[:, fmo: fmo + 512], p6a[:, tro: tro + 512]),
                    waits=wts, inc=eS)
                sfo = ((qg % 2) * 2 + pair) * 132
                sfv = bass.AP(Sf_t, sfo, [[528, 128], [66, 2], [1, 65]])
                d3v = bass.AP(p3, (t % 2) * 256, [[512, 128], [128, 2], [1, 65]])
                wts = [(eT, tD[t])]
                if (t - 2) in tSA:
                    wts.append((eV, tSA[t - 2]))
                if (t - 2) in tSBE:
                    wts.append((eS, tSBE[t - 2]))
                if c == 0:
                    tSA[t] = op("vector", lambda e, sfv=sfv, d3v=d3v:
                                e.tensor_copy(sfv, d3v), waits=wts, inc=eV)
                else:
                    tSA[t] = op("vector", lambda e, sfv=sfv, d3v=d3v:
                                e.scalar_tensor_tensor(sfv, d3v, 1.0, sfv,
                                                       ALU.mult, ALU.add),
                                waits=wts, inc=eV)

            def st_abar_sbe(t):
                qg, c, pair = its[t]
                fmo = T_FM + (t % 4) * 512
                ao = (t % 2) * 256
                wts = [(eS, tFME[t])]
                if (t - 1) in tAM:
                    wts.append((eV, tAM[t - 1]))
                ta = None
                for j in range(2):
                    ta = op("tensor", lambda e, fmo=fmo, ao=ao, j=j: e.matmul(
                        p5[:, ao + j * 128: ao + (j + 1) * 128],
                        BufC[:, fmo + 256 + j * 128: fmo + 256 + (j + 1) * 128],
                        BufC[:, fmo + j * 128: fmo + (j + 1) * 128],
                        start=True, stop=True),
                        waits=(wts if j == 0 else ()), inc=(eT if j == 1 else None))
                tA[t] = ta
                sfo = ((qg % 2) * 2 + pair) * 132
                sbo = ((c % 3) * 2 + pair) * 132
                sbv = bass.AP(sbT, sbo, [[1056, 128], [66, 2], [1, 65]])
                sfr = bass.AP(Sf_t, sfo, [[528, 128], [66, 2], [1, 65]])
                wts = [(eV, tSA[t])]
                if (t - 4) in tO:
                    wts.append((eT, tO[t - 4]))
                tSBE[t] = op("scalar", lambda e, sbv=sbv, sfr=sfr:
                             e.copy(sbv, sfr), waits=wts, inc=eS)

            def st_mask(t):
                ao = (t % 2) * 256
                mb = bass.AP(maskji, 0, [[128, 128], [0, 2], [1, 128]])
                wts = [(eT, tA[t])]
                if (t - 2) in tO:
                    wts.append((eT, tO[t - 2]))
                amo = (t % 2) * 256
                tAM[t] = op("vector", lambda e, ao=ao, amo=amo, mb=mb: e.tensor_tensor(
                    amT[:, amo: amo + 256].rearrange("p (a b) -> p a b", a=2),
                    p5[:, ao: ao + 256].rearrange("p (a b) -> p a b", a=2),
                    mb, op=ALU.mult), waits=wts, inc=eV)

            def st_o(t):
                qg, c, pair = its[t]
                pg = qg % 2
                fmo = T_FM + (t % 4) * 512
                oo = (t % 2) * 256
                amo = (t % 2) * 256
                wts = [(eV, tAM[t])]
                if (t - 1) in tSCL:
                    wts.append((eV, tSCL[t - 1]))
                if c > 0:
                    wts.append((eS, tSBE[t - 2]))
                to = None
                for j in range(2):
                    vsl = actB[:, a_v(pg) + c * 260 + (pair * 2 + j) * 65:
                               a_v(pg) + c * 260 + (pair * 2 + j) * 65 + 65]
                    to = op("tensor", lambda e, oo=oo, amo=amo, j=j, vsl=vsl, c=c:
                            e.matmul(pOb[:, oo + j * 128: oo + j * 128 + 65],
                                     amT[:, amo + j * 128: amo + (j + 1) * 128],
                                     vsl, start=True, stop=(c == 0)),
                            waits=(wts if j == 0 else ()),
                            inc=(eT if (c == 0 and j == 1) else None))
                    if c > 0:
                        sbo = (((c - 1) % 3) * 2 + pair) * 132
                        sbs = sbT[:, sbo + j * 66: sbo + j * 66 + 65]
                        to = op("tensor", lambda e, oo=oo, fmo=fmo, j=j, sbs=sbs:
                                e.matmul(pOb[:, oo + j * 128: oo + j * 128 + 65],
                                         BufC[:, fmo + j * 128: fmo + (j + 1) * 128],
                                         sbs, start=False, stop=True),
                                waits=(), inc=(eT if j == 1 else None))
                tO[t] = to

            def st_tail(t):
                qg, c, pair = its[t]
                oo = (t % 2) * 256
                dnv = bass.AP(pOb, oo + 64, [[512, 128], [128, 2], [1, 1]])
                tdn = op("vector", lambda e, dnv=dnv, t=t: e.tensor_scalar(
                    den_t[:, (t % 2) * 2: (t % 2) * 2 + 2].rearrange(
                        "p (a b) -> p a b", b=1), dnv, EPS_F, None,
                    ALU.add, ALU.bypass), waits=[(eT, tO[t])], inc=eV)
                trc = op("vector", lambda e, t=t: e.reciprocal(
                    rec_t[:, (t % 2) * 2: (t % 2) * 2 + 2],
                    den_t[:, (t % 2) * 2: (t % 2) * 2 + 2]),
                    waits=[(eV, tdn)], inc=eV)
                outv = BufC[:, c * 1024 + (qg * 4 + pair * 2) * 64:
                            c * 1024 + (qg * 4 + pair * 2) * 64 + 128] \
                    .rearrange("p (a b) -> p a b", a=2)
                o6v = bass.AP(pOb, oo, [[512, 128], [128, 2], [1, 64]])
                rcb = bass.AP(rec_t, (t % 2) * 2, [[8, 128], [1, 2], [0, 64]])
                tSCL[t] = op("vector", lambda e, outv=outv, o6v=o6v, rcb=rcb:
                             e.tensor_tensor(outv, o6v, rcb, op=ALU.mult),
                             waits=[(eV, trc)], inc=eV)

            STAGES_A = [(6, st_abar_sbe),
                        (5, st_fme_sacc), (4, st_tr_delta), (3, st_eps),
                        (2, st_exp)]
            STAGES_B = [(1, st_rm), (0, st_proj)]

            gens = [qkv_gen(qg) for qg in range(4)]
            for _ in gens[0]:
                pass
            done_g = [True, False, False, False]

            def adv_gen(gq):
                if gq <= 3 and not done_g[gq]:
                    try:
                        next(gens[gq])
                    except StopIteration:
                        done_g[gq] = True

            for sl in range(NE + 9):
                if 0 <= sl - 8 < NE:
                    st_o(sl - 8)
                if 0 <= sl - 7 < NE:
                    st_mask(sl - 7)
                if 0 <= sl - 8 < NE:
                    st_tail(sl - 8)
                gq = sl // 32 + 1
                adv_gen(gq)
                for ofs, fn in STAGES_A:
                    t = sl - ofs
                    if 0 <= t < NE:
                        fn(t)
                adv_gen(gq)
                for ofs, fn in STAGES_B:
                    t = sl - ofs
                    if 0 <= t < NE:
                        fn(t)
                if sl % 32 == 31 and gq <= 3 and not done_g[gq]:
                    for _ in gens[gq]:
                        pass
                    done_g[gq] = True
            barrier()

            tWo = dma("gpsimd", s_wo, BufC[:, C_WO:C_WO + 4096]
                      .rearrange("p (k j) -> p k j", k=4), Wo_sw[:, 0:4, :],
                      max_dma_last_dim=8192)
            tWo2 = dma("gpsimd", cwu[3], BufC[:, C_WO + 4096:C_WO + 8192]
                       .rearrange("p (k j) -> p k j", k=4), Wo_sw[:, 4:8, :],
                       max_dma_last_dim=8192)
            ppair = ((p0, p1), (p2, p3))
            tY, tAF, tWoMM = {}, {}, {}
            ypend = {}

            def emit_ypair(c):
                ba, bb, tmpo, tTmpc = ypend[c]
                op("vector", lambda e, c=c, ba=ba, tmpo=tmpo: e.tensor_tensor(
                    actB[:, A_Y + c * 1024: A_Y + c * 1024 + 512],
                    actB[:, tmpo: tmpo + 512], ba[:, 0:512], op=ALU.add),
                    waits=[(eS, tTmpc), (eT, tWoMM[c])], inc=eV)
                tY[c] = op("vector", lambda e, c=c, bb=bb, tmpo=tmpo: e.tensor_tensor(
                    actB[:, A_Y + c * 1024 + 512: A_Y + (c + 1) * 1024],
                    actB[:, tmpo + 512: tmpo + 1024], bb[:, 0:512], op=ALU.add),
                    waits=(), inc=eV)

            for c in range(NT):
                pi = c % 2
                fi = c % 3
                wts = []
                if c >= 2:
                    wts.append((eV, tAF[c - 2]))
                tt = None
                for j in range(8):
                    tt = op("tensor", lambda e, j=j, pi=pi, c=c: e.transpose(
                        p6[pi][:, j * 128:(j + 1) * 128],
                        BufC[:, c * 1024 + j * 128: c * 1024 + (j + 1) * 128],
                        ident_b[:, :]), waits=(wts if j == 0 else ()),
                        inc=(eT if j == 7 else None))
                wv = [(eT, tt)]
                if c >= 3:
                    wv.append((eT, tWoMM[c - 3]))
                tAF[c] = op("vector", lambda e, pi=pi, fi=fi: e.tensor_copy(
                    actB[:, A_LNTM + fi * 1024: A_LNTM + (fi + 1) * 1024],
                    p6[pi][:, 0:1024]), waits=wv, inc=eV)
                ba, bb = ppair[pi]
                for k in range(8):
                    wts = []
                    if k == 0:
                        wts = [(eV, tAF[c])]
                        if c <= 1:
                            wts.append((s_wo, tWo))
                        if c >= 2:
                            wts.append((eV, tY[c - 2]))
                    elif k == 4 and c == 0:
                        wts = [(cwu[3], tWo2)]
                    op("tensor", lambda e, k=k, fi=fi, ba=ba: e.matmul(
                        ba[:, 0:512],
                        actB[:, A_LNTM + fi * 1024 + k * 128: A_LNTM + fi * 1024 + (k + 1) * 128],
                        BufC[:, C_WO + k * 1024: C_WO + k * 1024 + 512],
                        start=(k == 0), stop=(k == 7)), waits=wts, inc=None)
                    tWoMM[c] = op("tensor", lambda e, k=k, fi=fi, bb=bb: e.matmul(
                        bb[:, 0:512],
                        actB[:, A_LNTM + fi * 1024 + k * 128: A_LNTM + fi * 1024 + (k + 1) * 128],
                        BufC[:, C_WO + k * 1024 + 512: C_WO + (k + 1) * 1024],
                        start=(k == 0), stop=(k == 7)), waits=(),
                        inc=(eT if k == 7 else None))
                tmpo = 20480 + pi * 1024
                wts = []
                if c >= 2:
                    wts.append((eV, tY[c - 2]))
                tTmp = op("scalar", lambda e, c=c, tmpo=tmpo: e.activation(
                    actB[:, tmpo: tmpo + 1024], x_tm[:, c * 1024:(c + 1) * 1024],
                    AF.Identity, bias=m2r2[:, c:c + 1], scale=rstd2[:, c:c + 1]),
                    waits=wts, inc=eS)
                ypend[c] = (ba, bb, tmpo, tTmp)
                if c >= 1:
                    emit_ypair(c - 1)
            emit_ypair(NT - 1)
            barrier()

            BufCf = BufC.bitcast(F32)
            t_le2 = op("vector", lambda e: e.memset(lneps[:, :], 1e6), inc=eV)
            dump2i = actB[:, A_LNTM + 2048:A_LNTM + 3072]
            ev_tag, tr_tag, hb_tag = {}, {}, {}

            def emit_iev(c):
                pi = c % 2
                te = None
                for k in range(8):
                    dst = fmA[:, k * 2048 + c * 128: k * 2048 + (c + 1) * 128]
                    src = p6[pi][:, k * 128:(k + 1) * 128]
                    te = op("vector", lambda e, dst=dst, src=src, k=k:
                            e.tensor_scalar(dst, src, gamF_s[:, k:k + 1],
                                            betF_s[:, k:k + 1], ALU.mult, ALU.add),
                            waits=([(eT, tr_tag[c])] if k == 0 else ()),
                            inc=(eV if k == 7 else None))
                ev_tag[c] = (eV, te)

            tRSi = {}

            def emit_istats(c):
                tRi = op("scalar", lambda e, c=c: e.activation(
                    dump2i, actB[:, A_Y + c * 1024: A_Y + (c + 1) * 1024],
                    AF.Identity, accum_out=musum[:, c:c + 1]),
                    waits=[(eV, tY[c])], inc=eS)
                tMUi = op("gpsimd", lambda e, c=c: e.tensor_scalar(
                    mu_c[:, c:c + 1], musum[:, c:c + 1], 1.0 / 1024.0, None,
                    ALU.mult, ALU.bypass), waits=[(eS, tRi)], inc=eG)
                tSQi = op("scalar", lambda e, c=c: e.activation(
                    dump2i, actB[:, A_Y + c * 1024: A_Y + (c + 1) * 1024],
                    AF.Square, bias=mu_c[:, c:c + 1], scale=-1.0,
                    accum_out=varsum[:, c:c + 1]), waits=[(eG, tMUi)], inc=eS)
                wl = [(eS, tSQi)] + ([(eV, t_le2)] if c == 0 else [])
                tLNi = op("scalar", lambda e, c=c: e.activation(
                    varsum[:, c:c + 1], varsum[:, c:c + 1], AF.Ln,
                    bias=lneps[:, 0:1], scale=1.0 / 1024.0), waits=wl, inc=eS)
                tRSi[c] = op("scalar", lambda e, c=c: e.activation(
                    rstd_c[:, c:c + 1], varsum[:, c:c + 1], AF.Exp, scale=-0.5),
                    waits=[(eS, tLNi)], inc=eS)

            def emit_ixform(c):
                pi = c % 2
                lntm = actB[:, A_LNTM + pi * 1024: A_LNTM + (pi + 1) * 1024]
                wts = [(eS, tRSi[c])]
                if c == 0:
                    wts.append((s_gam, t_gamall))
                if c >= 2:
                    wts.append((eT, tr_tag[c - 2]))
                tT3 = op("vector", lambda e, c=c, lntm=lntm: e.tensor_scalar(
                    lntm, actB[:, A_Y + c * 1024: A_Y + (c + 1) * 1024],
                    mu_c[:, c:c + 1], rstd_c[:, c:c + 1],
                    ALU.subtract, ALU.mult), waits=wts, inc=eV)
                w2 = [(eV, tT3)]
                if c >= 2:
                    w2.append(ev_tag[c - 2])
                tt = None
                for j in range(8):
                    tt = op("tensor", lambda e, j=j, pi=pi, lntm=lntm: e.transpose(
                        p6[pi][:, j * 128:(j + 1) * 128], lntm[:, j * 128:(j + 1) * 128],
                        ident_b[:, :]), waits=(w2 if j == 0 else ()),
                        inc=(eT if j == 7 else None))
                tr_tag[c] = tt

            for c in range(NT + 2):
                if c < NT:
                    emit_istats(c)
                if c >= 2:
                    emit_ixform(c - 2)
                if c >= 3:
                    emit_iev(c - 3)
            emit_iev(NT - 1)
            barrier()

            relu_hist = []
            w1mm_hist = []
            w2mm_hist = []
            o2_hist = []
            tr2_hist = []
            xu2_hist = []
            w2_last_of_quarter = 0
            for tq in range(4):
                ts0 = tq * 512
                tW1g = {}

                def w1_load(gf, tq=tq):
                    pi4 = gf % 2
                    w1sem = (swA, swB)[pi4]
                    wts = [(cw1, cast_w1)]
                    if w1mm_hist:
                        wts.append((eT, w1mm_hist[-1]))
                    tW1g[gf] = dma("sync", w1sem,
                                   BufC[:, pi4 * 4096:(pi4 + 1) * 4096]
                                   .rearrange("p (f k j) -> p f k j", f=4, k=8),
                                   W1_bf[:, gf * 4:(gf + 1) * 4, :, :], waits=wts)
                w1_load(0)
                w1_load(1)
                for f in range(NF):
                    pi = f % 2
                    gf = f // 4
                    if f % 4 == 0 and f > 0 and gf + 1 <= 7:
                        w1_load(gf + 1)
                    w1sem = (swA, swB)[gf % 2]
                    w1of = (gf % 2) * 4096 + (f % 4) * 1024
                    lm = None
                    for k in range(8):
                        wts = []
                        if k == 0:
                            if f % 4 == 0:
                                wts.append((w1sem, tW1g[gf]))
                            if len(relu_hist) >= 2:
                                wts.append((eV, relu_hist[-2]))
                        lm = op("tensor", lambda e, k=k, pi=pi, ts0=ts0, w1of=w1of:
                                e.matmul(
                            pP[pi][:, 0:512],
                            BufC[:, w1of + k * 128: w1of + (k + 1) * 128],
                            fmA[:, k * 2048 + ts0: k * 2048 + ts0 + 512],
                            start=(k == 0), stop=(k == 7)),
                            waits=wts, inc=(eT if k == 7 else None))
                    w1mm_hist.append(lm)
                    wts = [(eT, lm)]
                    if tq > 0 and f == 0:
                        wts.append((eT, w2_last_of_quarter))
                    tRl = op("vector", lambda e, pi=pi, f=f: e.tensor_scalar(
                        actB[:, A_RELU + f * 512: A_RELU + (f + 1) * 512],
                        pP[pi][:, 0:512], b1t[:, f:f + 1], 0.0, ALU.add, ALU.max),
                        waits=wts, inc=eV)
                    relu_hist.append(tRl)
                for d in range(8):
                    pi = d % 2
                    w2sem = (sW2a, sW2b)[pi]
                    wts = [(cw2, cast_w2)]
                    if len(w2mm_hist) >= 2:
                        wts.append((eT, w2mm_hist[-2]))
                    tW2 = dma("sync", w2sem, BufC[:, 8192 + pi * 4096: 8192 + (pi + 1) * 4096]
                              .rearrange("p (k j) -> p k j", k=32),
                              W2_bf[:, d, :, :], waits=wts)
                    lm = None
                    for f in range(NF):
                        wts = []
                        if f == 0:
                            wts = [(w2sem, tW2), (eV, relu_hist[-1])]
                            if len(o2_hist) >= 2:
                                wts.append((eS, o2_hist[-2]))
                        lm = op("tensor", lambda e, f=f, pi=pi:
                                e.matmul(
                            pQ[pi][:, 0:512],
                            BufC[:, 8192 + pi * 4096 + f * 128: 8192 + pi * 4096 + (f + 1) * 128],
                            actB[:, A_RELU + f * 512: A_RELU + (f + 1) * 512],
                            start=(f == 0), stop=(f == NF - 1)),
                            waits=wts, inc=(eT if f == NF - 1 else None))
                    w2mm_hist.append(lm)
                    w2_last_of_quarter = lm
                    o2reg = tmpF[:, 0:512] if pi == 0 else tmpF[:, 512:1024]
                    wts = [(eT, lm)]
                    if len(tr2_hist) >= 2:
                        wts.append((eT, tr2_hist[-2]))
                    tO2 = op("scalar", lambda e, pi=pi, d=d, o2reg=o2reg:
                             e.activation(o2reg, pQ[pi][:, 0:512], AF.Identity,
                                          bias=b2t[:, d:d + 1]),
                             waits=wts, inc=eS)
                    o2_hist.append(tO2)
                    ptr = (p4, p5)[pi]
                    wts = [(eS, tO2)]
                    if len(xu2_hist) >= 2:
                        wts.append((eV, xu2_hist[-2]))
                    tt = None
                    for j in range(4):
                        tt = op("tensor", lambda e, j=j, ptr=ptr, o2reg=o2reg: e.transpose(
                            ptr[:, j * 128:(j + 1) * 128],
                            o2reg[:, j * 128:(j + 1) * 128], ident_f[:, :]),
                            waits=(wts if j == 0 else ()), inc=(eT if j == 3 else None))
                    tr2_hist.append(tt)
                    tx = None
                    for j in range(4):
                        ct = tq * 4 + j
                        tx = op("vector", lambda e, j=j, ptr=ptr, ct=ct, d=d:
                                e.scalar_tensor_tensor(
                            x_tm[:, ct * 1024 + d * 128: ct * 1024 + (d + 1) * 128],
                            ptr[:, j * 128:(j + 1) * 128], 1.0,
                            x_tm[:, ct * 1024 + d * 128: ct * 1024 + (d + 1) * 128],
                            ALU.mult, ALU.add),
                            waits=([(eT, tt)] if j == 0 else ()), inc=eV)
                    xu2_hist.append(tx)
            barrier()

        if stage >= 3:
            ln3_ev = emit_outer_ln(1e-5, pipe=True, pipe_chunks=False,
                                   pipe_stagger=True)
            ffn2_x = {}
            emit_ffn(ln3_ev, xtags=ffn2_x)
            for c in range(NT):
                dma("sync" if c % 2 == 0 else "scalar", s_x,
                    out[c * 128:(c + 1) * 128, :],
                    x_tm[:, c * 1024:(c + 1) * 1024], waits=[ffn2_x[c]])
            barrier()
        else:
            for c in range(NT):
                dma("sync", s_x, out[c * 128:(c + 1) * 128, :],
                    x_tm[:, c * 1024:(c + 1) * 1024])
        op("sync", lambda e: None, waits=[(s_x, s_x.n)])

        with nc.Block() as block:
            @block.sync
            def _(e):
                for f in ops["sync"]:
                    f(e)

            @block.gpsimd
            def _(e):
                for f in ops["gpsimd"]:
                    f(e)

            @block.tensor
            def _(e):
                for f in ops["tensor"]:
                    f(e)

            @block.vector
            def _(e):
                for f in ops["vector"]:
                    f(e)

            @block.scalar
            def _(e):
                for f in ops["scalar"]:
                    f(e)

    return nc


def _omd(omega):
    d = np.zeros((128, 256), np.float32)
    d[0:64, 0:128] = omega.T
    d[64:128, 128:256] = omega.T
    return d


def _prep_inputs(inputs):
    f32 = np.float32
    Wq = np.asarray(inputs["L0_attn_Wq"], f32)
    Wk = np.asarray(inputs["L0_attn_Wk"], f32)
    Wv = np.asarray(inputs["L0_attn_Wv"], f32)
    Wo = np.asarray(inputs["L0_attn_Wo"], f32)
    W1 = np.asarray(inputs["L0_attn_W1"], f32)
    W2 = np.asarray(inputs["L0_attn_W2"], f32)
    b1 = np.asarray(inputs["L0_attn_b1"], f32)
    b2 = np.asarray(inputs["L0_attn_b2"], f32)
    gamma = np.asarray(inputs["L0_attn_gamma"], f32)
    beta = np.asarray(inputs["L0_attn_beta"], f32)
    omega = np.asarray(inputs["L0_attn_omega"], f32)
    wg = np.asarray(inputs["L0_ffn_w_gate"], np.int32)
    wu = np.asarray(inputs["L0_ffn_w_up"], np.int32)
    wd = np.asarray(inputs["L0_ffn_w_down"], np.int32)

    c = np.ascontiguousarray
    shared = {
        "wgT_i": c(wg.T),
        "wuT_i": c(wu.T),
        "wdT_i": c(wd.T),
        "Wq_t": c(Wq.reshape(8, 128, 2, 512).transpose(2, 1, 0, 3)),
        "Wk_t": c(Wk.reshape(8, 128, 2, 512).transpose(2, 1, 0, 3)),
        "Wv_t": c(Wv.reshape(8, 128, 2, 512).transpose(2, 1, 0, 3)),
        "Wo_sw": c(Wo.reshape(8, 128, 1024).transpose(1, 0, 2)),
        "W1_sw": c(W1.reshape(8, 128, 32, 128).transpose(1, 2, 0, 3)),
        "W2_sw": c(W2.reshape(32, 128, 8, 128).transpose(1, 2, 0, 3)),
        "omD": _omd(omega),
        "b1t": c(b1.reshape(32, 128).T),
        "b2t": c(b2.reshape(8, 128).T),
        "gam_bc": c(np.broadcast_to(gamma[None, :], (128, 1024))),
        "gamF": c(gamma.reshape(8, 128).T),
        "betF": c(beta.reshape(8, 128).T),
        "bet_bc": c(np.broadcast_to(beta[None, :], (128, 1024))),
        "ident_bf": np.eye(128, dtype=BFnp),
        "ident_f32": np.eye(128, dtype=f32),
        "maskji": (np.arange(128)[:, None] <= np.arange(128)[None, :]).astype(BFnp),
    }
    return shared


def run_device(inputs, stage=3, core_ids=None):
    shared = _prep_inputs(inputs)
    batch = np.asarray(inputs["L0_attn_batch"], np.float32)
    if core_ids is None:
        core_ids = list(range(8))
    nc = build(stage)
    in_maps = []
    for b in range(len(core_ids)):
        m = dict(shared)
        m["xin"] = np.ascontiguousarray(batch[b])
        in_maps.append(m)
    res = run_bass_kernel_spmd(nc, in_maps, core_ids=core_ids)
    return np.stack([res.results[i]["out"] for i in range(len(core_ids))], axis=0)


def kernel(**inputs):
    return run_device(inputs, stage=3, core_ids=list(range(8)))


# version 65
# speedup vs baseline: 1.2105x; 1.0053x over previous
import contextlib
import numpy as np
import ml_dtypes

import concourse.bass as bass
import concourse.mybir as mybir
from concourse.bass_utils import run_bass_kernel_spmd

F32 = mybir.dt.float32
BF16 = mybir.dt.bfloat16
I32 = mybir.dt.int32
AF = mybir.ActivationFunctionType
ALU = mybir.AluOpType
AX = mybir.AxisListType
BFnp = ml_dtypes.bfloat16

NT = 16
NHC = 24
NF = 32
SCALE_QK = float(64.0 ** -0.25)
INV127 = float(1.0 / 127.0)
HALF127 = float(0.5 / 127.0)
INV_SQM = float(1.0 / np.sqrt(np.float64(128.0)))
EPS_F = 1e-6
ENGINES = ("sync", "gpsimd", "tensor", "vector", "scalar")

A_Q = 0
A_K = 8192
A_V = 16384
A_Y = 0
A_RELU = 0
A_LNTM = 16384
ACTB_W = 24704

C_ATT = 0
C_QKVW = 16384
C_WO = 16384
BUFC_W = 24576


class Sem:
    def __init__(self, h):
        self.h = h
        self.n = 0


def build(stage=3, attn_lvl=6):
    nc = bass.Bass("TRN2", target_bir_lowering=False)

    xin = nc.dram_tensor("xin", [2048, 1024], F32, kind="ExternalInput").ap()
    wgT_i = nc.dram_tensor("wgT_i", [1024, 3072], I32, kind="ExternalInput").ap()
    wuT_i = nc.dram_tensor("wuT_i", [1024, 3072], I32, kind="ExternalInput").ap()
    wdT_i = nc.dram_tensor("wdT_i", [3072, 1024], I32, kind="ExternalInput").ap()
    Wq_t = nc.dram_tensor("Wq_t", [2, 128, 8, 512], F32, kind="ExternalInput").ap()
    Wk_t = nc.dram_tensor("Wk_t", [2, 128, 8, 512], F32, kind="ExternalInput").ap()
    Wv_t = nc.dram_tensor("Wv_t", [2, 128, 8, 512], F32, kind="ExternalInput").ap()
    Wo_sw = nc.dram_tensor("Wo_sw", [128, 8, 1024], F32, kind="ExternalInput").ap()
    W1_sw = nc.dram_tensor("W1_sw", [128, 32, 8, 128], F32, kind="ExternalInput").ap()
    W2_sw = nc.dram_tensor("W2_sw", [128, 8, 32, 128], F32, kind="ExternalInput").ap()
    omTd_in = nc.dram_tensor("omD", [128, 256], F32, kind="ExternalInput").ap()
    b1t_in = nc.dram_tensor("b1t", [128, 32], F32, kind="ExternalInput").ap()
    b2t_in = nc.dram_tensor("b2t", [128, 8], F32, kind="ExternalInput").ap()
    gam_in = nc.dram_tensor("gam_bc", [128, 1024], F32, kind="ExternalInput").ap()
    gamF_in = nc.dram_tensor("gamF", [128, 8], F32, kind="ExternalInput").ap()
    betF_in = nc.dram_tensor("betF", [128, 8], F32, kind="ExternalInput").ap()
    bet_in = nc.dram_tensor("bet_bc", [128, 1024], F32, kind="ExternalInput").ap()
    idb_in = nc.dram_tensor("ident_bf", [128, 128], BF16, kind="ExternalInput").ap()
    idf_in = nc.dram_tensor("ident_f32", [128, 128], F32, kind="ExternalInput").ap()
    msk_in = nc.dram_tensor("maskji", [128, 128], BF16, kind="ExternalInput").ap()
    out = nc.dram_tensor("out", [2048, 1024], F32, kind="ExternalOutput").ap()

    wgT_bf = nc.dram_tensor("wgT_bf", [1024, 3072], BF16, kind="Internal").ap()
    wuT_bf = nc.dram_tensor("wuT_bf", [1024, 3072], BF16, kind="Internal").ap()
    wdT_bf = nc.dram_tensor("wdT_bf", [3072, 1024], BF16, kind="Internal").ap()
    W1_bf = nc.dram_tensor("W1_bf", [128, 32, 8, 128], BF16, kind="Internal").ap()
    W2_bf = nc.dram_tensor("W2_bf", [128, 8, 32, 128], BF16, kind="Internal").ap()
    Wq_bf = nc.dram_tensor("Wq_bf", [2, 128, 8, 512], BF16, kind="Internal").ap()
    Wk_bf = nc.dram_tensor("Wk_bf", [2, 128, 8, 512], BF16, kind="Internal").ap()
    Wv_bf = nc.dram_tensor("Wv_bf", [2, 128, 8, 512], BF16, kind="Internal").ap()

    es = contextlib.ExitStack()
    with es:
        x_tm = es.enter_context(nc.sbuf_tensor("x_tm", [128, NT * 1024], F32))
        fmA = es.enter_context(nc.sbuf_tensor("fmA", [128, 8 * 2048], BF16))
        actB = es.enter_context(nc.sbuf_tensor("actB", [128, ACTB_W], BF16))
        BufC = es.enter_context(nc.sbuf_tensor("BufC", [128, BUFC_W], BF16))
        tmpF = es.enter_context(nc.sbuf_tensor("tmpF", [128, 1024], F32))
        gam_bf = es.enter_context(nc.sbuf_tensor("gam_bf", [128, 1024], BF16))
        gamF_s = es.enter_context(nc.sbuf_tensor("gamF_s", [128, 8], F32))
        betF_s = es.enter_context(nc.sbuf_tensor("betF_s", [128, 8], F32))
        bet_bf = es.enter_context(nc.sbuf_tensor("bet_bf", [128, 1024], BF16))
        Sf_t = es.enter_context(nc.sbuf_tensor("Sf_t", [128, 528], F32))
        amT = es.enter_context(nc.sbuf_tensor("amT", [128, 512], BF16))
        sbT = es.enter_context(nc.sbuf_tensor("sbT", [128, 1056], BF16))
        den_t = es.enter_context(nc.sbuf_tensor("den_t", [128, 8], F32))
        rec_t = es.enter_context(nc.sbuf_tensor("rec_t", [128, 8], F32))
        musum = es.enter_context(nc.sbuf_tensor("musum", [128, 16], F32))
        lneps = es.enter_context(nc.sbuf_tensor("lneps", [128, 1], F32))
        mu_c = es.enter_context(nc.sbuf_tensor("mu_c", [128, 16], F32))
        varsum = es.enter_context(nc.sbuf_tensor("varsum", [128, 16], F32))
        rstd_c = es.enter_context(nc.sbuf_tensor("rstd_c", [128, 16], F32))
        mu2 = es.enter_context(nc.sbuf_tensor("mu2", [128, 16], F32))
        rstd2 = es.enter_context(nc.sbuf_tensor("rstd2", [128, 16], F32))
        m2r2 = es.enter_context(nc.sbuf_tensor("m2r2", [128, 16], F32))
        st_nm = es.enter_context(nc.sbuf_tensor("st_nm", [128, 16], F32))
        omT_bf = es.enter_context(nc.sbuf_tensor("omT_bf", [128, 256], BF16))
        ident_b = es.enter_context(nc.sbuf_tensor("ident_b", [128, 128], BF16))
        ident_f = es.enter_context(nc.sbuf_tensor("ident_f", [128, 128], F32))
        maskji = es.enter_context(nc.sbuf_tensor("maskji_s", [128, 128], BF16))
        b1t = es.enter_context(nc.sbuf_tensor("b1t_s", [128, 32], F32))
        b2t = es.enter_context(nc.sbuf_tensor("b2t_s", [128, 8], F32))

        p0 = es.enter_context(nc.psum_tensor("p0", [128, 512], F32))
        p1 = es.enter_context(nc.psum_tensor("p1", [128, 512], F32))
        p2 = es.enter_context(nc.psum_tensor("p2", [128, 512], F32))
        p3 = es.enter_context(nc.psum_tensor("p3", [128, 512], F32))
        p4 = es.enter_context(nc.psum_tensor("p4", [128, 512], F32))
        p5 = es.enter_context(nc.psum_tensor("p5", [128, 512], F32))
        p6a = es.enter_context(nc.psum_tensor("p6a", [128, 1024], BF16))
        p6b = es.enter_context(nc.psum_tensor("p6b", [128, 1024], BF16))
        pP = (p0, p1)
        pQ = (p2, p3)
        p6 = (p6a, p6b)
        pOb = p6b.bitcast(F32)

        def new_sem(name):
            return Sem(es.enter_context(nc.semaphore(name)))
        s_x = new_sem("s_x")
        s_xc = [s_x] + [new_sem(f"s_x{k}") for k in (1, 2, 3)]
        s_c = new_sem("s_c")
        cwg = [new_sem(f"cwg{q}") for q in range(4)]
        cwu = [new_sem(f"cwu{q}") for q in range(4)]
        cwd = new_sem("cwd")
        cw1 = new_sem("cw1")
        cw2 = new_sem("cw2")
        s_gam = new_sem("s_gam")
        cwuh = [new_sem(f"cwuh{i}") for i in range(4)]
        swA = new_sem("swA")
        swB = new_sem("swB")
        s_wd = new_sem("s_wd")
        sW2a = new_sem("sW2a")
        sW2b = new_sem("sW2b")
        qkvA = new_sem("qkvA")
        qkvB = new_sem("qkvB")
        s_wo = new_sem("s_wo")
        sync_dsems = [s_x, s_c, swA, swB, s_wd, sW2a, sW2b, qkvB] + s_xc[1:]
        gp_dsems = cwg + cwu + cwuh + [cwd, cw1, cw2, s_gam, qkvA, s_wo]
        eT = Sem(es.enter_context(nc.semaphore("eT")))
        eV = Sem(es.enter_context(nc.semaphore("eV")))
        eS = Sem(es.enter_context(nc.semaphore("eS")))
        mSY = Sem(es.enter_context(nc.semaphore("mSY")))
        mGP = Sem(es.enter_context(nc.semaphore("mGP")))
        eG = Sem(es.enter_context(nc.semaphore("eG")))

        ops = {e: [] for e in ENGINES}

        def op(eng, fn, waits=(), inc=None, inc_by=1):
            tag = None
            if inc is not None:
                inc.n += inc_by
                tag = inc.n
            wl = tuple((s, v) for (s, v) in waits if v and v > 0)

            def run(e, fn=fn, wl=wl, inc=inc, inc_by=inc_by):
                for (s, v) in wl:
                    e.wait_ge(s.h, v)
                r = fn(e)
                if inc is not None:
                    r.then_inc(inc.h, inc_by)
            ops[eng].append(run)
            return tag

        def dma(q, sem, out_ap, in_ap, waits=(), **kw):
            return op(q, lambda e, o=out_ap, i=in_ap, kw=dict(kw): e.dma_start(o, i, **kw),
                      waits, inc=sem, inc_by=16)

        def barrier():
            t1 = op("sync", lambda e: e.nop(),
                    waits=[(sm, sm.n) for sm in sync_dsems], inc=mSY)
            t2 = op("gpsimd", lambda e: e.nop(),
                    waits=[(sm, sm.n) for sm in gp_dsems] + [(eG, eG.n)], inc=mGP)
            t3 = op("tensor", lambda e: e.nop(), waits=[(eT, eT.n)], inc=eT)
            t4 = op("vector", lambda e: e.nop(), waits=[(eV, eV.n)], inc=eV)
            t5 = op("scalar", lambda e: e.nop(), waits=[(eS, eS.n)], inc=eS)
            wl = [(mSY, t1), (mGP, t2), (eT, t3), (eV, t4), (eS, t5)]
            for eng in ENGINES:
                op(eng, lambda e: None, waits=wl)

        t_xchunk = {}
        for c in range(NT):
            t = dma("sync" if c % 2 == 0 else "scalar", s_xc[c // 4],
                    x_tm[:, c * 1024:(c + 1) * 1024],
                    xin[c * 128:(c + 1) * 128, :])
            t_xchunk[c // 4] = t
        t_xall = s_x.n
        dma("sync", s_c, ident_b[:, :], idb_in)
        dma("sync", s_c, ident_f[:, :], idf_in)
        dma("sync", s_c, maskji[:, :], msk_in)
        dma("sync", s_c, b1t[:, :], b1t_in)
        dma("sync", s_c, b2t[:, :], b2t_in)
        t_call = s_c.n

        cast_wd_box = [None]

        lneps_tag = [0]

        def emit_outer_ln(eps, mu_store=None, rstd_store=None, src_wait=None,
                          src_tags=None, xl_wait=None, pipe=False,
                          pipe_chunks=True, pipe_stagger=False):
            mu_d = mu_store if mu_store is not None else mu_c
            rs_d = rstd_store if rstd_store is not None else rstd_c
            tRSd = {}
            if pipe:
                t_le = op("vector", lambda e, eps=float(eps):
                          e.memset(lneps[:, :], eps), inc=eV)
                lneps_tag[0] = t_le
                dump2 = actB[:, A_LNTM + 2048:A_LNTM + 3072]

                def emit_pstats(c):
                    w = []
                    if pipe_chunks and c % 4 == 0:
                        w = [(s_xc[c // 4], t_xchunk[c // 4])]
                    tR = op("vector", lambda e, c=c: e.reduce_sum(
                        musum[:, c:c + 1], x_tm[:, c * 1024:(c + 1) * 1024],
                        axis=AX.X), waits=w, inc=eV)
                    tMU = op("vector", lambda e, c=c, mu=mu_d: e.tensor_scalar(
                        mu[:, c:c + 1], musum[:, c:c + 1], 1.0 / 1024.0, None,
                        ALU.mult, ALU.bypass), waits=[(eV, tR)], inc=eV)
                    tSQ = op("scalar", lambda e, c=c, mu=mu_d, dump2=dump2:
                             e.activation(
                        dump2, x_tm[:, c * 1024:(c + 1) * 1024], AF.Square,
                        bias=mu[:, c:c + 1], scale=-1.0,
                        accum_out=varsum[:, c:c + 1]),
                        waits=[(eV, tMU)], inc=eS)
                    wl = [(eS, tSQ)] + ([(eV, t_le)] if c == 0 else [])
                    tLN = op("scalar", lambda e, c=c: e.activation(
                        varsum[:, c:c + 1], varsum[:, c:c + 1], AF.Ln,
                        bias=lneps[:, 0:1], scale=1.0 / 1024.0),
                        waits=wl, inc=eS)
                    tRSd[c] = op("scalar", lambda e, c=c, rs=rs_d: e.activation(
                        rs[:, c:c + 1], varsum[:, c:c + 1], AF.Exp, scale=-0.5),
                        waits=[(eS, tLN)], inc=eS)
            else:
                tR = None
                for c in range(NT):
                    if src_tags is not None:
                        w = [src_tags[c]]
                    else:
                        w = [src_wait] if (src_wait and c == 0) else []
                    tR = op("vector", lambda e, c=c: e.reduce_sum(
                        musum[:, c:c + 1], x_tm[:, c * 1024:(c + 1) * 1024], axis=AX.X),
                        waits=w, inc=eV)
                tMU = op("vector", lambda e, mu=mu_d: e.tensor_scalar(
                    mu[:, :], musum[:, :], 1.0 / 1024.0, None, ALU.mult, ALU.bypass),
                    waits=[(eV, tR)], inc=eV)
                tVS = None
                for c in range(NT):
                    w = [(eV, tMU)] if c == 0 else [(eS, tVS)]
                    tVS = op("scalar", lambda e, c=c, mu=mu_d: e.activation(
                        actB[:, A_LNTM:A_LNTM + 1024], x_tm[:, c * 1024:(c + 1) * 1024], AF.Square,
                        bias=mu[:, c:c + 1], scale=-1.0, accum_out=varsum[:, c:c + 1]),
                        waits=w, inc=eS)
                tSV = op("vector", lambda e, eps=float(eps): e.tensor_scalar(
                    varsum[:, :], varsum[:, :], 1.0 / 1024.0, eps, ALU.mult, ALU.add),
                    waits=[(eS, tVS)], inc=eV)
                tLN = op("scalar", lambda e: e.activation(varsum[:, :], varsum[:, :], AF.Ln),
                         waits=[(eV, tSV)], inc=eS)
                tRS = op("scalar", lambda e, rs=rs_d: e.activation(
                    rs[:, :], varsum[:, :], AF.Exp, scale=-0.5),
                    waits=[(eS, tLN)], inc=eS)
                for c in range(NT):
                    tRSd[c] = tRS
                if mu_store is not None:
                    op("vector", lambda e, mu=mu_d, rs=rs_d: e.scalar_tensor_tensor(
                        m2r2[:, :], mu[:, :], -1.0, rs[:, :], ALU.mult, ALU.mult),
                        waits=[(eS, tRS)], inc=eV)
            ev_tag, xl_tag, tr_tag = {}, {}, {}

            def emit_pxform(c):
                pi = c % 2
                lntm = actB[:, A_LNTM + pi * 1024: A_LNTM + (pi + 1) * 1024]
                w = [(eS, tRSd[c])]
                if xl_wait is not None and c == 0:
                    w.append(xl_wait)
                if c >= 2:
                    w.append((eT, tr_tag[c - 2]))
                xl_tag[c] = op("vector", lambda e, c=c, lntm=lntm, mu=mu_d, rs=rs_d:
                               e.tensor_scalar(lntm, x_tm[:, c * 1024:(c + 1) * 1024],
                                               mu[:, c:c + 1], rs[:, c:c + 1],
                                               ALU.subtract, ALU.mult),
                               waits=w, inc=eV)
                w2 = [(eV, xl_tag[c]), (s_c, t_call)]
                if c >= 2:
                    w2.append(ev_tag[c - 2])
                tt = None
                for j in range(8):
                    tt = op("tensor", lambda e, j=j, pi=pi, lntm=lntm: e.transpose(
                        p6[pi][:, j * 128:(j + 1) * 128], lntm[:, j * 128:(j + 1) * 128],
                        ident_b[:, :]),
                        waits=(w2 if j == 0 else ()), inc=(eT if j == 7 else None))
                tr_tag[c] = tt

            def emit_pev(c):
                pi = c % 2
                dst = fmA[:, :].rearrange("p (k t) -> p k t", k=8)[:, :, c * 128:(c + 1) * 128]
                src = p6[pi][:, :].rearrange("p (k t) -> p k t", k=8)
                if pipe:
                    ev_tag[c] = (eS, op("scalar", lambda e, dst=dst, src=src:
                                        e.copy(dst, src),
                                        waits=[(eT, tr_tag[c])], inc=eS))
                else:
                    ev_tag[c] = (eV, op("vector", lambda e, dst=dst, src=src:
                                        e.tensor_copy(dst, src),
                                        waits=[(eT, tr_tag[c])], inc=eV))

            if pipe and pipe_stagger:
                for c in range(NT + 2):
                    if c < NT:
                        emit_pstats(c)
                    if c >= 2:
                        emit_pxform(c - 2)
                    if c >= 3:
                        emit_pev(c - 3)
                emit_pev(NT - 1)
            elif pipe:
                for c in range(NT):
                    emit_pstats(c)
                for c in range(NT):
                    emit_pxform(c)
                    emit_pev(c)
            else:
                for c in range(NT):
                    emit_pxform(c)
                    emit_pev(c)
            return ev_tag

        ffn_last = {"mm": 0, "evac": 0}

        def emit_ffn(ln_ev, xtags=None, stats_cb=None):
            pg = (p0, p1)
            pu = (p2, p3)
            n = [0]
            tagSilu, tagAct, tagG, tagU = {}, {}, {}, {}
            last_down_mm = [0]
            last_down_evac = [0]
            q_last_mm = {}
            for th in range(2):
                tt0 = th * 1024
                for q in range(4):
                    pq = q % 2
                    wg_off = pq * 12288
                    wu_off = pq * 12288 + 6144
                    swq = (cwg[0], cwg[1])[q % 2]
                    w = []
                    gq = th * 4 + q
                    if gq >= 2:
                        w.append((eT, q_last_mm[gq - 2]))
                    if th == 1 and q < 2:
                        w.append((eT, last_down_mm[0]))
                    src = wgT_i.rearrange("(k p) h -> p k h", p=128)[:, :, q * 768:(q + 1) * 768]
                    semh1 = (cwg[0], cwg[1])[q % 2]
                    semh2 = (cwg[2], cwg[3])[q % 2]
                    semu1 = (cwuh[0], cwuh[1])[q % 2]
                    semu2 = (cwuh[2], cwuh[3])[q % 2]
                    srcu = wuT_i.rearrange("(k p) h -> p k h", p=128)[:, :, q * 768:(q + 1) * 768]
                    tH1 = dma("gpsimd", semh1, BufC[:, wg_off:wg_off + 3072]
                              .rearrange("p (k h) -> p k h", k=4), src[:, 0:4, :],
                              waits=w)
                    tU1 = dma("gpsimd", semu1, BufC[:, wu_off:wu_off + 3072]
                              .rearrange("p (k h) -> p k h", k=4), srcu[:, 0:4, :])
                    tH2 = dma("gpsimd", semh2, BufC[:, wg_off + 3072:wg_off + 6144]
                              .rearrange("p (k h) -> p k h", k=4), src[:, 4:8, :])
                    tU2 = dma("gpsimd", semu2, BufC[:, wu_off + 3072:wu_off + 6144]
                              .rearrange("p (k h) -> p k h", k=4), srcu[:, 4:8, :])
                    if gq == 1 and cast_wd_box[0] is None:
                        cast_wd_box[0] = dma("gpsimd", cwd, wdT_bf[:, :],
                                             wdT_i[:, :], max_dma_last_dim=8192)
                    for hcl in range(6):
                        hc = q * 6 + hcl
                        for tsl in range(2):
                            i = n[0]
                            pi = i % 2
                            tg = None
                            for k in range(8):
                                wts = []
                                if k == 0:
                                    if hcl == 0:
                                        wts = [(semh1, tH1),
                                               ln_ev[th * 8 + tsl * 4 + 3]]
                                    if i >= 2:
                                        wts.append((eS, tagSilu[i - 2]))
                                    if th == 1 and i == 48:
                                        wts.append((eV, last_down_evac[0]))
                                elif k == 4 and hcl == 0 and tsl == 0:
                                    wts = [(semh2, tH2)]
                                tg = op("tensor", lambda e, k=k, pi=pi, wg_off=wg_off,
                                        hcl=hcl, tt0=tt0, tsl=tsl:
                                        e.matmul(pg[pi][:, 0:512],
                                                 BufC[:, wg_off + k * 768 + hcl * 128:
                                                      wg_off + k * 768 + (hcl + 1) * 128],
                                                 fmA[:, k * 2048 + tt0 + tsl * 512:
                                                     k * 2048 + tt0 + (tsl + 1) * 512],
                                                 start=(k == 0), stop=(k == 7)),
                                        waits=wts, inc=(eT if k == 7 else None))
                            tagG[i] = tg
                            tu = None
                            for k in range(8):
                                wts = []
                                if k == 0:
                                    if hcl == 0:
                                        wts = [(semu1, tU1)]
                                    if i >= 2:
                                        wts.append((eV, tagAct[i - 2]))
                                elif k == 4 and hcl == 0 and tsl == 0:
                                    wts = [(semu2, tU2)]
                                tu = op("tensor", lambda e, k=k, pi=pi, wu_off=wu_off,
                                        hcl=hcl, tt0=tt0, tsl=tsl:
                                        e.matmul(pu[pi][:, 0:512],
                                                 BufC[:, wu_off + k * 768 + hcl * 128:
                                                      wu_off + k * 768 + (hcl + 1) * 128],
                                                 fmA[:, k * 2048 + tt0 + tsl * 512:
                                                     k * 2048 + tt0 + (tsl + 1) * 512],
                                                 start=(k == 0), stop=(k == 7)),
                                        waits=wts, inc=(eT if k == 7 else None))
                            tagU[i] = tu
                            wts = [(eT, tagG[i])]
                            if i >= 2:
                                wts.append((eV, tagAct[i - 2]))
                            tagSilu[i] = op("scalar", lambda e, pi=pi: e.activation(
                                tmpF[:, pi * 512:(pi + 1) * 512], pg[pi][:, 0:512],
                                AF.Silu, scale=INV127),
                                waits=wts, inc=eS)
                            tagAct[i] = op("vector", lambda e, pi=pi, hc=hc, tsl=tsl:
                                           e.scalar_tensor_tensor(
                                actB[:, hc * 1024 + tsl * 512: hc * 1024 + (tsl + 1) * 512],
                                pu[pi][:, 0:512], INV127,
                                tmpF[:, pi * 512:(pi + 1) * 512], ALU.mult, ALU.mult),
                                waits=[(eT, tagU[i]), (eS, tagSilu[i])], inc=eV)
                            n[0] += 1
                    q_last_mm[gq] = tagU[n[0] - 1]
                dma("sync", s_wd, BufC[:, 0:12288].rearrange("p (hc d) -> p hc d", hc=12),
                    wdT_bf.rearrange("(hc p) d -> p hc d", p=128)[:, 0:12, :],
                    waits=[(cwd, cast_wd_box[0]), (eT, q_last_mm[th * 4 + 2])])
                tWdh = s_wd.n
                tWd1 = dma("sync", cwu[0], BufC[:, 12288:16384]
                           .rearrange("p (hc d) -> p hc d", hc=4),
                           wdT_bf.rearrange("(hc p) d -> p hc d", p=128)[:, 12:16, :],
                           waits=[(eT, q_last_mm[th * 4 + 3])])
                tWd2 = dma("sync", cwu[1], BufC[:, 16384:20480]
                           .rearrange("p (hc d) -> p hc d", hc=4),
                           wdT_bf.rearrange("(hc p) d -> p hc d", p=128)[:, 16:20, :])
                tWd3 = dma("sync", cwu[2], BufC[:, 20480:24576]
                           .rearrange("p (hc d) -> p hc d", hc=4),
                           wdT_bf.rearrange("(hc p) d -> p hc d", p=128)[:, 20:24, :])
                tXU = {}
                for tl in range(8):
                    c = th * 8 + tl
                    da, db = ((p4, p5), (p0, p1))[tl % 2]
                    td = None
                    for hc in range(NHC):
                        wts = []
                        if hc == 0:
                            if tl == 0:
                                wts = [(s_wd, tWdh), (eV, tagAct[n[0] - 1])]
                            if tl >= 2:
                                wts.append((eV, tXU[tl - 2]))
                        elif tl == 0 and hc == 12:
                            wts = [(cwu[0], tWd1)]
                        elif tl == 0 and hc == 16:
                            wts = [(cwu[1], tWd2)]
                        elif tl == 0 and hc == 20:
                            wts = [(cwu[2], tWd3)]
                        op("tensor", lambda e, hc=hc, tl=tl, da=da:
                           e.matmul(da[:, 0:512],
                                    actB[:, hc * 1024 + tl * 128: hc * 1024 + (tl + 1) * 128],
                                    BufC[:, hc * 1024: hc * 1024 + 512],
                                    start=(hc == 0), stop=(hc == NHC - 1)),
                           waits=wts, inc=None)
                        td = op("tensor", lambda e, hc=hc, tl=tl, db=db:
                                e.matmul(db[:, 0:512],
                                         actB[:, hc * 1024 + tl * 128: hc * 1024 + (tl + 1) * 128],
                                         BufC[:, hc * 1024 + 512: hc * 1024 + 1024],
                                         start=(hc == 0), stop=(hc == NHC - 1)),
                                waits=(), inc=(eT if hc == NHC - 1 else None))
                    op("vector", lambda e, c=c, da=da: e.scalar_tensor_tensor(
                        x_tm[:, c * 1024: c * 1024 + 512], da[:, 0:512], HALF127,
                        x_tm[:, c * 1024: c * 1024 + 512], ALU.mult, ALU.add),
                        waits=[(eT, td)], inc=eV)
                    tXU[tl] = op("vector", lambda e, c=c, db=db: e.scalar_tensor_tensor(
                        x_tm[:, c * 1024 + 512: (c + 1) * 1024], db[:, 0:512], HALF127,
                        x_tm[:, c * 1024 + 512: (c + 1) * 1024], ALU.mult, ALU.add),
                        waits=(), inc=eV)
                    last_down_mm[0] = td
                    last_down_evac[0] = tXU[tl]
                    ffn_last["mm"] = td
                    ffn_last["evac"] = tXU[tl]
                    if xtags is not None:
                        xtags[c] = (eV, tXU[tl])
                    if stats_cb is not None:
                        stats_cb(c)

        ln1_ev = emit_outer_ln(1e-5, pipe=True)
        ffn1_x = {}
        ln2_sqt = {}

        def ln2_stats_cb(c):
            tRc = op("vector", lambda e, c=c: e.reduce_sum(
                musum[:, c:c + 1], x_tm[:, c * 1024:(c + 1) * 1024], axis=AX.X),
                waits=[ffn1_x[c]], inc=eV)
            tMUc = op("vector", lambda e, c=c: e.tensor_scalar(
                mu2[:, c:c + 1], musum[:, c:c + 1], 1.0 / 1024.0, None,
                ALU.mult, ALU.bypass), waits=[(eV, tRc)], inc=eV)
            tSQc = op("scalar", lambda e, c=c: e.activation(
                tmpF[:, 0:1024], x_tm[:, c * 1024:(c + 1) * 1024], AF.Square,
                bias=mu2[:, c:c + 1], scale=-1.0, accum_out=varsum[:, c:c + 1]),
                waits=[(eV, tMUc)], inc=eS)
            wl = [(eS, tSQc)] + ([(eV, lneps_tag[0])] if c == 0 else [])
            tLNc = op("scalar", lambda e, c=c: e.activation(
                varsum[:, c:c + 1], varsum[:, c:c + 1], AF.Ln,
                bias=lneps[:, 0:1], scale=1.0 / 1024.0), waits=wl, inc=eS)
            ln2_sqt[c] = op("scalar", lambda e, c=c: e.activation(
                rstd2[:, c:c + 1], varsum[:, c:c + 1], AF.Exp, scale=-0.5),
                waits=[(eS, tLNc)], inc=eS)
            op("gpsimd", lambda e, c=c: e.tensor_scalar(
                m2r2[:, c:c + 1], mu2[:, c:c + 1], -1.0, rstd2[:, c:c + 1],
                ALU.mult, ALU.mult), waits=[(eS, ln2_sqt[c])], inc=eG)

        emit_ffn(ln1_ev, xtags=ffn1_x,
                 stats_cb=(ln2_stats_cb if stage >= 2 else None))
        cast_wq = dma("gpsimd", qkvA, Wq_bf, Wq_t, max_dma_last_dim=8192)
        cast_wk = dma("gpsimd", qkvA, Wk_bf, Wk_t, max_dma_last_dim=8192)
        cast_wv = dma("gpsimd", qkvA, Wv_bf, Wv_t, max_dma_last_dim=8192)
        t_qkvcast = qkvA.n
        dma("gpsimd", s_gam, gamF_s[:, :], gamF_in)
        dma("gpsimd", s_gam, betF_s[:, :], betF_in)
        dma("gpsimd", s_gam, gam_bf[:, :], gam_in, max_dma_last_dim=8192)
        dma("gpsimd", s_gam, bet_bf[:, :], bet_in, max_dma_last_dim=8192)
        dma("gpsimd", s_gam, omT_bf[:, :], omTd_in, max_dma_last_dim=8192)
        t_gamall = s_gam.n
        if stage < 2:
            barrier()

        if stage >= 2:
            ffn1_mm_tag = ffn_last["mm"]
            ln2_ev, xl_tag2, tr_tag2 = {}, {}, {}
            for c in range(NT):
                pi = c % 2
                lntm = actB[:, A_LNTM + pi * 1024: A_LNTM + (pi + 1) * 1024]
                w = [(eS, ln2_sqt[c])]
                if c <= 1:
                    w.append((eT, ffn1_mm_tag))
                if c >= 2:
                    w.append((eT, tr_tag2[c - 2]))
                xl_tag2[c] = op("vector", lambda e, c=c, lntm=lntm:
                                e.tensor_scalar(lntm, x_tm[:, c * 1024:(c + 1) * 1024],
                                                mu2[:, c:c + 1], rstd2[:, c:c + 1],
                                                ALU.subtract, ALU.mult),
                                waits=w, inc=eV)
                w2 = [(eV, xl_tag2[c])]
                if c >= 2:
                    w2.append(ln2_ev[c - 2])
                tt = None
                for j in range(8):
                    tt = op("tensor", lambda e, j=j, pi=pi, lntm=lntm: e.transpose(
                        p6[pi][:, j * 128:(j + 1) * 128], lntm[:, j * 128:(j + 1) * 128],
                        ident_b[:, :]), waits=(w2 if j == 0 else ()),
                        inc=(eT if j == 7 else None))
                tr_tag2[c] = tt
                dst = fmA[:, :].rearrange("p (k t) -> p k t", k=8)[:, :, c * 128:(c + 1) * 128]
                src = p6[pi][:, :].rearrange("p (k t) -> p k t", k=8)
                ln2_ev[c] = (eV, op("vector", lambda e, dst=dst, src=src:
                                    e.tensor_copy(dst, src), waits=[(eT, tt)], inc=eV))
            t_ln2 = ln2_ev[NT - 1][1]

            def ln2_ev_w(c):
                return ln2_ev[c]
            cast_w1 = dma("gpsimd", cw1, W1_bf[:, :, :, :], W1_sw,
                          waits=[(eV, t_ln2)], max_dma_last_dim=8192)
            cast_w2 = dma("gpsimd", cw2, W2_bf[:, :, :, :], W2_sw,
                          waits=[(eV, t_ln2)], max_dma_last_dim=8192)

            QGW = 12352

            def a_q(pg):
                return pg * QGW

            def a_k(pg):
                return pg * QGW + 4096

            def a_v(pg):
                return pg * QGW + 8192

            C_SLAB = 16384
            T_TM = 20480
            T_FM = 22528

            t_ones = None
            for pg in range(2):
                ap1 = bass.AP(actB, a_v(pg) + 64, [[ACTB_W, 128], [65, 64]])
                t_ones = op("vector", lambda e, ap1=ap1: e.memset(ap1, 1.0), inc=eV)

            slab_lastmm = {}
            slab_cnt = [0]
            qkv_rot_hist = []
            qkv_rot_cnt = [0]
            qk_done = {}
            v_done = {}

            def qkv_gen(qg):
                g, qq, pg = qg // 2, qg % 2, qg % 2
                for wi, (Wsrc, isV) in enumerate(((Wq_bf, 0), (Wk_bf, 0), (Wv_bf, 1))):
                    pw = slab_cnt[0] % 2
                    slab_cnt[0] += 1
                    qsem = qkvB
                    wts = [(qsem, qsem.n), (qkvA, t_qkvcast)]
                    if pw in slab_lastmm:
                        wts.append((eT, slab_lastmm[pw]))
                    else:
                        wts.append((eT, ffn1_mm_tag))
                    tW = dma("sync", qsem, BufC[:, C_SLAB + pw * 2048:
                                                C_SLAB + (pw + 1) * 2048]
                             .rearrange("p (k j) -> p k j", k=8),
                             Wsrc[g, :, :, qq * 256:(qq + 1) * 256],
                             waits=wts)
                    lm = None
                    if not isV:
                        dst0 = a_q(pg) if wi == 0 else a_k(pg)
                        for mloc in range(2):
                            for ts in range(4):
                                pr = p4
                                qkv_rot_cnt[0] += 1
                                for k in range(8):
                                    wts = []
                                    if k == 0:
                                        wts = [ln2_ev_w(ts * 4 + 3)]
                                        if mloc == 0 and ts == 0:
                                            wts.append((qsem, tW))
                                        if len(qkv_rot_hist) >= 1:
                                            wts.append(qkv_rot_hist[-1])
                                    lm = op("tensor", lambda e, k=k, mloc=mloc, ts=ts,
                                            pr=pr, pw=pw:
                                            e.matmul(pr[:, 0:512],
                                                     BufC[:, C_SLAB + pw * 2048 + k * 256
                                                          + mloc * 128:
                                                          C_SLAB + pw * 2048 + k * 256
                                                          + (mloc + 1) * 128],
                                                     fmA[:, k * 2048 + ts * 512:
                                                         k * 2048 + (ts + 1) * 512],
                                                     start=(k == 0), stop=(k == 7)),
                                            waits=wts, inc=(eT if k == 7 else None))
                                yield
                                tE = op("scalar", lambda e, pr=pr, mloc=mloc, ts=ts,
                                        dst0=dst0: e.activation(
                                    actB[:, dst0 + mloc * 2048 + ts * 512:
                                         dst0 + mloc * 2048 + (ts + 1) * 512],
                                    pr[:, 0:512], AF.Identity, scale=SCALE_QK),
                                    waits=[(eT, lm)], inc=eS)
                                qkv_rot_hist.append((eS, tE))
                                qk_done[qg] = tE
                                yield
                        slab_lastmm[pw] = lm
                    else:
                        for c16 in range(16):
                            pr = p4
                            po = (c16 % 2) * 256
                            qkv_rot_cnt[0] += 1
                            for k in range(8):
                                wts = []
                                if k == 0:
                                    wts = [ln2_ev_w(c16)]
                                    if c16 == 0:
                                        wts.append((qsem, tW))
                                    if c16 % 2 == 0 and len(qkv_rot_hist) >= 1:
                                        wts.append(qkv_rot_hist[-1])
                                lm = op("tensor", lambda e, k=k, c16=c16, pr=pr,
                                        pw=pw, po=po:
                                        e.matmul(pr[:, po:po + 256],
                                                 fmA[:, k * 2048 + c16 * 128:
                                                     k * 2048 + (c16 + 1) * 128],
                                                 BufC[:, C_SLAB + pw * 2048 + k * 256:
                                                      C_SLAB + pw * 2048 + (k + 1) * 256],
                                                 start=(k == 0), stop=(k == 7)),
                                        waits=wts, inc=(eT if k == 7 else None))
                            yield
                            if c16 % 2 == 1:
                                dst = bass.AP(actB, a_v(pg) + (c16 - 1) * 260,
                                              [[ACTB_W, 128], [65, 8], [1, 64]])
                                srcv = pr[:, 0:512].rearrange(
                                    "p (hh w) -> p hh w", w=64)
                                wv = [(eT, lm), (eV, t_ones)]
                                tE = op("vector", lambda e, dst=dst, srcv=srcv:
                                        e.tensor_copy(dst, srcv), waits=wv, inc=eV)
                                qkv_rot_hist.append((eV, tE))
                                v_done[qg] = tE
                            yield
                        slab_lastmm[pw] = lm

            NE = 128
            its = [(qg, c, pair) for qg in range(4) for c in range(16)
                   for pair in range(2)]
            tPK, tRM, tEXP, tEPS, tTR, tD, tFME = {}, {}, {}, {}, {}, {}, {}
            tSA, tSBE, tA, tAM, tO, tSCL = {}, {}, {}, {}, {}, {}
            last_projk = {}

            def st_proj(t):
                qg, c, pair = its[t]
                pg = qg % 2
                P = (p0, p1, p2)[t % 3]
                m = pair
                wts = []
                if c == 0:
                    wts = [(eS, qk_done[qg]), (eV, v_done[qg]), (s_gam, t_gamall)]
                if (t - 3) in tEXP:
                    wts.append((eS, tEXP[t - 3]))
                qsl = actB[:, a_q(pg) + m * 2048 + c * 128:
                           a_q(pg) + m * 2048 + (c + 1) * 128]
                op("tensor", lambda e, qsl=qsl, P=P:
                   e.matmul(P[:, 0:256], qsl, omT_bf[:, :], start=True, stop=True),
                   waits=wts, inc=None)
                ksl = actB[:, a_k(pg) + m * 2048 + c * 128:
                           a_k(pg) + m * 2048 + (c + 1) * 128]
                tPK[t] = op("tensor", lambda e, ksl=ksl, P=P:
                            e.matmul(P[:, 256:512], ksl, omT_bf[:, :],
                                     start=True, stop=True), waits=(), inc=eT)
                last_projk[qg] = tPK[t]

            def st_rm(t):
                P = (p0, p1, p2)[t % 3]
                wts = [(eT, tPK[t])]
                if (t - 2) in tEXP:
                    wts.append((eS, tEXP[t - 2]))
                tRM[t] = op("vector", lambda e, P=P, t=t: e.tensor_reduce(
                    st_nm[:, (t % 2) * 4: (t % 2) * 4 + 4],
                    P[:, 0:512].rearrange("p (a b) -> p a b", a=4),
                    axis=AX.X, op=ALU.max, negate=True), waits=wts, inc=eV)

            def st_exp(t):
                P = (p0, p1, p2)[t % 3]
                tmo = T_TM + (t % 4) * 512
                tE = None
                for j in range(4):
                    wts = []
                    if j == 0:
                        wts = [(eV, tRM[t])]
                        if (t - 4) in tD:
                            wts.append((eT, tD[t - 4]))
                    tE = op("scalar", lambda e, P=P, tmo=tmo, j=j, t=t:
                            e.activation(
                        BufC[:, tmo + j * 128: tmo + (j + 1) * 128],
                        P[:, j * 128:(j + 1) * 128], AF.Exp,
                        bias=st_nm[:, (t % 2) * 4 + j: (t % 2) * 4 + j + 1]),
                        waits=wts, inc=(eS if j == 3 else None))
                tEXP[t] = tE

            def st_eps(t):
                tmo = T_TM + (t % 4) * 512
                tEPS[t] = op("vector", lambda e, tmo=tmo: e.tensor_scalar(
                    BufC[:, tmo: tmo + 512], BufC[:, tmo: tmo + 512],
                    INV_SQM, EPS_F, ALU.mult, ALU.add),
                    waits=[(eS, tEXP[t])], inc=eV)

            def st_tr_delta(t):
                qg, c, pair = its[t]
                pg = qg % 2
                tmo = T_TM + (t % 4) * 512
                tro = (t % 2) * 512
                wts = [(eV, tEPS[t]), (s_c, t_call)]
                if (t - 1) in tFME:
                    wts.append((eS, tFME[t - 1]))
                tt = None
                for j in range(4):
                    tt = op("tensor", lambda e, tmo=tmo, tro=tro, j=j:
                             e.transpose(p6a[:, tro + j * 128: tro + (j + 1) * 128],
                                         BufC[:, tmo + j * 128: tmo + (j + 1) * 128],
                                         ident_b[:, :]),
                             waits=(wts if j == 0 else ()),
                             inc=(eT if j == 3 else None))
                tTR[t] = tt
                wts = []
                if (t - 1) in tSA:
                    wts.append((eV, tSA[t - 1]))
                td = None
                for j in range(2):
                    vsl = actB[:, a_v(pg) + c * 260 + (pair * 2 + j) * 65:
                               a_v(pg) + c * 260 + (pair * 2 + j) * 65 + 65]
                    td = op("tensor", lambda e, tmo=tmo, j=j, t=t, vsl=vsl:
                            e.matmul(p3[:, (t % 2) * 256 + j * 128:
                                         (t % 2) * 256 + j * 128 + 65],
                                     BufC[:, tmo + 256 + j * 128:
                                          tmo + 256 + (j + 1) * 128],
                                     vsl, start=True, stop=True),
                            waits=(wts if j == 0 else ()),
                            inc=(eT if j == 1 else None))
                tD[t] = td

            def st_fme_sacc(t):
                qg, c, pair = its[t]
                tro = (t % 2) * 512
                fmo = T_FM + (t % 4) * 512
                wts = [(eT, tTR[t])]
                if (t - 4) in tO:
                    wts.append((eT, tO[t - 4]))
                tFME[t] = op("scalar", lambda e, tro=tro, fmo=fmo: e.copy(
                    BufC[:, fmo: fmo + 512], p6a[:, tro: tro + 512]),
                    waits=wts, inc=eS)
                sfo = ((qg % 2) * 2 + pair) * 132
                sfv = bass.AP(Sf_t, sfo, [[528, 128], [66, 2], [1, 65]])
                d3v = bass.AP(p3, (t % 2) * 256, [[512, 128], [128, 2], [1, 65]])
                wts = [(eT, tD[t])]
                if (t - 2) in tSA:
                    wts.append((eV, tSA[t - 2]))
                if (t - 2) in tSBE:
                    wts.append((eS, tSBE[t - 2]))
                if c == 0:
                    tSA[t] = op("vector", lambda e, sfv=sfv, d3v=d3v:
                                e.tensor_copy(sfv, d3v), waits=wts, inc=eV)
                else:
                    tSA[t] = op("vector", lambda e, sfv=sfv, d3v=d3v:
                                e.scalar_tensor_tensor(sfv, d3v, 1.0, sfv,
                                                       ALU.mult, ALU.add),
                                waits=wts, inc=eV)

            def st_abar_sbe(t):
                qg, c, pair = its[t]
                fmo = T_FM + (t % 4) * 512
                ao = (t % 2) * 256
                wts = [(eS, tFME[t])]
                if (t - 1) in tAM:
                    wts.append((eV, tAM[t - 1]))
                ta = None
                for j in range(2):
                    ta = op("tensor", lambda e, fmo=fmo, ao=ao, j=j: e.matmul(
                        p5[:, ao + j * 128: ao + (j + 1) * 128],
                        BufC[:, fmo + 256 + j * 128: fmo + 256 + (j + 1) * 128],
                        BufC[:, fmo + j * 128: fmo + (j + 1) * 128],
                        start=True, stop=True),
                        waits=(wts if j == 0 else ()), inc=(eT if j == 1 else None))
                tA[t] = ta
                sfo = ((qg % 2) * 2 + pair) * 132
                sbo = ((c % 3) * 2 + pair) * 132
                sbv = bass.AP(sbT, sbo, [[1056, 128], [66, 2], [1, 65]])
                sfr = bass.AP(Sf_t, sfo, [[528, 128], [66, 2], [1, 65]])
                wts = [(eV, tSA[t])]
                if (t - 4) in tO:
                    wts.append((eT, tO[t - 4]))
                tSBE[t] = op("scalar", lambda e, sbv=sbv, sfr=sfr:
                             e.copy(sbv, sfr), waits=wts, inc=eS)

            def st_mask(t):
                ao = (t % 2) * 256
                mb = bass.AP(maskji, 0, [[128, 128], [0, 2], [1, 128]])
                wts = [(eT, tA[t])]
                if (t - 2) in tO:
                    wts.append((eT, tO[t - 2]))
                amo = (t % 2) * 256
                tAM[t] = op("vector", lambda e, ao=ao, amo=amo, mb=mb: e.tensor_tensor(
                    amT[:, amo: amo + 256].rearrange("p (a b) -> p a b", a=2),
                    p5[:, ao: ao + 256].rearrange("p (a b) -> p a b", a=2),
                    mb, op=ALU.mult), waits=wts, inc=eV)

            def st_o(t):
                qg, c, pair = its[t]
                pg = qg % 2
                fmo = T_FM + (t % 4) * 512
                oo = (t % 2) * 256
                amo = (t % 2) * 256
                wts = [(eV, tAM[t])]
                if (t - 1) in tSCL:
                    wts.append((eV, tSCL[t - 1]))
                if c > 0:
                    wts.append((eS, tSBE[t - 2]))
                to = None
                for j in range(2):
                    vsl = actB[:, a_v(pg) + c * 260 + (pair * 2 + j) * 65:
                               a_v(pg) + c * 260 + (pair * 2 + j) * 65 + 65]
                    to = op("tensor", lambda e, oo=oo, amo=amo, j=j, vsl=vsl, c=c:
                            e.matmul(pOb[:, oo + j * 128: oo + j * 128 + 65],
                                     amT[:, amo + j * 128: amo + (j + 1) * 128],
                                     vsl, start=True, stop=(c == 0)),
                            waits=(wts if j == 0 else ()),
                            inc=(eT if (c == 0 and j == 1) else None))
                    if c > 0:
                        sbo = (((c - 1) % 3) * 2 + pair) * 132
                        sbs = sbT[:, sbo + j * 66: sbo + j * 66 + 65]
                        to = op("tensor", lambda e, oo=oo, fmo=fmo, j=j, sbs=sbs:
                                e.matmul(pOb[:, oo + j * 128: oo + j * 128 + 65],
                                         BufC[:, fmo + j * 128: fmo + (j + 1) * 128],
                                         sbs, start=False, stop=True),
                                waits=(), inc=(eT if j == 1 else None))
                tO[t] = to

            def st_tail(t):
                qg, c, pair = its[t]
                oo = (t % 2) * 256
                dnv = bass.AP(pOb, oo + 64, [[512, 128], [128, 2], [1, 1]])
                tdn = op("vector", lambda e, dnv=dnv, t=t: e.tensor_scalar(
                    den_t[:, (t % 2) * 2: (t % 2) * 2 + 2].rearrange(
                        "p (a b) -> p a b", b=1), dnv, EPS_F, None,
                    ALU.add, ALU.bypass), waits=[(eT, tO[t])], inc=eV)
                trc = op("vector", lambda e, t=t: e.reciprocal(
                    rec_t[:, (t % 2) * 2: (t % 2) * 2 + 2],
                    den_t[:, (t % 2) * 2: (t % 2) * 2 + 2]),
                    waits=[(eV, tdn)], inc=eV)
                outv = BufC[:, c * 1024 + (qg * 4 + pair * 2) * 64:
                            c * 1024 + (qg * 4 + pair * 2) * 64 + 128] \
                    .rearrange("p (a b) -> p a b", a=2)
                o6v = bass.AP(pOb, oo, [[512, 128], [128, 2], [1, 64]])
                rcb = bass.AP(rec_t, (t % 2) * 2, [[8, 128], [1, 2], [0, 64]])
                tSCL[t] = op("vector", lambda e, outv=outv, o6v=o6v, rcb=rcb:
                             e.tensor_tensor(outv, o6v, rcb, op=ALU.mult),
                             waits=[(eV, trc)], inc=eV)

            STAGES_A = [(5, st_fme_sacc), (6, st_abar_sbe),
                        (4, st_tr_delta), (3, st_eps),
                        (2, st_exp)]
            STAGES_B = [(1, st_rm), (0, st_proj)]

            gens = [qkv_gen(qg) for qg in range(4)]
            for _ in gens[0]:
                pass
            done_g = [True, False, False, False]

            def adv_gen(gq):
                if gq <= 3 and not done_g[gq]:
                    try:
                        next(gens[gq])
                    except StopIteration:
                        done_g[gq] = True

            for sl in range(NE + 9):
                if 0 <= sl - 8 < NE:
                    st_o(sl - 8)
                if 0 <= sl - 7 < NE:
                    st_mask(sl - 7)
                if 0 <= sl - 8 < NE:
                    st_tail(sl - 8)
                gq = sl // 32 + 1
                adv_gen(gq)
                for ofs, fn in STAGES_A:
                    t = sl - ofs
                    if 0 <= t < NE:
                        fn(t)
                adv_gen(gq)
                for ofs, fn in STAGES_B:
                    t = sl - ofs
                    if 0 <= t < NE:
                        fn(t)
                if sl % 32 == 31 and gq <= 3 and not done_g[gq]:
                    for _ in gens[gq]:
                        pass
                    done_g[gq] = True
            barrier()

            tWo = dma("gpsimd", s_wo, BufC[:, C_WO:C_WO + 4096]
                      .rearrange("p (k j) -> p k j", k=4), Wo_sw[:, 0:4, :],
                      max_dma_last_dim=8192)
            tWo2 = dma("gpsimd", cwu[3], BufC[:, C_WO + 4096:C_WO + 8192]
                       .rearrange("p (k j) -> p k j", k=4), Wo_sw[:, 4:8, :],
                       max_dma_last_dim=8192)
            ppair = ((p0, p1), (p2, p3))
            tY, tAF, tWoMM = {}, {}, {}
            ypend = {}

            def emit_ypair(c):
                ba, bb, tmpo, tTmpc = ypend[c]
                op("vector", lambda e, c=c, ba=ba, tmpo=tmpo: e.tensor_tensor(
                    actB[:, A_Y + c * 1024: A_Y + c * 1024 + 512],
                    actB[:, tmpo: tmpo + 512], ba[:, 0:512], op=ALU.add),
                    waits=[(eS, tTmpc), (eT, tWoMM[c])], inc=eV)
                tY[c] = op("vector", lambda e, c=c, bb=bb, tmpo=tmpo: e.tensor_tensor(
                    actB[:, A_Y + c * 1024 + 512: A_Y + (c + 1) * 1024],
                    actB[:, tmpo + 512: tmpo + 1024], bb[:, 0:512], op=ALU.add),
                    waits=(), inc=eV)

            for c in range(NT):
                pi = c % 2
                fi = c % 3
                wts = []
                if c >= 2:
                    wts.append((eV, tAF[c - 2]))
                tt = None
                for j in range(8):
                    tt = op("tensor", lambda e, j=j, pi=pi, c=c: e.transpose(
                        p6[pi][:, j * 128:(j + 1) * 128],
                        BufC[:, c * 1024 + j * 128: c * 1024 + (j + 1) * 128],
                        ident_b[:, :]), waits=(wts if j == 0 else ()),
                        inc=(eT if j == 7 else None))
                wv = [(eT, tt)]
                if c >= 3:
                    wv.append((eT, tWoMM[c - 3]))
                tAF[c] = op("vector", lambda e, pi=pi, fi=fi: e.tensor_copy(
                    actB[:, A_LNTM + fi * 1024: A_LNTM + (fi + 1) * 1024],
                    p6[pi][:, 0:1024]), waits=wv, inc=eV)
                ba, bb = ppair[pi]
                for k in range(8):
                    wts = []
                    if k == 0:
                        wts = [(eV, tAF[c])]
                        if c <= 1:
                            wts.append((s_wo, tWo))
                        if c >= 2:
                            wts.append((eV, tY[c - 2]))
                    elif k == 4 and c == 0:
                        wts = [(cwu[3], tWo2)]
                    op("tensor", lambda e, k=k, fi=fi, ba=ba: e.matmul(
                        ba[:, 0:512],
                        actB[:, A_LNTM + fi * 1024 + k * 128: A_LNTM + fi * 1024 + (k + 1) * 128],
                        BufC[:, C_WO + k * 1024: C_WO + k * 1024 + 512],
                        start=(k == 0), stop=(k == 7)), waits=wts, inc=None)
                    tWoMM[c] = op("tensor", lambda e, k=k, fi=fi, bb=bb: e.matmul(
                        bb[:, 0:512],
                        actB[:, A_LNTM + fi * 1024 + k * 128: A_LNTM + fi * 1024 + (k + 1) * 128],
                        BufC[:, C_WO + k * 1024 + 512: C_WO + (k + 1) * 1024],
                        start=(k == 0), stop=(k == 7)), waits=(),
                        inc=(eT if k == 7 else None))
                tmpo = 20480 + pi * 1024
                wts = []
                if c >= 2:
                    wts.append((eV, tY[c - 2]))
                tTmp = op("scalar", lambda e, c=c, tmpo=tmpo: e.activation(
                    actB[:, tmpo: tmpo + 1024], x_tm[:, c * 1024:(c + 1) * 1024],
                    AF.Identity, bias=m2r2[:, c:c + 1], scale=rstd2[:, c:c + 1]),
                    waits=wts, inc=eS)
                ypend[c] = (ba, bb, tmpo, tTmp)
                if c >= 1:
                    emit_ypair(c - 1)
            emit_ypair(NT - 1)
            barrier()

            BufCf = BufC.bitcast(F32)
            t_le2 = op("vector", lambda e: e.memset(lneps[:, :], 1e6), inc=eV)
            dump2i = actB[:, A_LNTM + 2048:A_LNTM + 3072]
            ev_tag, tr_tag, hb_tag = {}, {}, {}

            def emit_iev(c):
                pi = c % 2
                te = None
                for k in range(8):
                    dst = fmA[:, k * 2048 + c * 128: k * 2048 + (c + 1) * 128]
                    src = p6[pi][:, k * 128:(k + 1) * 128]
                    te = op("vector", lambda e, dst=dst, src=src, k=k:
                            e.tensor_scalar(dst, src, gamF_s[:, k:k + 1],
                                            betF_s[:, k:k + 1], ALU.mult, ALU.add),
                            waits=([(eT, tr_tag[c])] if k == 0 else ()),
                            inc=(eV if k == 7 else None))
                ev_tag[c] = (eV, te)

            tRSi = {}

            def emit_istats(c):
                tRi = op("scalar", lambda e, c=c: e.activation(
                    dump2i, actB[:, A_Y + c * 1024: A_Y + (c + 1) * 1024],
                    AF.Identity, accum_out=musum[:, c:c + 1]),
                    waits=[(eV, tY[c])], inc=eS)
                tMUi = op("gpsimd", lambda e, c=c: e.tensor_scalar(
                    mu_c[:, c:c + 1], musum[:, c:c + 1], 1.0 / 1024.0, None,
                    ALU.mult, ALU.bypass), waits=[(eS, tRi)], inc=eG)
                tSQi = op("scalar", lambda e, c=c: e.activation(
                    dump2i, actB[:, A_Y + c * 1024: A_Y + (c + 1) * 1024],
                    AF.Square, bias=mu_c[:, c:c + 1], scale=-1.0,
                    accum_out=varsum[:, c:c + 1]), waits=[(eG, tMUi)], inc=eS)
                wl = [(eS, tSQi)] + ([(eV, t_le2)] if c == 0 else [])
                tLNi = op("scalar", lambda e, c=c: e.activation(
                    varsum[:, c:c + 1], varsum[:, c:c + 1], AF.Ln,
                    bias=lneps[:, 0:1], scale=1.0 / 1024.0), waits=wl, inc=eS)
                tRSi[c] = op("scalar", lambda e, c=c: e.activation(
                    rstd_c[:, c:c + 1], varsum[:, c:c + 1], AF.Exp, scale=-0.5),
                    waits=[(eS, tLNi)], inc=eS)

            def emit_ixform(c):
                pi = c % 2
                lntm = actB[:, A_LNTM + pi * 1024: A_LNTM + (pi + 1) * 1024]
                wts = [(eS, tRSi[c])]
                if c == 0:
                    wts.append((s_gam, t_gamall))
                if c >= 2:
                    wts.append((eT, tr_tag[c - 2]))
                tT3 = op("vector", lambda e, c=c, lntm=lntm: e.tensor_scalar(
                    lntm, actB[:, A_Y + c * 1024: A_Y + (c + 1) * 1024],
                    mu_c[:, c:c + 1], rstd_c[:, c:c + 1],
                    ALU.subtract, ALU.mult), waits=wts, inc=eV)
                w2 = [(eV, tT3)]
                if c >= 2:
                    w2.append(ev_tag[c - 2])
                tt = None
                for j in range(8):
                    tt = op("tensor", lambda e, j=j, pi=pi, lntm=lntm: e.transpose(
                        p6[pi][:, j * 128:(j + 1) * 128], lntm[:, j * 128:(j + 1) * 128],
                        ident_b[:, :]), waits=(w2 if j == 0 else ()),
                        inc=(eT if j == 7 else None))
                tr_tag[c] = tt

            for c in range(NT + 2):
                if c < NT:
                    emit_istats(c)
                if c >= 2:
                    emit_ixform(c - 2)
                if c >= 3:
                    emit_iev(c - 3)
            emit_iev(NT - 1)
            barrier()

            relu_hist = []
            w1mm_hist = []
            w2mm_hist = []
            o2_hist = []
            tr2_hist = []
            xu2_hist = []
            w2_last_of_quarter = 0
            for tq in range(4):
                ts0 = tq * 512
                tW1g = {}

                def w1_load(gf, tq=tq):
                    pi4 = gf % 2
                    w1sem = (swA, swB)[pi4]
                    wts = [(cw1, cast_w1)]
                    if w1mm_hist:
                        wts.append((eT, w1mm_hist[-1]))
                    tW1g[gf] = dma("sync", w1sem,
                                   BufC[:, pi4 * 4096:(pi4 + 1) * 4096]
                                   .rearrange("p (f k j) -> p f k j", f=4, k=8),
                                   W1_bf[:, gf * 4:(gf + 1) * 4, :, :], waits=wts)
                w1_load(0)
                w1_load(1)
                for f in range(NF):
                    pi = f % 2
                    gf = f // 4
                    if f % 4 == 0 and f > 0 and gf + 1 <= 7:
                        w1_load(gf + 1)
                    w1sem = (swA, swB)[gf % 2]
                    w1of = (gf % 2) * 4096 + (f % 4) * 1024
                    lm = None
                    for k in range(8):
                        wts = []
                        if k == 0:
                            if f % 4 == 0:
                                wts.append((w1sem, tW1g[gf]))
                            if len(relu_hist) >= 2:
                                wts.append((eV, relu_hist[-2]))
                        lm = op("tensor", lambda e, k=k, pi=pi, ts0=ts0, w1of=w1of:
                                e.matmul(
                            pP[pi][:, 0:512],
                            BufC[:, w1of + k * 128: w1of + (k + 1) * 128],
                            fmA[:, k * 2048 + ts0: k * 2048 + ts0 + 512],
                            start=(k == 0), stop=(k == 7)),
                            waits=wts, inc=(eT if k == 7 else None))
                    w1mm_hist.append(lm)
                    wts = [(eT, lm)]
                    if tq > 0 and f == 0:
                        wts.append((eT, w2_last_of_quarter))
                    tRl = op("vector", lambda e, pi=pi, f=f: e.tensor_scalar(
                        actB[:, A_RELU + f * 512: A_RELU + (f + 1) * 512],
                        pP[pi][:, 0:512], b1t[:, f:f + 1], 0.0, ALU.add, ALU.max),
                        waits=wts, inc=eV)
                    relu_hist.append(tRl)
                for d in range(8):
                    pi = d % 2
                    w2sem = (sW2a, sW2b)[pi]
                    wts = [(cw2, cast_w2)]
                    if len(w2mm_hist) >= 2:
                        wts.append((eT, w2mm_hist[-2]))
                    tW2 = dma("sync", w2sem, BufC[:, 8192 + pi * 4096: 8192 + (pi + 1) * 4096]
                              .rearrange("p (k j) -> p k j", k=32),
                              W2_bf[:, d, :, :], waits=wts)
                    lm = None
                    for f in range(NF):
                        wts = []
                        if f == 0:
                            wts = [(w2sem, tW2), (eV, relu_hist[-1])]
                            if len(o2_hist) >= 2:
                                wts.append((eS, o2_hist[-2]))
                        lm = op("tensor", lambda e, f=f, pi=pi:
                                e.matmul(
                            pQ[pi][:, 0:512],
                            BufC[:, 8192 + pi * 4096 + f * 128: 8192 + pi * 4096 + (f + 1) * 128],
                            actB[:, A_RELU + f * 512: A_RELU + (f + 1) * 512],
                            start=(f == 0), stop=(f == NF - 1)),
                            waits=wts, inc=(eT if f == NF - 1 else None))
                    w2mm_hist.append(lm)
                    w2_last_of_quarter = lm
                    o2reg = tmpF[:, 0:512] if pi == 0 else tmpF[:, 512:1024]
                    wts = [(eT, lm)]
                    if len(tr2_hist) >= 2:
                        wts.append((eT, tr2_hist[-2]))
                    tO2 = op("scalar", lambda e, pi=pi, d=d, o2reg=o2reg:
                             e.activation(o2reg, pQ[pi][:, 0:512], AF.Identity,
                                          bias=b2t[:, d:d + 1]),
                             waits=wts, inc=eS)
                    o2_hist.append(tO2)
                    ptr = (p4, p5)[pi]
                    wts = [(eS, tO2)]
                    if len(xu2_hist) >= 2:
                        wts.append((eV, xu2_hist[-2]))
                    tt = None
                    for j in range(4):
                        tt = op("tensor", lambda e, j=j, ptr=ptr, o2reg=o2reg: e.transpose(
                            ptr[:, j * 128:(j + 1) * 128],
                            o2reg[:, j * 128:(j + 1) * 128], ident_f[:, :]),
                            waits=(wts if j == 0 else ()), inc=(eT if j == 3 else None))
                    tr2_hist.append(tt)
                    tx = None
                    for j in range(4):
                        ct = tq * 4 + j
                        tx = op("vector", lambda e, j=j, ptr=ptr, ct=ct, d=d:
                                e.scalar_tensor_tensor(
                            x_tm[:, ct * 1024 + d * 128: ct * 1024 + (d + 1) * 128],
                            ptr[:, j * 128:(j + 1) * 128], 1.0,
                            x_tm[:, ct * 1024 + d * 128: ct * 1024 + (d + 1) * 128],
                            ALU.mult, ALU.add),
                            waits=([(eT, tt)] if j == 0 else ()), inc=eV)
                    xu2_hist.append(tx)
            barrier()

        if stage >= 3:
            ln3_ev = emit_outer_ln(1e-5, pipe=True, pipe_chunks=False,
                                   pipe_stagger=True)
            ffn2_x = {}
            emit_ffn(ln3_ev, xtags=ffn2_x)
            for c in range(NT):
                dma("sync" if c % 2 == 0 else "scalar", s_x,
                    out[c * 128:(c + 1) * 128, :],
                    x_tm[:, c * 1024:(c + 1) * 1024], waits=[ffn2_x[c]])
            barrier()
        else:
            for c in range(NT):
                dma("sync", s_x, out[c * 128:(c + 1) * 128, :],
                    x_tm[:, c * 1024:(c + 1) * 1024])
        op("sync", lambda e: None, waits=[(s_x, s_x.n)])

        with nc.Block() as block:
            @block.sync
            def _(e):
                for f in ops["sync"]:
                    f(e)

            @block.gpsimd
            def _(e):
                for f in ops["gpsimd"]:
                    f(e)

            @block.tensor
            def _(e):
                for f in ops["tensor"]:
                    f(e)

            @block.vector
            def _(e):
                for f in ops["vector"]:
                    f(e)

            @block.scalar
            def _(e):
                for f in ops["scalar"]:
                    f(e)

    return nc


def _omd(omega):
    d = np.zeros((128, 256), np.float32)
    d[0:64, 0:128] = omega.T
    d[64:128, 128:256] = omega.T
    return d


def _prep_inputs(inputs):
    f32 = np.float32
    Wq = np.asarray(inputs["L0_attn_Wq"], f32)
    Wk = np.asarray(inputs["L0_attn_Wk"], f32)
    Wv = np.asarray(inputs["L0_attn_Wv"], f32)
    Wo = np.asarray(inputs["L0_attn_Wo"], f32)
    W1 = np.asarray(inputs["L0_attn_W1"], f32)
    W2 = np.asarray(inputs["L0_attn_W2"], f32)
    b1 = np.asarray(inputs["L0_attn_b1"], f32)
    b2 = np.asarray(inputs["L0_attn_b2"], f32)
    gamma = np.asarray(inputs["L0_attn_gamma"], f32)
    beta = np.asarray(inputs["L0_attn_beta"], f32)
    omega = np.asarray(inputs["L0_attn_omega"], f32)
    wg = np.asarray(inputs["L0_ffn_w_gate"], np.int32)
    wu = np.asarray(inputs["L0_ffn_w_up"], np.int32)
    wd = np.asarray(inputs["L0_ffn_w_down"], np.int32)

    c = np.ascontiguousarray
    shared = {
        "wgT_i": c(wg.T),
        "wuT_i": c(wu.T),
        "wdT_i": c(wd.T),
        "Wq_t": c(Wq.reshape(8, 128, 2, 512).transpose(2, 1, 0, 3)),
        "Wk_t": c(Wk.reshape(8, 128, 2, 512).transpose(2, 1, 0, 3)),
        "Wv_t": c(Wv.reshape(8, 128, 2, 512).transpose(2, 1, 0, 3)),
        "Wo_sw": c(Wo.reshape(8, 128, 1024).transpose(1, 0, 2)),
        "W1_sw": c(W1.reshape(8, 128, 32, 128).transpose(1, 2, 0, 3)),
        "W2_sw": c(W2.reshape(32, 128, 8, 128).transpose(1, 2, 0, 3)),
        "omD": _omd(omega),
        "b1t": c(b1.reshape(32, 128).T),
        "b2t": c(b2.reshape(8, 128).T),
        "gam_bc": c(np.broadcast_to(gamma[None, :], (128, 1024))),
        "gamF": c(gamma.reshape(8, 128).T),
        "betF": c(beta.reshape(8, 128).T),
        "bet_bc": c(np.broadcast_to(beta[None, :], (128, 1024))),
        "ident_bf": np.eye(128, dtype=BFnp),
        "ident_f32": np.eye(128, dtype=f32),
        "maskji": (np.arange(128)[:, None] <= np.arange(128)[None, :]).astype(BFnp),
    }
    return shared


def run_device(inputs, stage=3, core_ids=None):
    shared = _prep_inputs(inputs)
    batch = np.asarray(inputs["L0_attn_batch"], np.float32)
    if core_ids is None:
        core_ids = list(range(8))
    nc = build(stage)
    in_maps = []
    for b in range(len(core_ids)):
        m = dict(shared)
        m["xin"] = np.ascontiguousarray(batch[b])
        in_maps.append(m)
    res = run_bass_kernel_spmd(nc, in_maps, core_ids=core_ids)
    return np.stack([res.results[i]["out"] for i in range(len(core_ids))], axis=0)


def kernel(**inputs):
    return run_device(inputs, stage=3, core_ids=list(range(8)))
